# version 09 (final)
# baseline (speedup 1.0000x reference)
import numpy as np
import ml_dtypes
import concourse.bass as bass
import concourse.mybir as mybir
from concourse.bass_utils import run_bass_kernel_spmd

F32 = mybir.dt.float32
BF16 = mybir.dt.bfloat16
F16 = mybir.dt.float16
I32 = mybir.dt.int32
AF = mybir.ActivationFunctionType
ALU = mybir.AluOpType
AX = mybir.AxisListType.X
BF = ml_dtypes.bfloat16

D = 2048
B = 2
T = 4096
NCORE = 8


ENGS = ("sync", "vector", "scalar", "gpsimd", "tensor")


class Buf:
    __slots__ = ("name", "w", "readers", "dsem", "dcnt", "last_q", "scoped")

    def __init__(self, name, scoped=False):
        self.name = name
        self.w = None
        self.readers = {}
        self.dsem = None
        self.dcnt = 0
        self.last_q = None
        self.scoped = scoped


class Em:
    def __init__(self, nc):
        self.nc = nc
        self.plans = [{e: [] for e in ENGS}]
        self.plan = self.plans[0]
        self.cnt = {e: 0 for e in ENGS}
        self.dcnt = {e: 0 for e in ENGS}
        self.know = {e: {} for e in ENGS}
        self.bufs = {}
        self.sems = {}
        self.strict_same = True
        self.n_waits = 0
        self.n_insts = 0
        self._consts = {}
        self.dma_sems = []
        self.dsem_cnt = {}
        self.dsem_lastq = {}
        self._pool_next = 0
        self._pool_stack = []
        self.NPOOL = 48

    def constf(self, val, dtype=F32):
        key = (dtype, float(val))
        if key in self._consts:
            return self._consts[key]
        h = self.sbuf(f"cst{len(self._consts)}", [128, 1], dtype)
        self.memset(h[:, :], float(val))
        self.nc.const_aps.register(dtype, float(val), h[0:128, 0:1])
        self._consts[key] = h
        self._const_vals = getattr(self, "_const_vals", [])
        self._const_vals.append((h, float(val)))
        return h

    def sbuf(self, name, shape, dtype=F32):
        h = self.nc.alloc_sbuf_tensor("sb_" + name, list(shape), dtype)
        self.bufs[h.name] = Buf(h.name)
        return h

    def psum(self, name, shape, dtype=F32):
        h = self.nc.alloc_psum_tensor("psm_" + name, list(shape), dtype)
        self.bufs[h.name] = Buf(h.name)
        return h

    def dram(self, name, shape, dtype=F32, kind="Internal", **kw):
        h = self.nc.dram_tensor(name, list(shape), dtype, kind=kind, **kw)
        self.bufs[h.name] = Buf(h.name)
        return h

    def _key(self, ap):
        t = getattr(ap, "tensor", ap)
        return getattr(t, "name", None)

    def new_body(self):
        self.plans.append({e: [] for e in ENGS})
        self.plan = self.plans[-1]
        self.cnt = {e: 0 for e in ENGS}
        self.know = {e: {} for e in ENGS}
        for s in self.dsem_cnt:
            self.dsem_cnt[s] = 0
        self._pool_next = 0
        self._pool_stack = []
        for b in self.bufs.values():
            b.w = None
            b.readers = {}
            b.dcnt = 0
            b.dsem = None
        for h, v in getattr(self, "_const_vals", []):
            self.memset(h[:, :], v)

    def raw(self, eng, fn):
        self.plan[eng].append(("r", fn))

    def scope(self):
        from contextlib import ExitStack
        em = self

        class _Scope:
            def __enter__(self):
                self.es = ExitStack()
                self.saved = getattr(em, "_cur_es", None)
                em._pool_stack.append(em._pool_next)
                em._cur_es = self.es
                return self.es

            def __exit__(self, *a):
                em.barrier_all()
                self.es.close()
                em._cur_es = self.saved
                em._pool_next = em._pool_stack.pop()
                return False
        return _Scope()

    def sbuf_scoped(self, name, shape, dtype=F32):
        g = self.nc.sbuf_tensor(name, list(shape), dtype)
        h = self._cur_es.enter_context(g)
        self.bufs[h.name] = Buf(h.name, scoped=True)
        return h

    def _wait(self, eng, sem, val):
        if val <= 0:
            return
        if self.know[eng].get(sem, 0) >= val:
            return
        self.know[eng][sem] = val
        self.plan[eng].append(("w", sem, val))
        self.n_waits += 1

    def _pre(self, eng, reads, writes, is_dma=False):
        for ap in reads:
            k = self._key(ap)
            if k is None:
                continue
            b = self.bufs.get(k)
            if b is None:
                continue
            if b.w is not None:
                s, v = b.w
                if s == "c_" + eng and not is_dma and not self.strict_same:
                    continue
                self._wait(eng, s, v)
        for ap in writes:
            k = self._key(ap)
            if k is None:
                continue
            b = self.bufs.get(k)
            if b is None:
                continue
            if b.w is not None:
                s, v = b.w
                if not (s == "c_" + eng):
                    self._wait(eng, s, v)
            for s, v in b.readers.items():
                if s == "c_" + eng:
                    continue
                self._wait(eng, s, v)

    def _post(self, eng, reads, writes, is_dma, dma_key=None):
        if is_dma:
            b = self.bufs[dma_key]
            if b.dsem is None:
                if b.scoped:
                    b.dsem = f"dsp{self._pool_next % self.NPOOL}"
                    self._pool_next += 1
                    if b.dsem not in self.dsem_cnt:
                        self.dma_sems.append(b.dsem)
                        self.dsem_cnt[b.dsem] = 0
                else:
                    b.dsem = "ds_" + dma_key
                    if b.dsem not in self.dsem_cnt:
                        self.dma_sems.append(b.dsem)
                        self.dsem_cnt[b.dsem] = 0
            self.dsem_cnt[b.dsem] += 16
            b.dcnt = self.dsem_cnt[b.dsem]
            b.last_q = eng
            self.dsem_lastq[b.dsem] = eng
            mark = (b.dsem, b.dcnt)
        else:
            self.cnt[eng] += 1
            mark = ("c_" + eng, self.cnt[eng])
        for ap in reads:
            k = self._key(ap)
            if k is None or k not in self.bufs:
                continue
            b = self.bufs[k]
            if b.readers.get(mark[0], 0) < mark[1]:
                b.readers[mark[0]] = mark[1]
        for ap in writes:
            k = self._key(ap)
            if k is None or k not in self.bufs:
                continue
            b = self.bufs[k]
            b.w = mark
            b.readers = {}
        return mark

    def emit(self, eng, fn, reads, writes, is_dma=False, dma_key=None):
        self._pre(eng, reads, writes, is_dma=is_dma)
        mark = self._post(eng, reads, writes, is_dma, dma_key=dma_key)
        inc = 16 if is_dma else 1
        self.plan[eng].append(("i", fn, mark[0], inc))
        self.n_insts += 1
        return mark

    def _dma_key(self, out, in_):
        ko, ki = self._key(out), self._key(in_)
        to = getattr(out, "tensor", out)
        if hasattr(to, "partition_size") or "DRam" not in type(to).__name__:
            pass
        if "DRam" in type(getattr(out, "tensor", out)).__name__:
            return ki if ki is not None else ko
        return ko

    def dma(self, q, out, in_, **kw):
        return self.emit(q, lambda e: e.dma_start(out=out, in_=in_, **kw),
                         [in_], [out], is_dma=True, dma_key=self._dma_key(out, in_))

    def dmat(self, q, out, in_):
        return self.emit(q, lambda e: e.dma_start_transpose(out=out, in_=in_),
                         [in_], [out], is_dma=True, dma_key=self._dma_key(out, in_))

    def mm(self, out, lhsT, rhs, start=True, stop=True):
        return self.emit("tensor",
                         lambda e: e.matmul(out, lhsT, rhs, start=start, stop=stop),
                         [lhsT, rhs], [out])

    def trans(self, out, in_, ident):
        return self.emit("tensor", lambda e: e.transpose(out, in_, ident),
                         [in_, ident], [out])

    def act(self, out, in_, func, bias=0.0, scale=1.0, accum_out=None):
        reads = [in_]
        for v in (bias, scale):
            if hasattr(v, "tensor") or hasattr(v, "name"):
                reads.append(v)
            elif isinstance(v, (int, float)) and float(v) not in (0.0, 1.0):
                reads.append(self.constf(v))
        writes = [out] + ([accum_out] if accum_out is not None else [])
        return self.emit("scalar",
                         lambda e: e.activation(out, in_, func, bias=bias, scale=scale,
                                                accum_out=accum_out),
                         reads, writes)

    def tt(self, out, a, b, op, eng="vector"):
        return self.emit(eng, lambda e: e.tensor_tensor(out, a, b, op), [a, b], [out])

    def ts(self, out, a, scalar, op, eng="vector"):
        reads = [a] + ([scalar] if hasattr(scalar, "tensor") or isinstance(scalar, bass.AP) else [])
        return self.emit(eng, lambda e: e.tensor_single_scalar(out, a, scalar, op),
                         reads, [out])

    def stt(self, out, in0, scalar, in1, op0, op1, eng="vector"):
        reads = [in0, in1] + ([scalar] if hasattr(scalar, "tensor") or isinstance(scalar, bass.AP) else [])
        return self.emit(eng,
                         lambda e: e.scalar_tensor_tensor(out, in0, scalar, in1, op0, op1),
                         reads, [out])

    def scan(self, out, a, u, init, op0=ALU.mult, op1=ALU.add, eng="vector"):
        reads = [a, u] + ([init] if hasattr(init, "tensor") or isinstance(init, bass.AP) else [])
        return self.emit(eng,
                         lambda e: e.tensor_tensor_scan(out, a, u, init, op0, op1),
                         reads, [out])

    def copy(self, out, in_, eng="vector"):
        return self.emit(eng, lambda e: e.tensor_copy(out, in_), [in_], [out])

    def red(self, out, in_, op, eng="vector", negate=None):
        return self.emit(eng,
                         lambda e: e.tensor_reduce(out, in_, AX, op, negate=negate),
                         [in_], [out])

    def memset(self, out, val, eng="vector"):
        return self.emit(eng, lambda e: e.memset(out, val), [], [out])

    def iota(self, out, pattern, base=0, chmul=0, eng="gpsimd"):
        return self.emit(eng,
                         lambda e: e.iota(out, pattern, base=base, channel_multiplier=chmul,
                                          allow_small_or_imprecise_dtypes=True),
                         [], [out])

    def affsel(self, out, in_, pattern, cmp, fill, base=0, chmul=0):
        return self.emit("gpsimd",
                         lambda e: e.affine_select(out=out, in_=in_, pattern=pattern,
                                                   compare_op=cmp, fill=fill, base=base,
                                                   channel_multiplier=chmul),
                         [in_], [out])

    def select(self, out, mask, a, b):
        return self.emit("vector", lambda e: e.select(out, mask, a, b),
                         [mask, a, b], [out])

    def finish(self):
        for s, v in self.dsem_cnt.items():
            if v:
                self._wait(self.dsem_lastq[s], s, v)

    def barrier_all(self, bufkeys=None):
        for e in ENGS:
            for s in ENGS:
                if self.cnt[s]:
                    self._wait(e, "c_" + s, self.cnt[s])
            for s, v in self.dsem_cnt.items():
                if v:
                    self._wait(e, s, v)

    def build(self):
        nc = self.nc
        from contextlib import ExitStack
        es = ExitStack()
        for e in ENGS:
            self.sems["c_" + e] = es.enter_context(nc.semaphore("c_" + e))
        for s in self.dma_sems:
            self.sems[s] = es.enter_context(nc.semaphore(s))
        sems = self.sems

        def runner(plan_e):
            def body(eng):
                for item in plan_e:
                    if item[0] == "w":
                        eng.wait_ge(sems[item[1]], item[2])
                    elif item[0] == "r":
                        item[1](eng)
                    else:
                        _, fn, sem, inc = item
                        fn(eng).then_inc(sems[sem], inc)
            return body

        with nc.Block() as block:
            if len(self.plans) == 1:
                block.sync(runner(self.plan["sync"]))
                block.vector(runner(self.plan["vector"]))
                block.scalar(runner(self.plan["scalar"]))
                block.gpsimd(runner(self.plan["gpsimd"]))
                block.tensor(runner(self.plan["tensor"]))
            else:
                assert len(self.plans) == 2

                def role_body(ename):
                    def body(eng):
                        with eng.register("role_" + ename) as r:
                            pid = eng.partition_id()
                            eng.reg_alu(r, pid, 2, ALU.mod)
                            with eng.If_eq(r, 0):
                                runner(self.plans[0][ename])(eng)
                            with eng.Else():
                                runner(self.plans[1][ename])(eng)
                    return body
                block.sync(role_body("sync"))
                block.vector(role_body("vector"))
                block.scalar(role_body("scalar"))
                block.gpsimd(role_body("gpsimd"))
                block.tensor(role_body("tensor"))
        es.close()
        return nc



def ceil_div(a, b):
    return (a + b - 1) // b


class Ctx:

    def __init__(self, em: Em, tcols: int):
        self.em = em
        self.uid = 0
        self.ones_col = em.sbuf("ones_col", [128, 1], BF16)
        em.memset(self.ones_col[:, :], 1.0)
        self.ones_row = em.sbuf("ones_row", [1, 128], BF16)
        em.memset(self.ones_row[:, :], 1.0)
        self.ident = em.sbuf("ident_bf", [128, 128], BF16)
        em.memset(self.ident[:, :], 1.0)
        em.affsel(self.ident[:, :], self.ident[:, :], [[-1, 128]], ALU.is_ge, 0.0, 0, 1)
        em.affsel(self.ident[:, :], self.ident[:, :], [[1, 128]], ALU.is_ge, 0.0, 0, -1)

    def name(self, p):
        self.uid += 1
        return f"{p}_{self.uid}"


def row_bcast(ctx, out_sb, row_ap, n, psum, func=AF.Copy, bias=0.0, scale=1.0):
    em = ctx.em
    for j0 in range(0, n, 512):
        j1 = min(j0 + 512, n)
        em.mm(psum[:, 0:j1 - j0], ctx.ones_row_f[:, :], row_ap[:, j0:j1])
        em.act(out_sb[:, j0:j1], psum[:, 0:j1 - j0], func, bias=bias, scale=scale)


NCH = 16


def chunk(sb, c, tb):
    return sb[:, c * tb:(c + 1) * tb]


def load_col(em, q, sb, dram_col, nch):
    for c in range(nch):
        em.dma(q, sb[:, c:c + 1], dram_col[c * 128:(c + 1) * 128, 0:1])


def load_fm(em, q, sb, dram_ap, rows, t0, t1, tb):
    nch = rows // 128
    out = sb[:, 0:nch * (t1 - t0)].rearrange("p (c t) -> p c t", c=nch)
    in_ = dram_ap[:, t0:t1].rearrange("(c p) t -> p c t", p=128)
    return em.dma(q, out, in_)


def store_fm(em, q, dram_ap, sb, rows, t0, t1, tb):
    nch = rows // 128
    in_ = sb[:, 0:nch * (t1 - t0)].rearrange("p (c t) -> p c t", c=nch)
    out = dram_ap[:, t0:t1].rearrange("(c p) t -> p c t", p=128)
    return em.dma(q, out, in_)


class WLoader:

    def __init__(self, ctx, name, depth=4, cols=512, dtype=BF16, scoped=False):
        em = ctx.em
        self.em = em
        self.cols = cols
        alloc = em.sbuf_scoped if scoped else em.sbuf
        self.f32 = [alloc(f"{name}_wf{i}", [128, cols], F32) for i in range(depth)]
        self.b16 = [alloc(f"{name}_wb{i}", [128, cols], dtype) for i in range(depth)]
        self.i = 0
        self.depth = depth

    def getf32(self, dram, r0, c0, rows=128, cols=128, q="gpsimd"):
        i = self.i % self.depth
        self.i += 1
        f = self.f32[i]
        self.em.dma(q, f[0:rows, 0:cols], dram[r0:r0 + rows, c0:c0 + cols])
        return f[0:rows, 0:cols]

    def get(self, dram, r0, c0, rows=128, cols=128, q=None, scale_col=None):
        i = self.i % self.depth
        self.i += 1
        if q is None:
            q = "gpsimd" if (self.i % 2) else "sync"
        f, b = self.f32[i], self.b16[i]
        self.em.dma(q, f[0:rows, 0:cols], dram[r0:r0 + rows, c0:c0 + cols])
        if scale_col is not None:
            self.em.emit("vector",
                         lambda e: e.tensor_single_scalar(b[0:rows, 0:cols], f[0:rows, 0:cols],
                                                          scale_col, ALU.mult),
                         [f, scale_col], [b])
        else:
            ce = "vector" if (self.i % 2) else "scalar"
            if ce == "scalar":
                self.em.act(b[0:rows, 0:cols], f[0:rows, 0:cols], AF.Copy)
            else:
                self.em.copy(b[0:rows, 0:cols], f[0:rows, 0:cols])
        return b[0:rows, 0:cols]


def emit_ln_rows(ctx, x_sb, tb, ps_a, ps_b, mean_row, rstd_row, eps, nch=NCH,
                 sq_scratch=None, dm=None, rms=False, ones=None):
    em = ctx.em
    dm = dm or (128 * nch)
    for c in range(nch):
        xc = chunk(x_sb, c, tb)
        if not rms:
            em.mm(ps_a[:, 0:tb], ones[:, :], xc, start=(c == 0), stop=(c == nch - 1))
        em.act(sq_scratch[:, 0:tb], xc, AF.Square)
        em.mm(ps_b[:, 0:tb], ctx.ones_col32[:, :], sq_scratch[:, 0:tb],
              start=(c == 0), stop=(c == nch - 1))
    if not rms:
        em.act(mean_row, ps_a[:, 0:tb], AF.Copy, scale=1.0 / dm)
        em.act(ctx.r1[:, 0:tb], mean_row, AF.Square)
        em.act(ctx.r2[:, 0:tb], ps_b[:, 0:tb], AF.Copy, scale=1.0 / dm)
        em.tt(ctx.r2[:, 0:tb], ctx.r2[:, 0:tb], ctx.r1[:, 0:tb], ALU.subtract)
        em.act(ctx.r1[:, 0:tb], ctx.r2[:, 0:tb], AF.Sqrt, bias=eps)
        em.emit("vector", lambda e: e.reciprocal(rstd_row, ctx.r1[:, 0:tb]),
                [ctx.r1], [rstd_row])
    else:
        em.act(ctx.r1[:, 0:tb], ps_b[:, 0:tb], AF.Sqrt, bias=eps, scale=1.0 / dm)
        em.emit("vector", lambda e: e.reciprocal(rstd_row, ctx.r1[:, 0:tb]),
                [ctx.r1], [rstd_row])


def emit_ln_apply(ctx, out_sb, x_sb, tb, mean_row, rstd_row, mb, rb, ps, gain_col=None,
                  nch=NCH, rms=False):
    em = ctx.em
    row_bcast(ctx, rb, rstd_row[:, 0:tb], tb, ps)
    if not rms:
        em.tt(ctx.r1[:, 0:tb], mean_row[:, 0:tb], rstd_row[:, 0:tb], ALU.mult)
        row_bcast(ctx, mb, ctx.r1[:, 0:tb], tb, ps)
    for c in range(nch):
        xc = chunk(x_sb, c, tb)
        oc = chunk(out_sb, c, tb)
        t = ctx.tmp_f[:, 0:tb]
        em.tt(t, xc, rb[:, 0:tb], ALU.mult)
        if not rms:
            em.tt(t, t, mb[:, 0:tb], ALU.subtract)
        if gain_col is not None:
            em.ts(oc, t, gain_col[c * 128:(c + 1) * 128, 0:1], ALU.mult)
        else:
            em.copy(oc, t)


def matmul_fm(ctx, out_sb, out_m0, x_sb, tb, wl, wdram, Ktot, m0, m1, epilog=None,
              nblocks=1, xstride=None, out_dtype=None, kstart=0):
    em = ctx.em
    nk = Ktot // 128
    mlo, mhi = m0 // 128, m1 // 128
    mgroup = max(1, 4 // nblocks)
    for mg0 in range(mlo, mhi, mgroup):
        mcs = list(range(mg0, min(mg0 + mgroup, mhi)))
        ps = {}
        for i, mc in enumerate(mcs):
            for blk in range(nblocks):
                ps[(i, blk)] = ctx.psum_pool[i * nblocks + blk]
        for k in range(nk):
            w = wl.get(wdram, kstart + k * 128, mg0 * 128, cols=128 * len(mcs))
            for i, mc in enumerate(mcs):
                for blk in range(nblocks):
                    em.mm(ps[(i, blk)][:, 0:tb], w[0:128, i * 128:(i + 1) * 128],
                          x_sb[blk](k), start=(k == 0), stop=(k == nk - 1))
        for i, mc in enumerate(mcs):
            for blk in range(nblocks):
                if epilog is None:
                    oc = chunk(out_sb[blk], mc - out_m0 // 128, tb)
                    em.act(oc, ps[(i, blk)][:, 0:tb], AF.Copy)
                else:
                    epilog(blk, mc, ps[(i, blk)][:, 0:tb], None)


def ctx_init(ctx):
    em = ctx.em
    em.memset(ctx.ones_col_f[:, :], 1.0)
    em.memset(ctx.ones_col32[:, :], 1.0)
    em.memset(ctx.ones_row_f[:, :], 1.0)
    em.memset(ctx.ident[:, :], 1.0)
    em.affsel(ctx.ident[:, :], ctx.ident[:, :], [[-1, 128]], ALU.is_ge, 0.0, 0, 1)
    em.affsel(ctx.ident[:, :], ctx.ident[:, :], [[1, 128]], ALU.is_ge, 0.0, 0, -1)


def make_ctx(em, tb=512):
    ctx = Ctx.__new__(Ctx)
    ctx.em = em
    ctx.uid = 0
    ctx.ones_col_f = em.sbuf("ones_col_f", [128, 1], BF16)
    ctx.ones_col32 = em.sbuf("ones_col32", [128, 1], F32)
    ctx.ones_row_f = em.sbuf("ones_row_f", [1, 128], F32)
    ctx.ident = em.sbuf("ident_bf", [128, 128], BF16)
    ctx_init(ctx)
    ctx.r1 = em.sbuf("row1", [1, tb], F32)
    ctx.r2 = em.sbuf("row2", [1, tb], F32)
    ctx.tmp_f = em.sbuf("tmp_f", [128, tb], F32)
    ctx.sq_f = em.sbuf("sq_f", [128, tb], F32)
    ctx.rb = em.sbuf("rb_bc", [128, tb], F32)
    ctx.mb = em.sbuf("mb_bc", [128, tb], F32)
    ctx.psum_pool = [em.psum(f"pp{i}", [128, 512], F32) for i in range(4)]
    ctx.ps_row = em.psum("ps_row", [1, 512], F32)
    ctx.ps_row2 = em.psum("ps_row2", [1, 512], F32)
    ctx.ps_bc = em.psum("ps_bc", [128, 512], F32)
    ctx.ps_t = em.psum("ps_tt", [128, 128], BF16)
    return ctx



def build_launch0():
    nc = bass.Bass("TRN2", target_bir_lowering=False)
    em = Em(nc)
    TQ = 1024
    TBB = 512

    def din(name, shape, dtype=F32):
        return em.dram(name, shape, dtype, kind="ExternalInput")

    def dout(name, shape, dtype=BF16):
        return em.dram(name, shape, dtype, kind="ExternalOutput")

    xT = din("xT", [D, TQ])
    wqa = din("wqa", [D, D])
    wkv = din("wkv", [D, 256])
    gnorm = din("gnorm", [D, 1])
    ln1 = din("ln1", [D, 1])
    wq0 = din("wq0", [D, D])
    wk0 = din("wk0", [D, D])
    wv0 = din("wv0", [D, D])
    o_xn = dout("o_xn", [D, TQ])
    o_qhat = dout("o_qhat", [D, TQ])
    o_khat = dout("o_khat", [128, TQ])
    o_va = dout("o_va", [TQ, 128])
    o_q0 = dout("o_q0", [D, TQ])
    o_k0 = dout("o_k0", [D, TQ])
    o_v0 = dout("o_v0", [TQ, D])

    ctx = make_ctx(em, tb=512)
    x_f = em.sbuf("x_f", [128, NCH * TBB], F32)
    xn = em.sbuf("xn", [128, NCH * TQ], BF16)
    xn2 = em.sbuf("xn2", [128, NCH * TQ], BF16)
    xp = em.sbuf("xprime", [128, NCH * TQ], BF16)
    gcol = em.sbuf("gcol", [128, 16], F32)
    l1col = em.sbuf("l1col", [128, 16], F32)
    load_col(em, "sync", gcol, gnorm.ap(), 16)
    load_col(em, "sync", l1col, ln1.ap(), 16)
    mrow = em.sbuf("mrow", [1, TBB], F32)
    rrow = em.sbuf("rrow", [1, TBB], F32)
    out_t = [em.sbuf(f"out_t{i}", [128, TBB], BF16) for i in range(4)]
    otc = [0]

    def otile():
        t = out_t[otc[0] % 4]
        otc[0] += 1
        return t

    wl = WLoader(ctx, "wl", depth=4, cols=512)

    def xn_c(c, t0, tbb=TBB):
        return xn[:, c * TQ + t0: c * TQ + t0 + tbb]

    for blk in range(2):
        t0 = blk * TBB
        load_fm(em, "sync", x_f, xT.ap(), D, t0, t0 + TBB, TBB)
        emit_ln_rows(ctx, x_f, TBB, ctx.ps_row, ctx.ps_row2, mrow[:, 0:TBB],
                     rrow[:, 0:TBB], 1e-5, sq_scratch=ctx.sq_f, ones=ctx.ones_col32)
        row_bcast(ctx, ctx.rb, rrow[:, 0:TBB], TBB, ctx.ps_bc)
        em.tt(ctx.r1[:, 0:TBB], mrow[:, 0:TBB], rrow[:, 0:TBB], ALU.mult)
        row_bcast(ctx, ctx.mb, ctx.r1[:, 0:TBB], TBB, ctx.ps_bc)
        for c in range(NCH):
            em.tt(ctx.tmp_f[:, 0:TBB], chunk(x_f, c, TBB), ctx.rb[:, 0:TBB], ALU.mult)
            em.tt(xn_c(c, t0), ctx.tmp_f[:, 0:TBB], ctx.mb[:, 0:TBB], ALU.subtract)
    store_fm(em, "sync", o_xn.ap(), xn, D, 0, TQ, TQ)

    for blk in range(2):
        t0 = blk * TBB
        for c in range(NCH):
            xc = xn_c(c, t0)
            em.mm(ctx.ps_row[:, 0:TBB], ctx.ones_col_f[:, :], xc, start=(c == 0),
                  stop=(c == NCH - 1))
            em.act(ctx.sq_f[:, 0:TBB], xc, AF.Square)
            em.mm(ctx.ps_row2[:, 0:TBB], ctx.ones_col32[:, :], ctx.sq_f[:, 0:TBB],
                  start=(c == 0), stop=(c == NCH - 1))
        em.act(mrow[:, 0:TBB], ctx.ps_row[:, 0:TBB], AF.Copy, scale=1.0 / D)
        em.act(ctx.r1[:, 0:TBB], mrow[:, 0:TBB], AF.Square)
        em.act(ctx.r2[:, 0:TBB], ctx.ps_row2[:, 0:TBB], AF.Copy, scale=1.0 / D)
        em.tt(ctx.r2[:, 0:TBB], ctx.r2[:, 0:TBB], ctx.r1[:, 0:TBB], ALU.subtract)
        em.act(ctx.r1[:, 0:TBB], ctx.r2[:, 0:TBB], AF.Sqrt, bias=1e-5)
        em.emit("vector", lambda e: e.reciprocal(rrow[:, 0:TBB], ctx.r1[:, 0:TBB]),
                [ctx.r1], [rrow])
        row_bcast(ctx, ctx.rb, rrow[:, 0:TBB], TBB, ctx.ps_bc)
        em.tt(ctx.r1[:, 0:TBB], mrow[:, 0:TBB], rrow[:, 0:TBB], ALU.mult)
        row_bcast(ctx, ctx.mb, ctx.r1[:, 0:TBB], TBB, ctx.ps_bc)
        for c in range(NCH):
            em.tt(ctx.tmp_f[:, 0:TBB], xn_c(c, t0), ctx.rb[:, 0:TBB], ALU.mult)
            em.tt(ctx.tmp_f[:, 0:TBB], ctx.tmp_f[:, 0:TBB], ctx.mb[:, 0:TBB], ALU.subtract)
            em.ts(xn2[:, c * TQ + t0: c * TQ + t0 + TBB], ctx.tmp_f[:, 0:TBB],
                  gcol[:, c:c + 1], ALU.mult)

    for blk in range(2):
        t0 = blk * TBB
        for c in range(NCH):
            em.act(ctx.sq_f[:, 0:TBB], xn_c(c, t0), AF.Square)
            em.mm(ctx.ps_row2[:, 0:TBB], ctx.ones_col32[:, :], ctx.sq_f[:, 0:TBB],
                  start=(c == 0), stop=(c == NCH - 1))
        em.act(ctx.r1[:, 0:TBB], ctx.ps_row2[:, 0:TBB], AF.Ln, bias=1e-6, scale=1.0 / D)
        em.act(rrow[:, 0:TBB], ctx.r1[:, 0:TBB], AF.Exp, scale=-0.5)
        row_bcast(ctx, ctx.rb, rrow[:, 0:TBB], TBB, ctx.ps_bc)
        for c in range(NCH):
            em.tt(ctx.tmp_f[:, 0:TBB], xn_c(c, t0), ctx.rb[:, 0:TBB], ALU.mult)
            em.ts(xp[:, c * TQ + t0: c * TQ + t0 + TBB], ctx.tmp_f[:, 0:TBB],
                  l1col[:, c:c + 1], ALU.mult)

    LN64 = float(np.log(64.0))
    LN4 = float(np.log(4.0))

    def blocks_of(sb):
        def mk(t0):
            return lambda c: sb[:, c * TQ + t0: c * TQ + t0 + TBB]
        return [mk(0), mk(TBB)]

    xn2_blocks = blocks_of(xn2)
    xp_blocks = blocks_of(xp)

    def ep_l2(dst_dram, lnc, sqscale):
        def ep(blk, mc, ps, oc):
            t0 = blk * TBB
            em.act(ctx.sq_f[:, 0:TBB], ps, AF.Square)
            em.mm(ctx.ps_row[:, 0:TBB], ctx.ones_col32[:, :], ctx.sq_f[:, 0:TBB])
            em.act(ctx.r1[:, 0:TBB], ctx.ps_row[:, 0:TBB], AF.Ln, bias=1e-12, scale=sqscale)
            em.act(ctx.r2[:, 0:TBB], ctx.r1[:, 0:TBB], AF.Exp, scale=-0.5, bias=lnc)
            row_bcast(ctx, ctx.rb, ctx.r2[:, 0:TBB], TBB, ctx.ps_bc)
            ot = otile()
            em.tt(ot[:, 0:TBB], ps, ctx.rb[:, 0:TBB], ALU.mult)
            em.dma("sync", dst_dram[mc * 128:(mc + 1) * 128, t0:t0 + TBB], ot[:, 0:TBB])
        return ep

    matmul_fm(ctx, None, 0, xn2_blocks, TBB, wl, wqa.ap(), D, 0, D,
              epilog=ep_l2(o_qhat.ap(), LN64, 256.0), nblocks=2)
    matmul_fm(ctx, None, 0, xn2_blocks, TBB, wl, wkv.ap(), D, 0, 128,
              epilog=ep_l2(o_khat.ap(), LN4, 1.0), nblocks=2)

    v_fm = em.sbuf("v_fm", [128, TQ], BF16)

    def ep_v(blk, mc, ps, oc):
        t0 = blk * TBB
        em.act(v_fm[:, t0:t0 + TBB], ps, AF.Copy)

    matmul_fm(ctx, None, 0, xn2_blocks, TBB, wl, wkv.ap(), D, 128, 256,
              epilog=ep_v, nblocks=2)
    ps_t = em.psum("ps_t", [128, 128], BF16)
    for ti in range(TQ // 128):
        em.trans(ps_t[:, :], v_fm[:, ti * 128:(ti + 1) * 128], ctx.ident[:, :])
        ot = otile()
        em.act(ot[:, 0:128], ps_t[:, :], AF.Copy)
        em.dma("sync", o_va.ap()[ti * 128:(ti + 1) * 128, :], ot[:, 0:128])

    def ep_store(dst):
        def ep(blk, mc, ps, oc):
            t0 = blk * TBB
            ot = otile()
            em.act(ot[:, 0:TBB], ps, AF.Copy)
            em.dma("sync", dst[mc * 128:(mc + 1) * 128, t0:t0 + TBB], ot[:, 0:TBB])
        return ep

    matmul_fm(ctx, None, 0, xp_blocks, TBB, wl, wq0.ap(), D, 0, D,
              epilog=ep_store(o_q0.ap()), nblocks=2)
    matmul_fm(ctx, None, 0, xp_blocks, TBB, wl, wk0.ap(), D, 0, D,
              epilog=ep_store(o_k0.ap()), nblocks=2)

    def ep_v0(blk, mc, ps, oc):
        t0 = blk * TBB
        v_loc = otile()
        em.act(v_loc[:, 0:TBB], ps, AF.Copy)
        for tj in range(TBB // 128):
            em.trans(ps_t[:, :], v_loc[:, tj * 128:(tj + 1) * 128], ctx.ident[:, :])
            ot = otile()
            em.act(ot[:, 0:128], ps_t[:, :], AF.Copy)
            em.dma("sync", o_v0.ap()[t0 + tj * 128: t0 + (tj + 1) * 128,
                                     mc * 128:(mc + 1) * 128], ot[:, 0:128])

    matmul_fm(ctx, None, 0, xp_blocks, TBB, wl, wv0.ap(), D, 0, D,
              epilog=ep_v0, nblocks=2)

    em.finish()
    return em.build(), em


def _fm(x):
    return np.ascontiguousarray(np.swapaxes(np.asarray(x), -1, -2))


def _col(w):
    return np.ascontiguousarray(np.asarray(w, np.float32).reshape(-1, 1))



def attn_transposed(ctx, em, out_chunk_fn, qh_fn, k_sb_fn, v_sb_fn, heads, TQ,
                    jmax_fn, mask_master=None, g0=None, p_pool=None, extra_scale=None):
    ps_s = ctx.psum_pool[2]
    ps_o = ctx.psum_pool[3]
    for h in heads:
        for i0 in range(0, TQ, 512):
            njc = jmax_fn(i0)
            for jc in range(njc):
                em.mm(ps_s[:, 0:512], k_sb_fn(h, jc), qh_fn(h)[:, i0:i0 + 512])
                p = p_pool[jc % 3]
                if extra_scale is None:
                    em.act(p[:, 0:512], ps_s[:, 0:512], AF.Exp)
                else:
                    em.act(p[:, 0:512], ps_s[:, 0:512], AF.Exp, scale=extra_scale)
                if mask_master is not None:
                    delta = (g0 + i0) - jc * 128
                    if delta < 512:
                        em.tt(p[:, 0:512], p[:, 0:512],
                              mask_master[:, delta + 512: delta + 1024], ALU.mult)
                em.mm(ctx.ps_row[:, 0:512], ctx.ones_col_f[:, :], p[:, 0:512],
                      start=(jc == 0), stop=(jc == njc - 1))
                em.mm(ps_o[:, 0:512], v_sb_fn(h, jc), p[:, 0:512],
                      start=(jc == 0), stop=(jc == njc - 1))
            em.act(ctx.r1[:, 0:512], ctx.ps_row[:, 0:512], AF.Copy)
            em.emit("vector", lambda e, r1=ctx.r1, r2=ctx.r2:
                    e.reciprocal(r2[:, 0:512], r1[:, 0:512]), [ctx.r1], [ctx.r2])
            row_bcast(ctx, ctx.rb, ctx.r2[:, 0:512], 512, ctx.ps_bc)
            out_chunk_fn(h, i0, ps_o[:, 0:512], ctx.rb)


def build_launch1(prefix_chunks):
    nc = bass.Bass("TRN2", target_bir_lowering=False)
    em = Em(nc)
    TQ = 1024
    TBB = 512

    def din(name, shape, dtype=F32):
        return em.dram(name, shape, dtype, kind="ExternalInput")

    xn_d = din("xn", [D, TQ], BF16)
    qhat_d = din("qhat", [D, TQ], BF16)
    khat_d = din("khat", [128, T], BF16)
    va_d = din("va", [T, 128], BF16)
    q0_d = din("q0", [D, TQ], BF16)
    k0_d = din("k0", [D, T], BF16)
    v0_d = din("v0", [T, D], BF16)
    x_d = din("x", [D, TQ], F32)
    wout = din("wout", [D, D])
    gout = din("gout", [D, 1])
    wo0 = din("wo0", [D, D])
    a_d = din("a_lora", [D, 16])
    b_d = din("b_lora", [16, 8192])
    ln2 = din("ln2", [D, 1])
    w_in = din("w_in", [D, 8192])
    w_up = din("w_up", [D, 8192])
    w_dn = din("w_dn", [8192, D])
    o_x1 = em.dram("o_x1", [D, TQ], F32, kind="ExternalOutput")

    ctx = make_ctx(em, tb=512)
    wl = WLoader(ctx, "wl", depth=4, cols=512)
    xn = em.sbuf("xn", [128, NCH * TQ], BF16)
    bufA = em.sbuf("bufA", [128, NCH * TQ], BF16)
    bufB = em.sbuf("bufB", [128, NCH * TQ], BF16)
    bufC = em.sbuf("bufC", [128, NCH * TQ], BF16)
    khat = em.sbuf("khat", [128, T], BF16)
    vak = em.sbuf("vak", [128, 32 * 128], BF16)
    k0h = [em.sbuf("k0h0", [128, T], BF16)]
    v0h = [em.sbuf("v0h0", [128, 32 * 128], BF16)]
    p_pool = [em.sbuf(f"pp_sm{i}", [128, 512], BF16) for i in range(3)]
    g_col = em.sbuf("g_col", [128, 16], F32)
    ln2c = em.sbuf("ln2c", [128, 16], F32)
    modr = em.sbuf("modr", [16, TQ], F32)
    mgr = em.sbuf("mgr", [16, TQ], BF16)
    xchunk = em.sbuf("xchunk", [128, 256], F32)
    gate_t = em.sbuf("gate_t", [128, 512], BF16)
    up_t = em.sbuf("up_t", [128, 512], BF16)
    mask = em.sbuf("maskc", [128, 1536], BF16)

    load_col(em, "sync", g_col, gout.ap(), 16)
    load_col(em, "sync", ln2c, ln2.ap(), 16)
    load_fm(em, "sync", xn, xn_d.ap(), D, 0, TQ, TQ)
    load_fm(em, "sync", bufA, qhat_d.ap(), D, 0, TQ, TQ)
    em.dma("sync", khat[:, :], khat_d.ap())
    for sq in range(8):
        em.dma("sync", vak[:, :].rearrange("p (c f) -> p c f", c=32)[:, sq * 4:(sq + 1) * 4, :],
               va_d.ap().rearrange("(c p) f -> p c f", p=128)[:, sq * 4:(sq + 1) * 4, :])
    em.memset(mask[:, :], 1.0)
    em.affsel(mask[:, :], mask[:, :], [[1, 1536]], ALU.is_ge, 0.0, -512, -1)

    def qh_fn(h):
        return bufA[:, h * TQ:(h + 1) * TQ]

    def out_attn(h, i0, ps_o, rb):
        em.tt(bufB[:, h * TQ + i0: h * TQ + i0 + 512], ps_o, rb[:, 0:512], ALU.mult)

    attn_transposed(ctx, em, out_attn, qh_fn,
                    lambda h, jc: khat[:, jc * 128:(jc + 1) * 128],
                    lambda h, jc: vak[:, jc * 128:(jc + 1) * 128],
                    list(range(16)), TQ, lambda i0: 32, p_pool=p_pool)

    sblocks = [lambda c: bufB[:, c * TQ: c * TQ + TBB],
               lambda c: bufB[:, c * TQ + TBB: c * TQ + 2 * TBB]]

    def ep_spre(blk, mc, ps, oc):
        em.act(bufC[:, mc * TQ + blk * TBB: mc * TQ + blk * TBB + TBB], ps, AF.Copy)

    matmul_fm(ctx, None, 0, sblocks, TBB, wl, wout.ap(), D, 0, D, epilog=ep_spre, nblocks=2)
    mrow = em.sbuf("mrow", [1, TBB], F32)
    rrow = em.sbuf("rrow", [1, TBB], F32)
    for blk in range(2):
        t0 = blk * TBB
        for c in range(NCH):
            xc = bufC[:, c * TQ + t0: c * TQ + t0 + TBB]
            em.mm(ctx.ps_row[:, 0:TBB], ctx.ones_col_f[:, :], xc, start=(c == 0),
                  stop=(c == NCH - 1))
            em.act(ctx.sq_f[:, 0:TBB], xc, AF.Square)
            em.mm(ctx.ps_row2[:, 0:TBB], ctx.ones_col32[:, :], ctx.sq_f[:, 0:TBB],
                  start=(c == 0), stop=(c == NCH - 1))
        em.act(mrow[:, 0:TBB], ctx.ps_row[:, 0:TBB], AF.Copy, scale=1.0 / D)
        em.act(ctx.r1[:, 0:TBB], mrow[:, 0:TBB], AF.Square)
        em.act(ctx.r2[:, 0:TBB], ctx.ps_row2[:, 0:TBB], AF.Copy, scale=1.0 / D)
        em.tt(ctx.r2[:, 0:TBB], ctx.r2[:, 0:TBB], ctx.r1[:, 0:TBB], ALU.subtract)
        em.act(ctx.r1[:, 0:TBB], ctx.r2[:, 0:TBB], AF.Sqrt, bias=1e-5)
        em.emit("vector", lambda e: e.reciprocal(rrow[:, 0:TBB], ctx.r1[:, 0:TBB]),
                [ctx.r1], [rrow])
        row_bcast(ctx, ctx.rb, rrow[:, 0:TBB], TBB, ctx.ps_bc)
        em.tt(ctx.r1[:, 0:TBB], mrow[:, 0:TBB], rrow[:, 0:TBB], ALU.mult)
        row_bcast(ctx, ctx.mb, ctx.r1[:, 0:TBB], TBB, ctx.ps_bc)
        for c in range(NCH):
            xc = bufC[:, c * TQ + t0: c * TQ + t0 + TBB]
            em.tt(ctx.tmp_f[:, 0:TBB], xc, ctx.rb[:, 0:TBB], ALU.mult)
            em.tt(ctx.tmp_f[:, 0:TBB], ctx.tmp_f[:, 0:TBB], ctx.mb[:, 0:TBB], ALU.subtract)
            em.ts(bufB[:, c * TQ + t0: c * TQ + t0 + TBB], ctx.tmp_f[:, 0:TBB],
                  g_col[:, c:c + 1], ALU.mult)

    load_fm(em, "sync", bufA, q0_d.ap(), D, 0, TQ, TQ)

    def load_head_kv(h, slot):
        npc = prefix_chunks
        em.dma("sync", k0h[slot][:, 0:npc * 128], k0_d.ap()[h * 128:(h + 1) * 128, 0:npc * 128])
        nseg = (npc + 3) // 4
        for sq in range(nseg):
            c1 = min((sq + 1) * 4, npc)
            em.dma("sync",
                   v0h[slot][:, :].rearrange("p (c f) -> p c f", c=32)[:, sq * 4:c1, :],
                   v0_d.ap()[:, h * 128:(h + 1) * 128]
                   .rearrange("(c p) f -> p c f", p=128)[:, sq * 4:c1, :])

    SC = 1.0 / float(np.sqrt(128.0))

    def out_attn0(h, i0, ps_o, rb):
        em.tt(bufA[:, h * TQ + i0: h * TQ + i0 + 512], ps_o, rb[:, 0:512], ALU.mult)

    g0 = (prefix_chunks * 128) - TQ

    for h in range(16):
        load_head_kv(h, 0)
        attn_transposed(ctx, em, out_attn0,
                        lambda hh: bufA[:, hh * TQ:(hh + 1) * TQ],
                        lambda hh, jc: k0h[0][:, jc * 128:(jc + 1) * 128],
                        lambda hh, jc: v0h[0][:, jc * 128:(jc + 1) * 128],
                        [h], TQ, lambda i0: (g0 + i0 + 512) // 128,
                        mask_master=mask, g0=g0, p_pool=p_pool, extra_scale=SC)

    ablocks = [lambda c: bufA[:, c * TQ: c * TQ + TBB],
               lambda c: bufA[:, c * TQ + TBB: c * TQ + 2 * TBB]]

    def ep_attnf(blk, mc, ps, oc):
        em.act(bufC[:, mc * TQ + blk * TBB: mc * TQ + blk * TBB + TBB], ps, AF.Copy)

    matmul_fm(ctx, None, 0, ablocks, TBB, wl, wo0.ap(), D, 0, D, epilog=ep_attnf, nblocks=2)

    ps_m = ctx.ps_bc
    for blk in range(2):
        t0 = blk * TBB
        for c in range(NCH):
            w = wl.get(a_d.ap(), c * 128, 0, rows=128, cols=16)
            em.mm(ps_m[0:16, 0:TBB], w, bufC[:, c * TQ + t0: c * TQ + t0 + TBB],
                  start=(c == 0), stop=(c == NCH - 1))
        em.act(modr[0:16, t0:t0 + TBB], ps_m[0:16, 0:TBB], AF.Tanh)
    for blk in range(2):
        t0 = blk * TBB
        for c in range(NCH):
            em.act(ctx.sq_f[:, 0:TBB], xn[:, c * TQ + t0: c * TQ + t0 + TBB], AF.Square)
            em.mm(ctx.ps_row2[:, 0:TBB], ctx.ones_col32[:, :], ctx.sq_f[:, 0:TBB],
                  start=(c == 0), stop=(c == NCH - 1))
        em.act(ctx.r1[:, 0:TBB], ctx.ps_row2[:, 0:TBB], AF.Ln, bias=1e-6, scale=1.0 / D)
        em.act(rrow[:, 0:TBB], ctx.r1[:, 0:TBB], AF.Exp, scale=-0.5)
        row_bcast(ctx, ctx.rb, rrow[:, 0:TBB], TBB, ctx.ps_bc)
        for c in range(NCH):
            em.tt(ctx.tmp_f[:, 0:TBB], xn[:, c * TQ + t0: c * TQ + t0 + TBB],
                  ctx.rb[:, 0:TBB], ALU.mult)
            em.ts(bufA[:, c * TQ + t0: c * TQ + t0 + TBB], ctx.tmp_f[:, 0:TBB],
                  ln2c[:, c:c + 1], ALU.mult)
    for blk in range(2):
        t0 = blk * TBB
        for c in range(NCH):
            w = wl.get(a_d.ap(), c * 128, 0, rows=128, cols=16)
            em.mm(ps_m[0:16, 0:TBB], w, bufA[:, c * TQ + t0: c * TQ + t0 + TBB],
                  start=(c == 0), stop=(c == NCH - 1))
        em.tt(ctx.tmp_f[0:16, 0:TBB], modr[0:16, t0:t0 + TBB], ps_m[0:16, 0:TBB], ALU.mult)
        em.ts(mgr[0:16, t0:t0 + TBB], ctx.tmp_f[0:16, 0:TBB], 0.1, ALU.mult)

    TM = 256
    inter = bufC
    for blk in range(4):
        t0 = blk * TM
        h2b = [lambda c, t0=t0: bufA[:, c * TQ + t0: c * TQ + t0 + TM]]
        for fc in range(64):
            ps = ctx.psum_pool[0]
            ps2 = ctx.psum_pool[1]
            for k in range(NCH):
                w = wl.get(w_in.ap(), k * 128, fc * 128)
                em.mm(ps[:, 0:TM], w, h2b[0](k), start=(k == 0), stop=False)
            wb = wl.get(b_d.ap(), 0, fc * 128, rows=16, cols=128)
            em.mm(ps[:, 0:TM], wb, mgr[0:16, t0:t0 + TM], start=False, stop=True)
            em.act(gate_t[:, 0:TM], ps[:, 0:TM], AF.Silu)
            for k in range(NCH):
                w = wl.get(w_up.ap(), k * 128, fc * 128)
                em.mm(ps2[:, 0:TM], w, h2b[0](k), start=(k == 0), stop=(k == NCH - 1))
            em.act(up_t[:, 0:TM], ps2[:, 0:TM], AF.Copy)
            em.tt(inter[:, fc * TM:(fc + 1) * TM], gate_t[:, 0:TM], up_t[:, 0:TM], ALU.mult)
        for dmc in range(NCH):
            ps = ctx.psum_pool[0]
            for fc in range(64):
                w = wl.get(w_dn.ap(), fc * 128, dmc * 128)
                em.mm(ps[:, 0:TM], w, inter[:, fc * TM:(fc + 1) * TM],
                      start=(fc == 0), stop=(fc == 63))
            em.tt(ctx.tmp_f[:, 0:TM], ps[:, 0:TM], xn[:, dmc * TQ + t0: dmc * TQ + t0 + TM],
                  ALU.add)
            em.act(gate_t[:, 0:TM], ctx.tmp_f[:, 0:TM], AF.Silu)
            em.tt(gate_t[:, 0:TM], gate_t[:, 0:TM],
                  bufB[:, dmc * TQ + t0: dmc * TQ + t0 + TM], ALU.mult)
            em.dma("sync", xchunk[:, 0:TM], x_d.ap()[dmc * 128:(dmc + 1) * 128, t0:t0 + TM])
            em.tt(ctx.tmp_f[:, 0:TM], xchunk[:, 0:TM], gate_t[:, 0:TM], ALU.add)
            em.dma("sync", o_x1.ap()[dmc * 128:(dmc + 1) * 128, t0:t0 + TM],
                   ctx.tmp_f[:, 0:TM])

    em.finish()
    return em.build(), em



def build_launch2(prefix_blocks):
    nc = bass.Bass("TRN2", target_bir_lowering=False)
    em = Em(nc)
    TQ = 1024
    TBB = 512
    TEND = prefix_blocks * 512
    OWN0 = TEND - TQ

    def din(name, shape, dtype=F32):
        return em.dram(name, shape, dtype, kind="ExternalInput")

    x1_d = din("x1f", [D, T])
    wx_d = din("wxT", [D, 96])
    wdt_d = din("wdtT", [64, D])
    bdt_d = din("bdt", [D, 1])
    alog_d = din("alog", [D, 16])
    dcol_d = din("dcol", [D, 1])
    o_s1 = em.dram("o_s1", [D, TQ], F32, kind="ExternalOutput")
    o_xn1 = em.dram("o_xn1", [D, TQ], BF16, kind="ExternalOutput")
    bc_scratch = em.dram("bc_scr", [32, T], F32)

    ctx = make_ctx(em, tb=512)
    wl = WLoader(ctx, "wl", depth=4, cols=512)

    xblk = em.sbuf("xblk", [128, NCH * TBB], F32)
    xp = em.sbuf("xp", [96, T], F32)
    mrow_all = em.sbuf("mrow_all", [1, T], F32)
    rrow_all = em.sbuf("rrow_all", [1, T], F32)
    w1s = em.sbuf("w1s", [96, 1], F32)
    acol = em.sbuf("acol", [128, 16 * 16], F32)
    bdtc = em.sbuf("bdtc", [128, 16], F32)
    dpc = em.sbuf("dpc", [128, 16], F32)
    xn1dc = em.sbuf("xn1dc", [128, T], BF16)
    dtdc = em.sbuf("dtdc", [128, T], F32)
    s1p = em.sbuf("s1p", [128, T], BF16)
    dA = em.sbuf("dAt", [128, T], BF16)
    ut = em.sbuf("ut", [128, T], BF16)
    ht = em.sbuf("ht", [128, T], BF16)
    ydc = em.sbuf("ydc", [128, TQ], F32)
    yt32 = em.sbuf("yt32", [128, TQ], F32)
    bbc = em.sbuf("bbc", [128, TBB], BF16)
    cbc = em.sbuf("cbc", [128, TQ], BF16)
    brow = em.sbuf("brow", [1, T], F32)
    crow = em.sbuf("crow", [1, T], F32)
    zrow = em.sbuf("zrow", [128, 1], F32)
    em.memset(zrow[:, :], 0.0)

    load_col(em, "sync", bdtc, bdt_d.ap(), 16)
    load_col(em, "sync", dpc, dcol_d.ap(), 16)
    for c in range(NCH):
        em.dma("sync", ctx.tmp_f[:, 0:16], alog_d.ap()[c * 128:(c + 1) * 128, :])
        em.act(acol[:, c * 16:(c + 1) * 16], ctx.tmp_f[:, 0:16], AF.Exp)
    em.ts(acol[:, :], acol[:, :], -1.0, ALU.mult)

    for k in range(NCH):
        w = wl.getf32(wx_d.ap(), k * 128, 0, rows=128, cols=96)
        em.mm(ctx.ps_bc[0:96, 0:1], w, ctx.ones_col32[:, :], start=(k == 0),
              stop=(k == NCH - 1))
    em.act(w1s[:, :], ctx.ps_bc[0:96, 0:1], AF.Copy)

    for blk in range(prefix_blocks):
        t0 = blk * TBB
        load_fm(em, "sync", xblk, x1_d.ap(), D, t0, t0 + TBB, TBB)
        emit_ln_rows(ctx, xblk, TBB, ctx.ps_row, ctx.ps_row2,
                     mrow_all[:, t0:t0 + TBB], rrow_all[:, t0:t0 + TBB],
                     1e-5, sq_scratch=ctx.sq_f, ones=ctx.ones_col32)
        for k in range(NCH):
            w = wl.getf32(wx_d.ap(), k * 128, 0, rows=128, cols=96)
            em.mm(ctx.ps_bc[0:96, 0:TBB], w, chunk(xblk, k, TBB), start=(k == 0),
                  stop=(k == NCH - 1))
        em.act(xp[0:96, t0:t0 + TBB], ctx.ps_bc[0:96, 0:TBB], AF.Copy)
    for blk in range(prefix_blocks):
        t0 = blk * TBB
        em.mm(ctx.ps_bc[0:96, 0:TBB], ctx.ones_row_f[:, 0:96], mrow_all[:, t0:t0 + TBB])
        em.ts(ctx.tmp_f[0:96, 0:TBB], ctx.ps_bc[0:96, 0:TBB], w1s[0:96, 0:1], ALU.mult)
        em.tt(ctx.tmp_f[0:96, 0:TBB], xp[0:96, t0:t0 + TBB], ctx.tmp_f[0:96, 0:TBB],
              ALU.subtract)
        em.mm(ctx.ps_bc[0:96, 0:TBB], ctx.ones_row_f[:, 0:96], rrow_all[:, t0:t0 + TBB])
        em.emit("vector", lambda e, t0=t0: e.tensor_tensor(
            xp[0:96, t0:t0 + TBB], ctx.tmp_f[0:96, 0:TBB], ctx.ps_bc[0:96, 0:TBB],
            ALU.mult), [ctx.tmp_f, ctx.ps_bc], [xp])
    em.dma("sync", bc_scratch.ap()[:, 0:TEND], xp[64:96, 0:TEND])

    for dc in range(NCH):
        for blk in range(prefix_blocks):
            t0 = blk * TBB
            w = wl.getf32(wdt_d.ap(), 0, dc * 128, rows=64, cols=128)
            em.mm(ctx.psum_pool[0][:, 0:TBB], w, xp[0:64, t0:t0 + TBB])
            em.act(ctx.tmp_f[:, 0:TBB], ctx.psum_pool[0][:, 0:TBB], AF.Identity,
                   bias=bdtc[:, dc:dc + 1])
            em.act(ctx.sq_f[:, 0:TBB], ctx.tmp_f[:, 0:TBB], AF.Abs)
            em.act(ctx.sq_f[:, 0:TBB], ctx.sq_f[:, 0:TBB], AF.Exp, scale=-1.0)
            em.act(ctx.sq_f[:, 0:TBB], ctx.sq_f[:, 0:TBB], AF.Ln, bias=1.0)
            em.ts(ctx.tmp_f[:, 0:TBB], ctx.tmp_f[:, 0:TBB], 0.0, ALU.max)
            em.tt(dtdc[:, t0:t0 + TBB], ctx.tmp_f[:, 0:TBB], ctx.sq_f[:, 0:TBB], ALU.add)
        for blk in range(prefix_blocks):
            t0 = blk * TBB
            em.dma("sync", ctx.sq_f[:, 0:TBB],
                   x1_d.ap()[dc * 128:(dc + 1) * 128, t0:t0 + TBB])
            em.mm(ctx.ps_bc[:, 0:TBB], ctx.ones_row_f[:, :], rrow_all[:, t0:t0 + TBB])
            em.act(ctx.rb[:, 0:TBB], ctx.ps_bc[:, 0:TBB], AF.Copy)
            em.mm(ctx.ps_bc[:, 0:TBB], ctx.ones_row_f[:, :], mrow_all[:, t0:t0 + TBB])
            em.act(ctx.mb[:, 0:TBB], ctx.ps_bc[:, 0:TBB], AF.Copy)
            em.tt(ctx.tmp_f[:, 0:TBB], ctx.sq_f[:, 0:TBB], ctx.mb[:, 0:TBB], ALU.subtract)
            em.tt(xn1dc[:, t0:t0 + TBB], ctx.tmp_f[:, 0:TBB], ctx.rb[:, 0:TBB], ALU.mult)
        em.dma("sync", o_xn1.ap()[dc * 128:(dc + 1) * 128, :], xn1dc[:, OWN0:TEND])
        em.tt(s1p[:, 0:TEND], dtdc[:, 0:TEND], xn1dc[:, 0:TEND], ALU.mult)
        em.memset(ydc[:, :], 0.0)
        for n in range(16):
            em.act(dA[:, 0:TEND], dtdc[:, 0:TEND], AF.Exp,
                   scale=acol[:, dc * 16 + n: dc * 16 + n + 1])
            em.dma("sync", brow[:, 0:TEND], bc_scratch.ap()[n:n + 1, 0:TEND])
            em.dma("sync", crow[:, 0:TEND], bc_scratch.ap()[16 + n:17 + n, 0:TEND])
            for sb0 in range(0, TEND, TBB):
                em.mm(ctx.ps_bc[:, 0:TBB], ctx.ones_row_f[:, :], brow[:, sb0:sb0 + TBB])
                em.act(bbc[:, 0:TBB], ctx.ps_bc[:, 0:TBB], AF.Copy)
                em.tt(ut[:, sb0:sb0 + TBB], s1p[:, sb0:sb0 + TBB], bbc[:, 0:TBB], ALU.mult)
            em.scan(ht[:, 0:TEND], dA[:, 0:TEND], ut[:, 0:TEND], zrow[:, 0:1])
            for sb0 in range(OWN0, TEND, TBB):
                em.mm(ctx.ps_bc[:, 0:TBB], ctx.ones_row_f[:, :], crow[:, sb0:sb0 + TBB])
                em.act(cbc[:, sb0 - OWN0:sb0 - OWN0 + TBB], ctx.ps_bc[:, 0:TBB], AF.Copy)
            em.tt(yt32[:, 0:TQ], ht[:, OWN0:TEND], cbc[:, 0:TQ], ALU.mult)
            em.tt(ydc[:, 0:TQ], ydc[:, 0:TQ], yt32[:, 0:TQ], ALU.add)
        for half in range(2):
            h0 = half * TBB
            tmp = ctx.tmp_f if half == 0 else ctx.sq_f
            em.ts(tmp[:, 0:TBB], xn1dc[:, OWN0 + h0:OWN0 + h0 + TBB],
                  dpc[:, dc:dc + 1], ALU.mult)
            em.tt(tmp[:, 0:TBB], tmp[:, 0:TBB], ydc[:, h0:h0 + TBB], ALU.add)
            em.dma("sync", o_s1.ap()[dc * 128:(dc + 1) * 128, h0:h0 + TBB], tmp[:, 0:TBB])

    em.finish()
    return em.build(), em



def emit_decode(ctx, nib_i32, scale_t, out_bf, w):
    em = ctx.em
    nf = ctx.dq_nf
    g = ctx.dq_g
    j = ctx.dq_j
    r1 = ctx.dq_r1
    r2 = ctx.dq_r2
    em.copy(nf[:, 0:w], nib_i32, eng="vector")
    em.ts(g[:, 0:w], nf[:, 0:w], 8.0, ALU.is_ge, eng="vector")
    em.stt(j[:, 0:w], g[:, 0:w], -8.0, nf[:, 0:w], ALU.mult, ALU.add, eng="vector")
    em.act(r1[:, 0:w], j[:, 0:w], AF.Relu, bias=-4.0)
    em.act(r2[:, 0:w], j[:, 0:w], AF.Relu, scale=2.0, bias=-12.0)
    em.tt(j[:, 0:w], j[:, 0:w], r1[:, 0:w], ALU.add, eng="vector")
    em.tt(j[:, 0:w], j[:, 0:w], r2[:, 0:w], ALU.add, eng="vector")
    em.emit("vector", lambda e: e.tensor_scalar(out=g[:, 0:w], in0=g[:, 0:w],
                                                scalar1=-1.0, scalar2=0.5,
                                                op0=ALU.mult, op1=ALU.add),
            [g], [g])
    em.tt(j[:, 0:w], j[:, 0:w], g[:, 0:w], ALU.mult, eng="vector")
    em.tt(out_bf, j[:, 0:w], scale_t, ALU.mult, eng="vector")


class BLoader:
    def __init__(self, ctx, name, depth=4, cols=512, scoped=False):
        em = ctx.em
        self.em = em
        alloc = em.sbuf_scoped if scoped else em.sbuf
        self.b = [alloc(f"{name}_bl{i}", [128, cols], BF16) for i in range(depth)]
        self.i = 0
        self.depth = depth

    def get(self, dram, r0, c0, rows=128, cols=512, q=None):
        i = self.i % self.depth
        self.i += 1
        if q is None:
            q = "gpsimd" if (self.i % 2) else "sync"
        t = self.b[i]
        self.em.dma(q, t[0:rows, 0:cols], dram[r0:r0 + rows, c0:c0 + cols])
        return t


def build_launch3():
    nc = bass.Bass("TRN2", target_bir_lowering=False)
    em = Em(nc)
    TALL = 8192

    def din(name, shape, dtype=F32):
        return em.dram(name, shape, dtype, kind="ExternalInput")

    xn1_d = din("xn1f", [D, TALL], BF16)
    gp_d = din("gpT", [1024, 2048], I32)
    gs_d = din("gsT", [64, 2048])
    up_d = din("upT", [1024, 2048], I32)
    us_d = din("usT", [64, 2048])
    dn_d = din("dnT", [1024, 2048], I32)
    ds_d = din("dsT", [64, 2048])
    o_part = em.dram("o_part", [D, TALL], BF16, kind="ExternalOutput")
    wg_dec = em.dram("wg_dec", [2048, 2048], BF16)
    wu_dec = em.dram("wu_dec", [2048, 2048], BF16)
    wd_dec = em.dram("wd_dec", [2048, 2048], BF16)

    ctx = make_ctx(em, tb=512)
    ctx.dq_nf = em.sbuf("dq_nf", [128, 512], F32)
    ctx.dq_g = em.sbuf("dq_g", [128, 512], F32)
    ctx.dq_j = em.sbuf("dq_j", [128, 512], F32)
    ctx.dq_r1 = em.sbuf("dq_r1", [128, 512], F32)
    ctx.dq_r2 = em.sbuf("dq_r2", [128, 512], F32)
    it_raw = em.sbuf("it_raw", [128, 512], I32)
    it_lo = em.sbuf("it_lo", [128, 512], I32)
    it_hi = em.sbuf("it_hi", [128, 512], I32)
    sc8 = em.sbuf("sc8", [8, 2048], F32)
    sct = em.sbuf("sct", [128, 512], F32)
    e8 = em.sbuf("e8", [8, 128], F32)
    dec_o = [em.sbuf(f"dec_o{i}", [128, 512], BF16) for i in range(4)]
    em.memset(e8[:, :], 1.0)
    em.affsel(e8[:, :], e8[:, :], [[1, 128]], ALU.is_ge, 0.0, 0, -16)
    em.affsel(e8[:, :], e8[:, :], [[-1, 128]], ALU.is_ge, 0.0, 15, 16)

    for (pd, sd, od) in ((gp_d, gs_d, wg_dec), (up_d, us_d, wu_dec), (dn_d, ds_d, wd_dec)):
        for kc in range(8):
            em.dma("sync", sc8[:, :], sd.ap()[kc * 8:(kc + 1) * 8, :])
            for nc4 in range(4):
                n0 = nc4 * 512
                em.dma("sync", it_raw[:, :], pd.ap()[kc * 128:(kc + 1) * 128, n0:n0 + 512])
                em.mm(ctx.ps_bc[:, 0:512], e8[:, :], sc8[0:8, n0:n0 + 512].transpose([0, 1])
                      if False else sc8[:, n0:n0 + 512])
                em.act(sct[:, :], ctx.ps_bc[:, 0:512], AF.Copy)
                em.ts(it_lo[:, :], it_raw[:, :], 15, ALU.bitwise_and, eng="vector")
                em.ts(it_hi[:, :], it_raw[:, :], 4, ALU.logical_shift_right, eng="vector")
                em.ts(it_hi[:, :], it_hi[:, :], 15, ALU.bitwise_and, eng="vector")
                lo_t = dec_o[(kc * 4 + nc4) % 2]
                hi_t = dec_o[2 + (kc * 4 + nc4) % 2]
                emit_decode(ctx, it_lo[:, :], sct[:, :], lo_t[:, 0:512], 512)
                em.dma("sync", od.ap()[kc * 256:kc * 256 + 128, n0:n0 + 512], lo_t[:, 0:512])
                emit_decode(ctx, it_hi[:, :], sct[:, :], hi_t[:, 0:512], 512)
                em.dma("sync", od.ap()[kc * 256 + 128:kc * 256 + 256, n0:n0 + 512], hi_t[:, 0:512])

    xnt = em.sbuf("xnt", [128, 16 * 1024], BF16)
    inter_t = em.sbuf("inter_t", [128, 16 * 1024], BF16)
    gate_t = em.sbuf("gate_t3", [128, 512], BF16)
    up_t = em.sbuf("up_t3", [128, 512], BF16)
    out_t = [em.sbuf(f"out3_{i}", [128, 512], BF16) for i in range(2)]
    bl = BLoader(ctx, "bl", depth=4, cols=512)
    TBL = 1024
    xn1r = xn1_d.ap().rearrange("(r two) t -> two r t", two=2)
    for tb in range(TALL // TBL):
        t0 = tb * TBL
        for kc in range(8):
            em.dma("sync", xnt[:, (2 * kc) * TBL:(2 * kc + 1) * TBL],
                   xn1_d.ap()[kc * 256: (kc + 1) * 256, t0:t0 + TBL]
                   .rearrange("(p two) t -> two p t", two=2)[0])
            em.dma("sync", xnt[:, (2 * kc + 1) * TBL:(2 * kc + 2) * TBL],
                   xn1_d.ap()[kc * 256: (kc + 1) * 256, t0:t0 + TBL]
                   .rearrange("(p two) t -> two p t", two=2)[1])
        for nch in range(16):
            ps0, ps1 = ctx.psum_pool[0], ctx.psum_pool[1]
            for half in range(2):
                h0 = half * 512
                ps = ps0 if half == 0 else ps1
                for k in range(16):
                    w = bl.get(wg_dec.ap(), k * 128, nch * 128, cols=128)
                    em.mm(ps[:, 0:512], w[0:128, 0:128], xnt[:, k * TBL + h0:k * TBL + h0 + 512],
                          start=(k == 0), stop=(k == 15))
                em.act(gate_t[:, 0:512], ps[:, 0:512], AF.Gelu_apprx_tanh)
                for k in range(16):
                    w = bl.get(wu_dec.ap(), k * 128, nch * 128, cols=128)
                    em.mm(ps[:, 0:512], w[0:128, 0:128], xnt[:, k * TBL + h0:k * TBL + h0 + 512],
                          start=(k == 0), stop=(k == 15))
                em.act(up_t[:, 0:512], ps[:, 0:512], AF.Copy)
                em.tt(inter_t[:, nch * TBL + h0: nch * TBL + h0 + 512], gate_t[:, 0:512],
                      up_t[:, 0:512], ALU.mult)
        for dmc in range(16):
            for half in range(2):
                h0 = half * 512
                ps = ctx.psum_pool[2]
                for k in range(16):
                    w = bl.get(wd_dec.ap(), k * 128, dmc * 128, cols=128)
                    em.mm(ps[:, 0:512], w[0:128, 0:128],
                          inter_t[:, k * TBL + h0: k * TBL + h0 + 512],
                          start=(k == 0), stop=(k == 15))
                ot = out_t[half]
                em.act(ot[:, 0:512], ps[:, 0:512], AF.Copy)
                em.dma("sync", o_part.ap()[dmc * 128:(dmc + 1) * 128, t0 + h0:t0 + h0 + 512],
                       ot[:, 0:512])

    em.finish()
    return em.build(), em



def build_launch4(halo):
    nc = bass.Bass("TRN2", target_bir_lowering=False)
    em = Em(nc)
    TQ = 1024
    TH = TQ + halo
    NKV = 4

    def din(name, shape, dtype=F32):
        return em.dram(name, shape, dtype, kind="ExternalInput")

    x1_d = din("x1h", [D, TH])
    s1_d = din("s1h", [D, TH])
    parts = [din(f"p{i}", [D, TH], BF16) for i in range(8)]
    wq_d = din("wqT", [D, D])
    bq_d = din("bq", [D, 1])
    wk_d = din("wkT", [D, 512])
    bk_d = din("bk", [512, 1])
    wv_d = din("wvT", [D, 512])
    bv_d = din("bv", [512, 1])
    wo_d = din("woT", [D, D])
    bo_d = din("bo", [D, 1])
    qn_d = din("qnw", [128, 1])
    kn_d = din("knw", [128, 1])
    sc_d = din("att_scale", [1, 1])
    ths_d = din("thS", [64, TH])
    thc_d = din("thC", [64, TH])
    w0_d = din("w0T", [D, 8192])
    b0_d = din("b0", [8192, 1])
    w1_d = din("w1T", [8192, D])
    b1_d = din("b1", [D, 1])
    o_x3 = em.dram("o_x3", [D, TQ], F32, kind="ExternalOutput")
    x2_scr = em.dram("x2_scr", [D, TH], F32)

    ctx = make_ctx(em, tb=512)
    wl = WLoader(ctx, "wl", depth=4, cols=512)
    wl16 = WLoader(ctx, "wl16", depth=4, cols=512, dtype=F16)
    xn2 = em.sbuf("xn2", [128, NCH * TH], BF16)
    qbuf = em.sbuf("qbuf", [128, NCH * TQ], BF16)
    s2b = em.sbuf("s2b", [128, NCH * TQ], BF16)
    ks = em.sbuf("ks", [128, NKV * TH], BF16)
    vtok = em.sbuf("vtok", [128, NKV * 9 * 128], BF16)
    cosb = em.sbuf("cosb", [64, TH], BF16)
    sinb = em.sbuf("sinb", [64, TH], BF16)
    qnc = em.sbuf("qnc", [128, 1], F32)
    knc = em.sbuf("knc", [128, 1], F32)
    scol = em.sbuf("scol", [128, 1], F32)
    bqc = em.sbuf("bqc", [128, 16], F32)
    bkc = em.sbuf("bkc", [128, 4], F32)
    bvc = em.sbuf("bvc", [128, 4], F32)
    boc = em.sbuf("boc", [128, 16], F32)
    b0c = em.sbuf("b0c", [128, 64], F32)
    b1c = em.sbuf("b1c", [128, 16], F32)
    acc = [em.sbuf(f"acc{i}", [128, TH], F32) for i in range(2)]
    xn16 = em.sbuf("xn16", [128, 16 * 128], F16)
    d16 = em.sbuf("d16", [128, 64 * 128], F16)
    e16 = em.sbuf("e16", [128, 128], F16)
    g2t = em.sbuf("g2t", [128, 128], BF16)
    p_pool = [em.sbuf(f"pp_sm{i}", [128, 512], BF16) for i in range(3)]
    wmask = em.sbuf("wmask", [128, 1664], BF16)
    mrow = em.sbuf("mrow", [1, 512], F32)
    rrow = em.sbuf("rrow", [1, 512], F32)

    load_col(em, "sync", bqc, bq_d.ap(), 16)
    load_col(em, "sync", bkc, bk_d.ap(), 4)
    load_col(em, "sync", bvc, bv_d.ap(), 4)
    load_col(em, "sync", boc, bo_d.ap(), 16)
    load_col(em, "sync", b0c, b0_d.ap(), 64)
    load_col(em, "sync", b1c, b1_d.ap(), 16)
    em.dma("sync", qnc[:, :], qn_d.ap())
    em.dma("sync", knc[:, :], kn_d.ap())
    sc_sb = em.sbuf("sc_sb", [1, 1], F32)
    em.dma("sync", sc_sb[:, :], sc_d.ap())
    em.mm(ctx.ps_bc[:, 0:1], ctx.ones_row_f[:, :], sc_sb[:, :])
    em.act(scol[:, :], ctx.ps_bc[:, 0:1], AF.Copy)
    for t0 in range(0, TH, 512):
        t1 = min(t0 + 512, TH)
        em.dma("sync", ctx.tmp_f[0:64, 0:t1 - t0], ths_d.ap()[:, t0:t1])
        em.act(sinb[0:64, t0:t1], ctx.tmp_f[0:64, 0:t1 - t0], AF.Sin)
        em.dma("sync", ctx.tmp_f[0:64, 0:t1 - t0], thc_d.ap()[:, t0:t1])
        em.act(cosb[0:64, t0:t1], ctx.tmp_f[0:64, 0:t1 - t0], AF.Sin)
    em.memset(wmask[:, :], 1.0)
    em.affsel(wmask[:, :], wmask[:, :], [[1, 1664]], ALU.is_ge, 0.0, -576, -1)
    em.affsel(wmask[:, :], wmask[:, :], [[-1, 1664]], ALU.is_ge, 0.0, 576 + 64, 1)

    pstage = [em.sbuf(f"pstage{i}", [128, TH], BF16) for i in range(2)]
    for c in range(NCH):
        sl = slice(c * 128, (c + 1) * 128)
        em.dma("sync", pstage[0][:, 0:TH], parts[0].ap()[sl, :])
        em.dma("sync", pstage[1][:, 0:TH], parts[1].ap()[sl, :])
        em.tt(acc[0][:, 0:TH], pstage[0][:, 0:TH], pstage[1][:, 0:TH], ALU.add)
        for i in range(1, 4):
            em.dma("sync", pstage[0][:, 0:TH], parts[2 * i].ap()[sl, :])
            em.dma("sync", pstage[1][:, 0:TH], parts[2 * i + 1].ap()[sl, :])
            em.tt(acc[1][:, 0:TH], pstage[0][:, 0:TH], pstage[1][:, 0:TH], ALU.add)
            em.tt(acc[0][:, 0:TH], acc[0][:, 0:TH], acc[1][:, 0:TH], ALU.add)
        em.act(acc[1][:, 0:TH], acc[0][:, 0:TH], AF.Silu)
        em.dma("sync", acc[0][:, 0:TH], s1_d.ap()[sl, :])
        em.tt(acc[1][:, 0:TH], acc[1][:, 0:TH], acc[0][:, 0:TH], ALU.mult)
        em.dma("sync", acc[0][:, 0:TH], x1_d.ap()[sl, :])
        em.tt(acc[0][:, 0:TH], acc[0][:, 0:TH], acc[1][:, 0:TH], ALU.add)
        em.dma("sync", x2_scr.ap()[sl, :], acc[0][:, 0:TH])
    for t0 in range(0, TH, 512):
        t1 = min(t0 + 512, TH)
        tb = t1 - t0
        for c in range(NCH):
            em.dma("sync", ctx.sq_f[:, 0:tb], x2_scr.ap()[c * 128:(c + 1) * 128, t0:t1])
            em.mm(ctx.ps_row[:, 0:tb], ctx.ones_col32[:, :], ctx.sq_f[:, 0:tb],
                  start=(c == 0), stop=(c == NCH - 1))
            em.act(ctx.tmp_f[:, 0:tb], ctx.sq_f[:, 0:tb], AF.Square)
            em.mm(ctx.ps_row2[:, 0:tb], ctx.ones_col32[:, :], ctx.tmp_f[:, 0:tb],
                  start=(c == 0), stop=(c == NCH - 1))
        em.act(mrow[:, 0:tb], ctx.ps_row[:, 0:tb], AF.Copy, scale=1.0 / D)
        em.act(ctx.r1[:, 0:tb], mrow[:, 0:tb], AF.Square)
        em.act(ctx.r2[:, 0:tb], ctx.ps_row2[:, 0:tb], AF.Copy, scale=1.0 / D)
        em.tt(ctx.r2[:, 0:tb], ctx.r2[:, 0:tb], ctx.r1[:, 0:tb], ALU.subtract)
        em.act(ctx.r1[:, 0:tb], ctx.r2[:, 0:tb], AF.Sqrt, bias=1e-5)
        em.emit("vector", lambda e, tb=tb: e.reciprocal(rrow[:, 0:tb], ctx.r1[:, 0:tb]),
                [ctx.r1], [rrow])
        row_bcast(ctx, ctx.rb, rrow[:, 0:tb], tb, ctx.ps_bc)
        em.tt(ctx.r1[:, 0:tb], mrow[:, 0:tb], rrow[:, 0:tb], ALU.mult)
        row_bcast(ctx, ctx.mb, ctx.r1[:, 0:tb], tb, ctx.ps_bc)
        for c in range(NCH):
            em.dma("sync", ctx.sq_f[:, 0:tb], x2_scr.ap()[c * 128:(c + 1) * 128, t0:t1])
            em.tt(ctx.tmp_f[:, 0:tb], ctx.sq_f[:, 0:tb], ctx.rb[:, 0:tb], ALU.mult)
            em.tt(xn2[:, c * TH + t0:c * TH + t1], ctx.tmp_f[:, 0:tb], ctx.mb[:, 0:tb],
                  ALU.subtract)

    odt = em.sbuf("odt", [64, 512], BF16)

    def emit_rope(buf, off, n, tab0):
        ev = buf[0:64, off:off + n]
        od = buf[64:128, off:off + n]
        c_ = cosb[0:64, tab0:tab0 + n]
        s_ = sinb[0:64, tab0:tab0 + n]
        t1 = ctx.tmp_f[0:64, 0:n]
        t2 = ctx.sq_f[0:64, 0:n]
        t3 = ctx.rb[0:64, 0:n]
        t4 = ctx.mb[0:64, 0:n]
        em.copy(odt[0:64, 0:n], od)
        em.tt(t1, ev, c_, ALU.mult)
        em.tt(t2, odt[0:64, 0:n], s_, ALU.mult)
        em.tt(t3, ev, s_, ALU.mult)
        em.tt(t4, odt[0:64, 0:n], c_, ALU.mult)
        em.tt(ev, t1, t2, ALU.subtract)
        em.tt(od, t3, t4, ALU.add)

    def head_norm(ps, bias_col, w_col, out_ap, n, eps=1e-8):
        em.act(ctx.mb[:, 0:n], ps, AF.Identity, bias=bias_col)
        em.act(ctx.sq_f[:, 0:n], ctx.mb[:, 0:n], AF.Square)
        em.mm(ctx.ps_row[:, 0:n], ctx.ones_col32[:, :], ctx.sq_f[:, 0:n])
        em.act(ctx.r1[:, 0:n], ctx.ps_row[:, 0:n], AF.Ln, bias=eps, scale=1.0 / 128.0)
        em.act(ctx.r2[:, 0:n], ctx.r1[:, 0:n], AF.Exp, scale=-0.5)
        em.mm(ctx.ps_bc[:, 0:n], ctx.ones_row_f[:, :], ctx.r2[:, 0:n])
        em.act(ctx.rb[:, 0:n], ctx.ps_bc[:, 0:n], AF.Copy)
        em.tt(ctx.tmp_f[:, 0:n], ctx.mb[:, 0:n], ctx.rb[:, 0:n], ALU.mult)
        em.ts(out_ap, ctx.tmp_f[:, 0:n], w_col, ALU.mult)

    for h in range(16):
        for i0 in range(0, TQ, 512):
            ps = ctx.psum_pool[0]
            for k in range(NCH):
                w = wl.get(wq_d.ap(), k * 128, h * 128)
                em.mm(ps[:, 0:512], w, xn2[:, k * TH + halo + i0: k * TH + halo + i0 + 512],
                      start=(k == 0), stop=(k == NCH - 1))
            head_norm(ps[:, 0:512], bqc[:, h:h + 1], qnc[:, 0:1],
                      qbuf[:, h * TQ + i0: h * TQ + i0 + 512], 512)
            emit_rope(qbuf, h * TQ + i0, 512, halo + i0)
    ps_t = em.psum("ps_tr", [128, 128], BF16)
    for kv in range(NKV):
        for i0 in range(0, TH, 512):
            t1 = min(i0 + 512, TH)
            n = t1 - i0
            ps = ctx.psum_pool[0]
            for k in range(NCH):
                w = wl.get(wk_d.ap(), k * 128, kv * 128)
                em.mm(ps[:, 0:n], w, xn2[:, k * TH + i0: k * TH + t1],
                      start=(k == 0), stop=(k == NCH - 1))
            head_norm(ps[:, 0:n], bkc[:, kv:kv + 1], knc[:, 0:1],
                      ks[:, kv * TH + i0: kv * TH + t1], n)
            emit_rope(ks, kv * TH + i0, n, i0)
        for i0 in range(0, TH, 512):
            t1 = min(i0 + 512, TH)
            n = t1 - i0
            ps = ctx.psum_pool[1]
            for k in range(NCH):
                w = wl.get(wv_d.ap(), k * 128, kv * 128)
                em.mm(ps[:, 0:n], w, xn2[:, k * TH + i0: k * TH + t1],
                      start=(k == 0), stop=(k == NCH - 1))
            em.act(p_pool[0][:, 0:n], ps[:, 0:n], AF.Identity, bias=bvc[:, kv:kv + 1])
            for tj in range(0, n, 128):
                em.trans(ps_t[:, :], p_pool[0][:, tj:tj + 128], ctx.ident[:, :])
                em.act(vtok[:, kv * 1152 + i0 + tj: kv * 1152 + i0 + tj + 128],
                       ps_t[:, :], AF.Copy)

    nct = TH // 128

    def out_attn2(h, i0, ps_o, rb):
        em.tt(s2b[:, h * TQ + i0: h * TQ + i0 + 512], ps_o, rb[:, 0:512], ALU.mult)

    ps_s = ctx.psum_pool[2]
    ps_o = ctx.psum_pool[3]
    for h in range(16):
        kv = h // 4
        for i0 in range(0, TQ, 512):
            jlo = max(0, (halo + i0 - 64) // 128)
            jhi = (halo + i0 + 512 + 127) // 128
            njc = jhi - jlo
            first = True
            for jc in range(jlo, jhi):
                em.mm(ps_s[:, 0:512], ks[:, kv * TH + jc * 128: kv * TH + (jc + 1) * 128],
                      qbuf[:, h * TQ + i0: h * TQ + i0 + 512])
                p = p_pool[jc % 3]
                em.act(p[:, 0:512], ps_s[:, 0:512], AF.Exp, scale=scol[:, 0:1])
                delta = (halo + i0) - jc * 128
                em.tt(p[:, 0:512], p[:, 0:512], wmask[:, delta + 576: delta + 576 + 512],
                      ALU.mult)
                em.mm(ctx.ps_row[:, 0:512], ctx.ones_col_f[:, :], p[:, 0:512],
                      start=first, stop=(jc == jhi - 1))
                em.mm(ps_o[:, 0:512], vtok[:, kv * 1152 + jc * 128: kv * 1152 + (jc + 1) * 128],
                      p[:, 0:512], start=first, stop=(jc == jhi - 1))
                first = False
            em.act(ctx.r1[:, 0:512], ctx.ps_row[:, 0:512], AF.Copy)
            em.emit("vector", lambda e: e.reciprocal(ctx.r2[:, 0:512], ctx.r1[:, 0:512]),
                    [ctx.r1], [ctx.r2])
            row_bcast(ctx, ctx.rb, ctx.r2[:, 0:512], 512, ctx.ps_bc)
            out_attn2(h, i0, ps_o[:, 0:512], ctx.rb)

    sblocks = [lambda c: s2b[:, c * TQ: c * TQ + 512],
               lambda c: s2b[:, c * TQ + 512: c * TQ + 1024]]

    def ep_s2(blk, mc, ps, oc):
        em.act(qbuf[:, mc * TQ + blk * 512: mc * TQ + blk * 512 + 512], ps, AF.Identity,
               bias=boc[:, mc:mc + 1])
    matmul_fm(ctx, None, 0, sblocks, 512, wl, wo_d.ap(), D, 0, D, epilog=ep_s2, nblocks=2)

    TM2 = 128
    for tb in range(TQ // TM2):
        t0 = tb * TM2
        for c in range(NCH):
            em.copy(xn16[:, c * TM2:(c + 1) * TM2],
                    xn2[:, c * TH + halo + t0: c * TH + halo + t0 + TM2])
        for fc in range(64):
            ps = ctx.psum_pool[0]
            for k in range(NCH):
                w = wl16.get(w0_d.ap(), k * 128, fc * 128)
                em.mm(ps[:, 0:TM2], w, xn16[:, k * TM2:(k + 1) * TM2],
                      start=(k == 0), stop=(k == NCH - 1))
            em.act(d16[:, fc * TM2:(fc + 1) * TM2], ps[:, 0:TM2], AF.Gelu,
                   bias=b0c[:, fc:fc + 1])
        for dmc in range(NCH):
            ps = ctx.psum_pool[1]
            for fc in range(64):
                w = wl16.get(w1_d.ap(), fc * 128, dmc * 128)
                em.mm(ps[:, 0:TM2], w, d16[:, fc * TM2:(fc + 1) * TM2],
                      start=(fc == 0), stop=(fc == 63))
            em.act(e16[:, 0:TM2], ps[:, 0:TM2], AF.Identity, bias=b1c[:, dmc:dmc + 1])
            em.act(g2t[:, 0:TM2], e16[:, 0:TM2], AF.Silu)
            em.tt(ctx.tmp_f[:, 0:TM2], g2t[:, 0:TM2],
                  qbuf[:, dmc * TQ + t0: dmc * TQ + t0 + TM2], ALU.mult)
            em.dma("sync", ctx.sq_f[:, 0:TM2],
                   x2_scr.ap()[dmc * 128:(dmc + 1) * 128, halo + t0: halo + t0 + TM2])
            em.tt(ctx.tmp_f[:, 0:TM2], ctx.tmp_f[:, 0:TM2], ctx.sq_f[:, 0:TM2], ALU.add)
            em.dma("sync", o_x3.ap()[dmc * 128:(dmc + 1) * 128, t0:t0 + TM2],
                   ctx.tmp_f[:, 0:TM2])

    em.finish()
    return em.build(), em



def build_launch5(prefix_blocks):
    nc = bass.Bass("TRN2", target_bir_lowering=False)
    em = Em(nc)
    TQ = 1024
    TBB = 512
    TEND = prefix_blocks * TBB
    OWN0 = TEND - TQ
    NH = 32
    Q = 128

    def din(name, shape, dtype=F32):
        return em.dram(name, shape, dtype, kind="ExternalInput")

    x3_d = din("x3f", [D, T])
    win_d = din("winT", [D, 4256])
    wcv_d = din("convw", [2176, 4])
    cvb_d = din("convb", [2176, 1])
    wout_d = din("woutT", [D, D])
    alog_d = din("alog32", [32, 1])
    dtb_d = din("dtb32", [32, 1])
    dexp_d = din("dexp", [D, 1])
    wg_d = din("wg8", [D, 8])
    w1_d = din("w1Ts", [8 * D, 1024])
    w2_d = din("w2Ts", [8 * 1024, D])
    o_x4 = em.dram("o_x4", [D, TQ], F32, kind="ExternalOutput")
    lam_scr = em.dram("lam_scr", [32, TBB])
    eend_scr = em.dram("eend_scr", [32, 4])
    mrow_scr = em.dram("mrow_scr", [8, TBB], BF16)

    ctx = make_ctx(em, tb=512)
    wl = WLoader(ctx, "wl", depth=4, cols=512)
    ident32 = em.sbuf("ident32", [128, 128], F32)
    em.memset(ident32[:, :], 1.0)
    em.affsel(ident32[:, :], ident32[:, :], [[-1, 128]], ALU.is_ge, 0.0, 0, 1)
    em.affsel(ident32[:, :], ident32[:, :], [[1, 128]], ALU.is_ge, 0.0, 0, -1)
    identd = em.sbuf("identd", [128, 64], BF16)
    em.memset(identd[:, :], 1.0)
    em.affsel(identd[:, :], identd[:, :], [[-1, 64]], ALU.is_ge, 0.0, 0, 1)
    em.affsel(identd[:, :], identd[:, :], [[1, 64]], ALU.is_ge, 0.0, 0, -1)
    em.dma("sync", identd[64:128, 0:64], identd[0:64, 0:64])
    tri = em.sbuf("tri", [128, 128], BF16)
    em.memset(tri[:, :], 1.0)
    em.affsel(tri[:, :], tri[:, :], [[1, 128]], ALU.is_ge, 0.0, 0, -1)

    xnb = em.sbuf("xnb", [128, NCH * TBB], BF16)
    xbcb = em.sbuf("xbcb", [128, 17 * 515], BF16)
    xcv = em.sbuf("xcv", [128, 17 * TBB], BF16)
    zb = em.sbuf("zb", [128, NCH * TBB], BF16)
    yfm = em.sbuf("yfm", [128, NCH * TBB], BF16)
    out_acc = em.sbuf("out_acc", [128, NCH * 256], F32)
    he = em.sbuf("he", [128, 8 * 256], BF16)
    S = em.sbuf("Sstate", [64, NH * 64], BF16)
    dts = em.sbuf("dts", [32, TBB], F32)
    lam = em.sbuf("lam", [32, TBB], F32)
    alam = em.sbuf("alam", [32, TBB], F32)
    edec = em.sbuf("edec", [32, TBB], F32)
    ww = em.sbuf("ww", [32, TBB], F32)
    eendr = em.sbuf("eendr", [32, 4], F32)
    z32 = em.sbuf("z32", [32, TBB], F32)
    acol32 = em.sbuf("acol32", [32, 1], F32)
    dtbc = em.sbuf("dtbc", [32, 1], F32)
    abc = em.sbuf("abc", [128, 32], F32)
    arow = em.sbuf("arow", [1, 32], F32)
    alamT = em.sbuf("alamT", [128, 128], F32)
    nalamT = em.sbuf("nalamT", [128, 128], F32)
    wwT = em.sbuf("wwT", [128, 128], F32)
    dtT = em.sbuf("dtT", [128, 128], F32)
    elamT = em.sbuf("elamT", [128, 128], F32)
    btok = em.sbuf("btok", [128, 64 * 4], BF16)
    Bsb = em.sbuf("Bsb", [64, TBB], BF16)
    Csb = em.sbuf("Csb", [64, TBB], BF16)
    alr = em.sbuf("alr", [1, 4096], F32)
    eer = em.sbuf("eer", [1, 128], F32)
    mrr = em.sbuf("mrr", [1, 8 * TBB], BF16)
    ones_row_bf = em.sbuf("ones_row_bf", [1, 128], BF16)
    em.memset(ones_row_bf[:, :], 1.0)
    mh_t = em.sbuf("mh_t", [128, 128], BF16)
    dtile = em.sbuf("dtile", [128, 128], BF16)
    ut_t = em.sbuf("ut_t", [128, 64], BF16)
    us_t = em.sbuf("us_t", [128, 64], BF16)
    yint = em.sbuf("yint", [128, 64], F32)
    ytok = em.sbuf("ytok", [128, 64], BF16)
    contrib = em.sbuf("contrib", [64, 64], BF16)
    deccol = em.sbuf("deccol", [64, 1], F32)
    cvwc = em.sbuf("cvwc", [128, 17 * 4], F32)
    cvbc = em.sbuf("cvbc", [128, 17], F32)
    dexpc = em.sbuf("dexpc", [128, 16], F32)
    g8 = em.sbuf("g8", [128, 8], F32)
    cmp8 = em.sbuf("cmp8", [128, 8], F32)
    cnt8 = em.sbuf("cnt8", [128, 1], F32)
    m8 = em.sbuf("m8", [128, 8], F32)
    mbc = em.sbuf("mbc", [128, 256], BF16)
    mrow5 = em.sbuf("mrow5", [1, TBB], F32)
    rrow5 = em.sbuf("rrow5", [1, TBB], F32)
    ps_t = em.psum("ps_t5", [128, 128], BF16)

    load_col(em, "sync", cvbc, cvb_d.ap(), 17)
    load_col(em, "sync", dexpc, dexp_d.ap(), 16)
    for ch in range(17):
        em.dma("sync", cvwc[:, ch * 4:(ch + 1) * 4], wcv_d.ap()[ch * 128:(ch + 1) * 128, :])
    em.dma("sync", dtbc[:, :], dtb_d.ap())
    em.dma("sync", ctx.tmp_f[0:32, 0:1], alog_d.ap())
    em.act(acol32[0:32, 0:1], ctx.tmp_f[0:32, 0:1], AF.Exp)
    em.ts(acol32[0:32, 0:1], acol32[0:32, 0:1], -1.0, ALU.mult)
    em.trans(ctx.ps_bc[0:1, 0:32], acol32[0:32, 0:1], ident32[0:32, 0:32])
    em.act(arow[:, :], ctx.ps_bc[0:1, 0:32], AF.Copy)
    em.mm(ctx.ps_bc[:, 0:32], ctx.ones_row_f[:, :], arow[:, :])
    em.act(abc[:, :], ctx.ps_bc[:, 0:32], AF.Copy)
    em.memset(S[:, :], 0.0)
    em.memset(z32[:, :], 0.0)
    em.memset(xbcb[:, :], 0.0)

    XCH = 16

    def xbcb_sl(ch, a, b):
        return xbcb[:, ch * 515 + a: ch * 515 + b]

    for bk in range(prefix_blocks):
        t0 = bk * TBB
        own = t0 >= OWN0
        ot0 = t0 - OWN0
        for c in range(NCH):
            em.dma("sync", ctx.sq_f[:, 0:TBB], x3_d.ap()[c * 128:(c + 1) * 128, t0:t0 + TBB])
            em.mm(ctx.ps_row[:, 0:TBB], ctx.ones_col32[:, :], ctx.sq_f[:, 0:TBB],
                  start=(c == 0), stop=(c == NCH - 1))
            em.act(ctx.tmp_f[:, 0:TBB], ctx.sq_f[:, 0:TBB], AF.Square)
            em.mm(ctx.ps_row2[:, 0:TBB], ctx.ones_col32[:, :], ctx.tmp_f[:, 0:TBB],
                  start=(c == 0), stop=(c == NCH - 1))
        em.act(mrow5[:, 0:TBB], ctx.ps_row[:, 0:TBB], AF.Copy, scale=1.0 / D)
        em.act(ctx.r1[:, 0:TBB], mrow5[:, 0:TBB], AF.Square)
        em.act(ctx.r2[:, 0:TBB], ctx.ps_row2[:, 0:TBB], AF.Copy, scale=1.0 / D)
        em.tt(ctx.r2[:, 0:TBB], ctx.r2[:, 0:TBB], ctx.r1[:, 0:TBB], ALU.subtract)
        em.act(ctx.r1[:, 0:TBB], ctx.r2[:, 0:TBB], AF.Sqrt, bias=1e-5)
        em.emit("vector", lambda e: e.reciprocal(rrow5[:, 0:TBB], ctx.r1[:, 0:TBB]),
                [ctx.r1], [rrow5])
        row_bcast(ctx, ctx.rb, rrow5[:, 0:TBB], TBB, ctx.ps_bc)
        em.tt(ctx.r1[:, 0:TBB], mrow5[:, 0:TBB], rrow5[:, 0:TBB], ALU.mult)
        row_bcast(ctx, ctx.mb, ctx.r1[:, 0:TBB], TBB, ctx.ps_bc)
        for c in range(NCH):
            em.dma("sync", ctx.sq_f[:, 0:TBB], x3_d.ap()[c * 128:(c + 1) * 128, t0:t0 + TBB])
            em.tt(ctx.tmp_f[:, 0:TBB], ctx.sq_f[:, 0:TBB], ctx.rb[:, 0:TBB], ALU.mult)
            em.tt(chunk(xnb, c, TBB), ctx.tmp_f[:, 0:TBB], ctx.mb[:, 0:TBB], ALU.subtract)
        for oc in range(17):
            ps = ctx.psum_pool[0]
            for k in range(NCH):
                w = wl.get(win_d.ap(), k * 128, 2048 + oc * 128)
                em.mm(ps[:, 0:TBB], w, chunk(xnb, k, TBB), start=(k == 0),
                      stop=(k == NCH - 1))
            em.copy(xbcb_sl(oc, 0, 3), xbcb_sl(oc, 512, 515), eng="gpsimd")
            em.act(xbcb_sl(oc, 3, 515), ps[:, 0:TBB], AF.Copy)
        ps = ctx.psum_pool[0]
        for k in range(NCH):
            w = wl.get(win_d.ap(), k * 128, 4224, cols=32)
            em.mm(ps[0:32, 0:TBB], w[0:128, 0:32], chunk(xnb, k, TBB), start=(k == 0),
                  stop=(k == NCH - 1))
        em.act(ctx.tmp_f[0:32, 0:TBB], ps[0:32, 0:TBB], AF.Identity, bias=dtbc[0:32, 0:1])
        em.act(ctx.sq_f[0:32, 0:TBB], ctx.tmp_f[0:32, 0:TBB], AF.Abs)
        em.act(ctx.sq_f[0:32, 0:TBB], ctx.sq_f[0:32, 0:TBB], AF.Exp, scale=-1.0)
        em.act(ctx.sq_f[0:32, 0:TBB], ctx.sq_f[0:32, 0:TBB], AF.Ln, bias=1.0)
        em.ts(ctx.tmp_f[0:32, 0:TBB], ctx.tmp_f[0:32, 0:TBB], 0.0, ALU.max)
        em.tt(dts[0:32, 0:TBB], ctx.tmp_f[0:32, 0:TBB], ctx.sq_f[0:32, 0:TBB], ALU.add)
        if own:
            for oc in range(16):
                ps = ctx.psum_pool[1]
                for k in range(NCH):
                    w = wl.get(win_d.ap(), k * 128, oc * 128)
                    em.mm(ps[:, 0:TBB], w, chunk(xnb, k, TBB), start=(k == 0),
                          stop=(k == NCH - 1))
                em.act(chunk(zb, oc, TBB), ps[:, 0:TBB], AF.Copy)
        for ch in range(17):
            em.ts(ctx.tmp_f[:, 0:TBB], xbcb_sl(ch, 0, 512), cvwc[:, ch * 4:ch * 4 + 1],
                  ALU.mult)
            for kk in range(1, 4):
                em.stt(ctx.tmp_f[:, 0:TBB], xbcb_sl(ch, kk, 512 + kk),
                       cvwc[:, ch * 4 + kk:ch * 4 + kk + 1], ctx.tmp_f[:, 0:TBB],
                       ALU.mult, ALU.add)
            em.act(xcv[:, ch * TBB:(ch + 1) * TBB], ctx.tmp_f[:, 0:TBB], AF.Silu,
                   bias=cvbc[:, ch:ch + 1])
        em.copy(Bsb[0:64, 0:TBB], xcv[0:64, 16 * TBB:17 * TBB])
        em.copy(Csb[0:64, 0:TBB], xcv[64:128, 16 * TBB:17 * TBB])
        for c in range(4):
            cs = slice(c * Q, (c + 1) * Q)
            em.scan(lam[0:32, cs], dts[0:32, cs], z32[0:32, 0:Q], 0.0,
                    op0=ALU.add, op1=ALU.add)
        em.ts(alam[0:32, 0:TBB], lam[0:32, 0:TBB], acol32[0:32, 0:1], ALU.mult)
        for c in range(4):
            cs = slice(c * Q, (c + 1) * Q)
            em.act(edec[0:32, cs], alam[0:32, cs], AF.Exp, scale=-1.0,
                   bias=alam[0:32, c * Q + Q - 1:c * Q + Q])
        em.tt(ww[0:32, 0:TBB], dts[0:32, 0:TBB], edec[0:32, 0:TBB], ALU.mult)
        for c in range(4):
            em.act(eendr[0:32, c:c + 1], alam[0:32, c * Q + Q - 1:c * Q + Q], AF.Exp)
        em.dma("sync", lam_scr.ap()[:, 0:TBB], alam[0:32, 0:TBB])
        em.dma("sync", eend_scr.ap()[:, :], eendr[0:32, 0:4])

        em.dma("sync", eer[:, 0:128], eend_scr.ap().rearrange("h c -> (h c)").rearrange("(o f) -> o f", o=1))
        for c in range(4):
            cs = slice(c * Q, (c + 1) * Q)
            em.trans(ctx.ps_bc[:, 0:32], alam[0:32, cs], ident32[0:32, 0:32])
            em.act(alamT[:, c * 32:(c + 1) * 32], ctx.ps_bc[:, 0:32], AF.Copy)
            em.trans(ctx.ps_bc[:, 0:32], ww[0:32, cs], ident32[0:32, 0:32])
            em.act(wwT[:, c * 32:(c + 1) * 32], ctx.ps_bc[:, 0:32], AF.Copy)
            em.trans(ctx.ps_bc[:, 0:32], dts[0:32, cs], ident32[0:32, 0:32])
            em.act(dtT[:, c * 32:(c + 1) * 32], ctx.ps_bc[:, 0:32], AF.Copy)
        em.ts(nalamT[:, 0:128], alamT[:, 0:128], -1.0, ALU.mult)
        em.act(elamT[:, 0:128], alamT[:, 0:128], AF.Exp)
        for c in range(4):
            em.trans(ps_t[:, 0:64], Bsb[0:64, c * Q:(c + 1) * Q], ctx.ident[0:64, 0:64])
            em.act(btok[:, c * 64:(c + 1) * 64], ps_t[:, 0:64], AF.Copy)
        for c in range(4):
            cs = slice(c * Q, (c + 1) * Q)
            em.dma("sync", alr[:, 0:4096].rearrange("o (h t) -> o h t", h=32),
                   lam_scr.ap()[:, c * Q:(c + 1) * Q].rearrange("h t -> (h) t")
                   .rearrange("h t -> h () t").transpose([1, 0, 2]))
            ps_cb = ctx.psum_pool[0]
            em.mm(ps_cb[:, 0:Q], Bsb[0:64, cs], Csb[0:64, cs])
            em.tt(mh_t[:, 0:Q], ps_cb[:, 0:Q], tri[:, :], ALU.mult)
            for h in range(NH):
                hc = h * 64
                xch = h // 2
                xrw = (h % 2) * 64
                em.mm(ctx.ps_bc[:, 0:Q], ctx.ones_row_f[:, :],
                      alr[:, h * Q:(h + 1) * Q])
                em.act(ctx.rb[:, 0:Q], ctx.ps_bc[:, 0:Q], AF.Identity,
                       bias=nalamT[:, c * 32 + h:c * 32 + h + 1])
                em.ts(ctx.rb[:, 0:Q], ctx.rb[:, 0:Q], 0.0, ALU.min)
                em.act(dtile[:, 0:Q], ctx.rb[:, 0:Q], AF.Exp)
                em.tt(dtile[:, 0:Q], dtile[:, 0:Q], mh_t[:, 0:Q], ALU.mult)
                em.trans(ps_t[:, 0:64], xcv[xrw:xrw + 64, xch * TBB + c * Q:
                                            xch * TBB + (c + 1) * Q],
                         identd[xrw:xrw + 64, 0:64])
                em.act(ut_t[:, 0:64], ps_t[:, 0:64], AF.Copy,
                       scale=dtT[:, c * 32 + h:c * 32 + h + 1])
                em.act(us_t[:, 0:64], ps_t[:, 0:64], AF.Copy,
                       scale=wwT[:, c * 32 + h:c * 32 + h + 1])
                ps_y = ctx.psum_pool[1]
                ps_y2 = ctx.psum_pool[2]
                em.mm(ps_y[:, 0:64], dtile[:, 0:Q], ut_t[:, 0:64])
                em.mm(ps_y2[:, 0:64], Csb[0:64, cs], S[:, hc:hc + 64])
                em.act(yint[:, 0:64], ps_y2[:, 0:64], AF.Copy,
                       scale=elamT[:, c * 32 + h:c * 32 + h + 1])
                em.tt(ytok[:, 0:64], yint[:, 0:64], ps_y[:, 0:64], ALU.add)
                em.trans(ps_t[0:64, 0:128], ytok[:, 0:64], ctx.ident[:, :])
                em.act(yfm[xrw:xrw + 64, xch * TBB + c * Q: xch * TBB + (c + 1) * Q],
                       ps_t[0:64, 0:128], AF.Copy)
                ps_s = ctx.psum_pool[3]
                em.mm(ps_s[0:64, 0:64], btok[:, c * 64:(c + 1) * 64], us_t[:, 0:64])
                em.act(contrib[0:64, 0:64], ps_s[0:64, 0:64], AF.Copy)
                em.mm(ctx.ps_bc[0:64, 0:1], ctx.ones_row_f[:, 0:64],
                      eer[:, h * 4 + c:h * 4 + c + 1])
                em.act(deccol[0:64, 0:1], ctx.ps_bc[0:64, 0:1], AF.Copy)
                em.stt(S[:, hc:hc + 64], S[:, hc:hc + 64], deccol[0:64, 0:1],
                       contrib[0:64, 0:64], ALU.mult, ALU.add)
        if own:
            for oc in range(16):
                em.ts(ctx.tmp_f[:, 0:TBB], xcv[:, oc * TBB:(oc + 1) * TBB],
                      dexpc[:, oc:oc + 1], ALU.mult)
                em.tt(ctx.tmp_f[:, 0:TBB], ctx.tmp_f[:, 0:TBB], chunk(yfm, oc, TBB),
                      ALU.add)
                em.dma("sync", zb[:, 0:TBB], io["z_scr"][oc * 128:(oc + 1) * 128, :])
                em.act(ctx.sq_f[:, 0:TBB], zb[:, 0:TBB], AF.Silu)
                em.tt(chunk(yfm, oc, TBB), ctx.tmp_f[:, 0:TBB], ctx.sq_f[:, 0:TBB],
                      ALU.mult)
            for tc in range(4):
                ps = ctx.psum_pool[0]
                for k in range(NCH):
                    w = wl.get(wg_d.ap(), k * 128, 0, cols=8)
                    em.mm(ps[0:128, 0:8], xnb[:, k * TBB + tc * 128: k * TBB + (tc + 1) * 128],
                          w[0:128, 0:8], start=(k == 0), stop=(k == NCH - 1))
                em.act(g8[:, 0:8], ps[0:128, 0:8], AF.Copy)
                for e in range(8):
                    em.ts(cmp8[:, 0:8], g8[:, 0:8], g8[:, e:e + 1], ALU.is_gt)
                    em.red(cnt8[:, 0:1], cmp8[:, 0:8], ALU.add)
                    em.ts(m8[:, e:e + 1], cnt8[:, 0:1], 3.5, ALU.is_le)
                em.trans(ctx.ps_bc[0:8, 0:128], m8[:, 0:8], ident32[:, :])
                em.act(mbc[0:8, 0:128], ctx.ps_bc[0:8, 0:128], AF.Copy)
                em.dma("sync", mrow_scr.ap()[:, tc * 128:(tc + 1) * 128],
                       mbc[0:8, 0:128])
            em.dma("sync", mrr[:, 0:8 * TBB], mrow_scr.ap().rearrange("e t -> (e t)").rearrange("(o f) -> o f", o=1))
            for th in range(2):
                tofs = th * 256
                for dmc in range(16):
                    em.memset(out_acc[:, dmc * 256:(dmc + 1) * 256], 0.0)
                for e in range(8):
                    em.mm(ctx.ps_bc[:, 0:256], ones_row_bf[:, :],
                          mrr[:, e * TBB + tofs: e * TBB + tofs + 256])
                    em.act(mbc[:, 0:256], ctx.ps_bc[:, 0:256], AF.Copy)
                    for hc in range(8):
                        ps = ctx.psum_pool[1]
                        for k in range(NCH):
                            w = wl.get(w1_d.ap(), e * 2048 + k * 128, hc * 128)
                            em.mm(ps[:, 0:256], w,
                                  xnb[:, k * TBB + tofs: k * TBB + tofs + 256],
                                  start=(k == 0), stop=(k == NCH - 1))
                        em.act(he[:, hc * 256: hc * 256 + 256], ps[:, 0:256], AF.Relu)
                        em.tt(he[:, hc * 256: hc * 256 + 256], he[:, hc * 256: hc * 256 + 256],
                              mbc[:, 0:256], ALU.mult)
                    for dmc in range(16):
                        ps = ctx.psum_pool[2]
                        for hc in range(8):
                            w = wl.get(w2_d.ap(), e * 1024 + hc * 128, dmc * 128)
                            em.mm(ps[:, 0:256], w, he[:, hc * 256: hc * 256 + 256],
                                  start=(hc == 0), stop=(hc == 7))
                        em.emit("vector", lambda e2, dmc=dmc: e2.tensor_tensor(
                            out_acc[:, dmc * 256:(dmc + 1) * 256],
                            out_acc[:, dmc * 256:(dmc + 1) * 256],
                            ctx.psum_pool[2][:, 0:256], ALU.add),
                            [out_acc, ctx.psum_pool[2]], [out_acc])
                for dmc in range(16):
                    ps = ctx.psum_pool[3]
                    for k in range(NCH):
                        w = wl.get(wout_d.ap(), k * 128, dmc * 128)
                        em.mm(ps[:, 0:256], w, chunk(yfm, k, TBB)[:, tofs:tofs + 256],
                              start=(k == 0), stop=(k == NCH - 1))
                    em.act(ctx.sq_f[:, 0:256], out_acc[:, dmc * 256:(dmc + 1) * 256],
                           AF.Silu)
                    em.emit("vector", lambda e2: e2.tensor_tensor(
                        ctx.tmp_f[:, 0:256], ctx.sq_f[:, 0:256],
                        ctx.psum_pool[3][:, 0:256], ALU.mult),
                        [ctx.sq_f, ctx.psum_pool[3]], [ctx.tmp_f])
                    em.dma("sync", ctx.sq_f[:, 0:256],
                           x3_d.ap()[dmc * 128:(dmc + 1) * 128,
                                     t0 + tofs: t0 + tofs + 256])
                    em.tt(ctx.tmp_f[:, 0:256], ctx.tmp_f[:, 0:256], ctx.sq_f[:, 0:256],
                          ALU.add)
                    em.dma("sync", o_x4.ap()[dmc * 128:(dmc + 1) * 128,
                                             ot0 + tofs: ot0 + tofs + 256],
                           ctx.tmp_f[:, 0:256])

    em.finish()
    return em.build(), em



_PROG_CACHE = {}


def _get_prog(key, builder, *args):
    if key not in _PROG_CACHE:
        _PROG_CACHE[key] = builder(*args)
    return _PROG_CACHE[key]


def _perm_rows_l1(lo, n):
    out = []
    for b0 in range(lo, lo + n, 256):
        out.extend(range(b0, b0 + 256, 2))
        out.extend(range(b0 + 1, b0 + 256, 2))
    return np.array(out)


def _rope_perm16():
    p = []
    for h in range(16):
        base = h * 128
        p.extend(range(base, base + 128, 2))
        p.extend(range(base + 1, base + 128, 2))
    return np.array(p)


def kernel(**inputs):
    return kernel_fused(**inputs)


def kernel_multilaunch(**inputs):
    d = {k: np.asarray(v) for k, v in inputs.items()}
    x = d["L0_attn_x"].astype(np.float32)
    C = np.ascontiguousarray

    def bf(a):
        return C(np.asarray(a).astype(BF))

    nc0, _ = _get_prog("l0", build_launch0)
    wqa = C(d["L0_attn_Wq"])
    wkv = C(d["L0_attn_Wkv"])
    gn = _col(d["L0_attn_g_norm"])
    l1 = _col(d["L0_ffn_ln1_w"])
    wq0 = C(d["L0_ffn_Wq"].T)
    wk0 = C(d["L0_ffn_Wk"].T)
    wv0 = C(d["L0_ffn_Wv"].T)
    xT = [C(x[b].T) for b in range(B)]
    ims = []
    for c in range(8):
        b, q = c // 4, c % 4
        ims.append(dict(xT=C(xT[b][:, q * 1024:(q + 1) * 1024]), wqa=wqa, wkv=wkv,
                        gnorm=gn, ln1=l1, wq0=wq0, wk0=wk0, wv0=wv0))
    r0 = run_bass_kernel_spmd(nc0, ims, core_ids=list(range(8))).results

    def gather(key, tm=False):
        out = []
        for b in range(B):
            parts = [np.asarray(r0[b * 4 + q][key]) for q in range(4)]
            out.append(np.concatenate(parts, axis=0 if tm else 1))
        return out

    xn_f = gather("o_xn")
    khat_f = gather("o_khat")
    va_f = gather("o_va", tm=True)
    k0_f = gather("o_k0")
    v0_f = gather("o_v0", tm=True)

    wout = C(d["L0_attn_Wout"])
    gout = _col(d["L0_attn_g_out"])
    wo0 = C(d["L0_ffn_Wo"].T)
    a_l = C(d["L0_ffn_A"])
    b_l = C(d["L0_ffn_B"])
    ln2 = _col(d["L0_ffn_ln2_w"])
    w_in = C(d["L0_ffn_W_in_base"].T)
    w_up = C(d["L0_ffn_W_up_base"].T)
    w_dn = C(d["L0_ffn_W_down"].T)
    x1 = [np.zeros((D, T), np.float32) for _ in range(B)]
    for pc in (8, 16, 24, 32):
        cores = [c for c in range(8) if (c % 4 + 1) * 8 == pc]
        nc1, _ = _get_prog(("l1", pc), build_launch1, pc)
        ims = []
        for c in cores:
            b, q = c // 4, c % 4
            sl = slice(q * 1024, (q + 1) * 1024)
            ims.append(dict(
                xn=C(xn_f[b][:, sl]), qhat=C(np.asarray(r0[c]["o_qhat"])),
                khat=C(khat_f[b]), va=C(va_f[b]),
                q0=C(np.asarray(r0[c]["o_q0"])), k0=C(k0_f[b]), v0=C(v0_f[b]),
                x=C(xT[b][:, sl]), wout=wout, gout=gout, wo0=wo0, a_lora=a_l,
                b_lora=b_l, ln2=ln2, w_in=w_in, w_up=w_up, w_dn=w_dn))
        rr = run_bass_kernel_spmd(nc1, ims, core_ids=cores).results
        for i, c in enumerate(cores):
            b, q = c // 4, c % 4
            x1[b][:, q * 1024:(q + 1) * 1024] = np.asarray(rr[i]["o_x1"])

    wxT = C(d["L1_attn_Wx"].T)
    wdtT = C(d["L1_attn_Wdt"].T)
    bdt = _col(d["L1_attn_bdt"])
    alog = C(d["L1_attn_A_log"])
    dcol = _col(d["L1_attn_D"])
    s1 = [np.zeros((D, T), np.float32) for _ in range(B)]
    xn1 = [np.zeros((D, T), BF) for _ in range(B)]
    for pb in (2, 4, 6, 8):
        cores = [c for c in range(8) if (c % 4 + 1) * 2 == pb]
        nc2, _ = _get_prog(("l2", pb), build_launch2, pb)
        ims = [dict(x1f=C(x1[c // 4]), wxT=wxT, wdtT=wdtT, bdt=bdt, alog=alog,
                    dcol=dcol) for c in cores]
        rr = run_bass_kernel_spmd(nc2, ims, core_ids=cores).results
        for i, c in enumerate(cores):
            b, q = c // 4, c % 4
            sl = slice(q * 1024, (q + 1) * 1024)
            s1[b][:, sl] = np.asarray(rr[i]["o_s1"])
            xn1[b][:, sl] = np.asarray(rr[i]["o_xn1"])

    xn1f = C(np.concatenate([xn1[0], xn1[1]], axis=1))
    gp, gs = d["L1_ffn_gate_W_packed"], d["L1_ffn_gate_scales"]
    up, us = d["L1_ffn_up_W_packed"], d["L1_ffn_up_scales"]
    dn, ds = d["L1_ffn_down_W_packed"], d["L1_ffn_down_scales"]
    dnT = C(dn.T)
    dsT = C(ds.T)
    nc3, _ = _get_prog("l3", build_launch3)
    ims = []
    for c in range(8):
        pr = _perm_rows_l1(c * 2048, 2048)
        ims.append(dict(xn1f=xn1f, gpT=C(gp[pr].T), gsT=C(gs[pr].T), upT=C(up[pr].T),
                        usT=C(us[pr].T), dnT=C(dnT[c * 1024:(c + 1) * 1024]),
                        dsT=C(dsT[c * 64:(c + 1) * 64])))
    r3 = run_bass_kernel_spmd(nc3, ims, core_ids=list(range(8))).results
    partsb = [np.asarray(r3[c]["o_part"]) for c in range(8)]

    pq = _rope_perm16()
    pk = _rope_perm16()[:512]
    pk128 = np.concatenate([np.arange(0, 128, 2), np.arange(1, 128, 2)])
    WqT = C(d["L2_attn_Wq"].T[:, pq])
    bq = _col(d["L2_attn_bq"][pq])
    WkT = C(d["L2_attn_Wk"].T[:, pk])
    bk = _col(d["L2_attn_bk"][pk])
    WvT = C(d["L2_attn_Wv"].T)
    bv = _col(d["L2_attn_bv"])
    WoT = C(d["L2_attn_Wo"].T)
    bo = _col(d["L2_attn_bo"])
    qn = _col(d["L2_attn_qn_w"][pk128])
    kn = _col(d["L2_attn_kn_w"][pk128])
    scale = np.asarray(d["L2_attn_scale"], np.float32).reshape(1, 1)
    inv = 1.0 / 10000.0 ** (np.arange(0, 128, 2).astype(np.float64) / 128)
    W0T = C(d["L2_ffn_W0"].T)
    b0 = _col(d["L2_ffn_bias0"])
    W1T = C(d["L2_ffn_W1"].T)
    b1 = _col(d["L2_ffn_bias1"])
    x3 = [np.zeros((D, T), np.float32) for _ in range(B)]
    for halo in (0, 128):
        cores = [c for c in range(8) if (c % 4 == 0) == (halo == 0)]
        nc4, _ = _get_prog(("l4", halo), build_launch4, halo)
        ims = []
        for c in cores:
            b, q = c // 4, c % 4
            lo = q * 1024 - halo
            sl = slice(lo, (q + 1) * 1024)
            pos = np.arange(lo, (q + 1) * 1024).astype(np.float64)
            th = pos[None, :] * inv[:, None]
            thS = ((th + np.pi) % (2 * np.pi) - np.pi).astype(np.float32)
            thC = ((th + np.pi / 2 + np.pi) % (2 * np.pi) - np.pi).astype(np.float32)
            im = dict(x1h=C(x1[b][:, sl]), s1h=C(s1[b][:, sl]),
                      wqT=WqT, bq=bq, wkT=WkT, bk=bk, wvT=WvT, bv=bv, woT=WoT, bo=bo,
                      qnw=qn, knw=kn, att_scale=scale, thS=C(thS), thC=C(thC),
                      w0T=W0T, b0=b0, w1T=W1T, b1=b1)
            for i in range(8):
                im[f"p{i}"] = C(partsb[i][:, b * T:][:, :T][:, sl])
            ims.append(im)
        rr = run_bass_kernel_spmd(nc4, ims, core_ids=cores).results
        for i, c in enumerate(cores):
            b, q = c // 4, c % 4
            x3[b][:, q * 1024:(q + 1) * 1024] = np.asarray(rr[i]["o_x3"])

    w1 = d["L3_ffn_w1"]
    w2 = d["L3_ffn_w2"]
    w1T = C(np.concatenate([w1[e].T for e in range(8)], axis=0))
    w2T = C(np.concatenate([w2[e].T for e in range(8)], axis=0))
    base5 = dict(
        winT=C(d["L3_attn_W_in"].T), convw=C(d["L3_attn_conv_w"][:, 0, :]),
        convb=_col(d["L3_attn_conv_b"]), woutT=C(d["L3_attn_W_out"].T),
        alog32=_col(d["L3_attn_A_log"]), dtb32=_col(d["L3_attn_dt_bias"]),
        dexp=_col(np.repeat(np.asarray(d["L3_attn_D_param"]), 64)),
        wg8=C(d["L3_ffn_wg"]), w1Ts=w1T, w2Ts=w2T)
    x4 = np.zeros((B, T, D), np.float32)
    for pb in (2, 4, 6, 8):
        cores = [c for c in range(8) if (c % 4 + 1) * 2 == pb]
        nc5, _ = _get_prog(("l5", pb), build_launch5, pb)
        ims = []
        for c in cores:
            im = dict(base5)
            im["x3f"] = C(x3[c // 4])
            ims.append(im)
        rr = run_bass_kernel_spmd(nc5, ims, core_ids=cores).results
        for i, c in enumerate(cores):
            b, q = c // 4, c % 4
            x4[b, q * 1024:(q + 1) * 1024] = np.asarray(rr[i]["o_x4"]).T
    return x4



class Pair:
    NF = 24

    def __init__(self, em, nonce_dram):
        self.em = em
        nc = em.nc
        self.flags = [em.dram(f"shflags{r}", [1, self.NF], F32, addr_space="Shared")
                      for r in range(2)]
        self.flagv = em.sbuf("pp_flagv", [1, self.NF], F32)
        self.exp_i = em.sbuf("pp_expi", [1, self.NF], I32)
        self.nonce_t = em.sbuf("pp_nonce", [1, 1], F32)
        self.poll_f = em.sbuf("pp_pollf", [1, 1], F32)
        self.poll_i = em.sbuf("pp_polli", [1, 1], I32)
        self.nonce_dram = nonce_dram
        self.regs = {}
        em.dma_sems.append("ds_poll")

    def setup(self):
        em = self.em
        em.dma("sync", self.nonce_t[:, :], self.nonce_dram.ap())
        em.iota(self.flagv[0:1, 0:self.NF], [[1, self.NF]], base=1)
        em.ts(self.flagv[0:1, :], self.flagv[0:1, :], self.nonce_t[0:1, 0:1],
              ALU.add, eng="vector")
        em.copy(self.exp_i[0:1, :], self.flagv[0:1, :], eng="vector")
        em.barrier_all()

    def _get_regs(self, eng, key):
        if key not in self.regs:
            rc = eng.alloc_register(f"ppc_{key}")
            rg = eng.alloc_register(f"ppg_{key}")
            re = eng.alloc_register(f"ppe_{key}")
            eng.reg_mov(rc, 0)
            self.regs[key] = (rc, rg, re)
        return self.regs[key]

    def send(self, fidx, role, shared_bufs):
        em = self.em
        for bh in shared_bufs:
            b = em.bufs[bh.name]
            if b.dsem is not None and b.dcnt:
                em._wait("gpsimd", b.dsem, b.dcnt)
        em.dma("gpsimd", self.flags[role].ap()[0:1, fidx:fidx + 1],
               self.flagv[0:1, fidx:fidx + 1])

    def wait(self, fidx, role, body_idx):
        em = self.em
        src = self.flags[1 - role]
        pf, pi, ei = self.poll_f, self.poll_i, self.exp_i

        def fn(eng):
            rc, rg, re = self._get_regs(eng, body_idx)
            eng.reg_load(re, ei[0:1, fidx:fidx + 1])
            eng.reg_mov(rg, 1)
            with eng.While(rg):
                eng.dma_start(out=pf[0:1, 0:1], in_=src.ap()[0:1, fidx:fidx + 1])\
                    .then_inc(em.sems["ds_poll"], 16)
                eng.reg_add(rc, rc, 16)
                eng.wait_ge(em.sems["ds_poll"], rc)
                eng.tensor_copy(pi[0:1, 0:1], pf[0:1, 0:1])
                eng.reg_load(rg, pi[0:1, 0:1])
                eng.reg_alu(rg, rg, re, ALU.subtract)
        em.raw("gpsimd", fn)

F_KV, F_XN1, F_H1, F_PART, F_X2E, F_CVT, F_S3 = 1, 2, 3, 4, 5, 6, 7


def emit_p0(em, ctx, io, sub, uid):
    TQ = 1024
    TBB = 512
    with em.scope():
        sb = em.sbuf_scoped
        wl = WLoader(ctx, f"wl0_{uid}", depth=4, cols=512, scoped=True)
        x_f = sb(f"x_f{uid}", [128, NCH * TBB], F32)
        xn = sb(f"xn{uid}", [128, NCH * TQ], BF16)
        xn2 = sb(f"xn2{uid}", [128, NCH * TQ], BF16)
        xp = sb(f"xprime{uid}", [128, NCH * TQ], BF16)
        gcol = sb(f"gcol{uid}", [128, 16], F32)
        l1col = sb(f"l1col{uid}", [128, 16], F32)
        mrow = sb(f"mrow{uid}", [1, TBB], F32)
        rrow = sb(f"rrow{uid}", [1, TBB], F32)
        out_t = [sb(f"out_t{uid}_{i}", [128, TBB], BF16) for i in range(4)]
        v_fm = sb(f"v_fm{uid}", [128, TQ], BF16)
        otc = [0]

        def otile():
            t = out_t[otc[0] % 4]
            otc[0] += 1
            return t

        load_col(em, "sync", gcol, io["gnorm"], 16)
        load_col(em, "sync", l1col, io["ln1"], 16)

        def xn_c(c, t0, tbb=TBB):
            return xn[:, c * TQ + t0: c * TQ + t0 + tbb]

        for blk in range(2):
            t0 = blk * TBB
            load_fm(em, "sync", x_f, io["xT"], D, sub * TQ + t0, sub * TQ + t0 + TBB, TBB)
            emit_ln_rows(ctx, x_f, TBB, ctx.ps_row, ctx.ps_row2, mrow[:, 0:TBB],
                         rrow[:, 0:TBB], 1e-5, sq_scratch=ctx.sq_f, ones=ctx.ones_col32)
            row_bcast(ctx, ctx.rb, rrow[:, 0:TBB], TBB, ctx.ps_bc)
            em.tt(ctx.r1[:, 0:TBB], mrow[:, 0:TBB], rrow[:, 0:TBB], ALU.mult)
            row_bcast(ctx, ctx.mb, ctx.r1[:, 0:TBB], TBB, ctx.ps_bc)
            for c in range(NCH):
                em.tt(ctx.tmp_f[:, 0:TBB], chunk(x_f, c, TBB), ctx.rb[:, 0:TBB], ALU.mult)
                em.tt(xn_c(c, t0), ctx.tmp_f[:, 0:TBB], ctx.mb[:, 0:TBB], ALU.subtract)
        store_fm(em, "sync", io["xn_scr"], xn, D, sub * TQ, sub * TQ + TQ, TQ)

        for blk in range(2):
            t0 = blk * TBB
            for c in range(NCH):
                xc = xn_c(c, t0)
                em.mm(ctx.ps_row[:, 0:TBB], ctx.ones_col_f[:, :], xc, start=(c == 0),
                      stop=(c == NCH - 1))
                em.act(ctx.sq_f[:, 0:TBB], xc, AF.Square)
                em.mm(ctx.ps_row2[:, 0:TBB], ctx.ones_col32[:, :], ctx.sq_f[:, 0:TBB],
                      start=(c == 0), stop=(c == NCH - 1))
            em.act(mrow[:, 0:TBB], ctx.ps_row[:, 0:TBB], AF.Copy, scale=1.0 / D)
            em.act(ctx.r1[:, 0:TBB], mrow[:, 0:TBB], AF.Square)
            em.act(ctx.r2[:, 0:TBB], ctx.ps_row2[:, 0:TBB], AF.Copy, scale=1.0 / D)
            em.tt(ctx.r2[:, 0:TBB], ctx.r2[:, 0:TBB], ctx.r1[:, 0:TBB], ALU.subtract)
            em.act(ctx.r1[:, 0:TBB], ctx.r2[:, 0:TBB], AF.Sqrt, bias=1e-5)
            em.emit("vector", lambda e: e.reciprocal(rrow[:, 0:TBB], ctx.r1[:, 0:TBB]),
                    [ctx.r1], [rrow])
            row_bcast(ctx, ctx.rb, rrow[:, 0:TBB], TBB, ctx.ps_bc)
            em.tt(ctx.r1[:, 0:TBB], mrow[:, 0:TBB], rrow[:, 0:TBB], ALU.mult)
            row_bcast(ctx, ctx.mb, ctx.r1[:, 0:TBB], TBB, ctx.ps_bc)
            for c in range(NCH):
                em.tt(ctx.tmp_f[:, 0:TBB], xn_c(c, t0), ctx.rb[:, 0:TBB], ALU.mult)
                em.tt(ctx.tmp_f[:, 0:TBB], ctx.tmp_f[:, 0:TBB], ctx.mb[:, 0:TBB],
                      ALU.subtract)
                em.ts(xn2[:, c * TQ + t0: c * TQ + t0 + TBB], ctx.tmp_f[:, 0:TBB],
                      gcol[:, c:c + 1], ALU.mult)
        for blk in range(2):
            t0 = blk * TBB
            for c in range(NCH):
                em.act(ctx.sq_f[:, 0:TBB], xn_c(c, t0), AF.Square)
                em.mm(ctx.ps_row2[:, 0:TBB], ctx.ones_col32[:, :], ctx.sq_f[:, 0:TBB],
                      start=(c == 0), stop=(c == NCH - 1))
            em.act(ctx.r1[:, 0:TBB], ctx.ps_row2[:, 0:TBB], AF.Ln, bias=1e-6, scale=1.0 / D)
            em.act(rrow[:, 0:TBB], ctx.r1[:, 0:TBB], AF.Exp, scale=-0.5)
            row_bcast(ctx, ctx.rb, rrow[:, 0:TBB], TBB, ctx.ps_bc)
            for c in range(NCH):
                em.tt(ctx.tmp_f[:, 0:TBB], xn_c(c, t0), ctx.rb[:, 0:TBB], ALU.mult)
                em.ts(xp[:, c * TQ + t0: c * TQ + t0 + TBB], ctx.tmp_f[:, 0:TBB],
                      l1col[:, c:c + 1], ALU.mult)

        LN64 = float(np.log(64.0))
        LN4 = float(np.log(4.0))

        def blocks_of(t):
            def mk(t0):
                return lambda c: t[:, c * TQ + t0: c * TQ + t0 + TBB]
            return [mk(0), mk(TBB)]

        xn2_blocks = blocks_of(xn2)
        xp_blocks = blocks_of(xp)

        def ep_l2(dst_dram, lnc, sqscale):
            def ep(blk, mc, ps, oc):
                t0 = blk * TBB
                em.act(ctx.sq_f[:, 0:TBB], ps, AF.Square)
                em.mm(ctx.ps_row[:, 0:TBB], ctx.ones_col32[:, :], ctx.sq_f[:, 0:TBB])
                em.act(ctx.r1[:, 0:TBB], ctx.ps_row[:, 0:TBB], AF.Ln, bias=1e-12,
                       scale=sqscale)
                em.act(ctx.r2[:, 0:TBB], ctx.r1[:, 0:TBB], AF.Exp, scale=-0.5, bias=lnc)
                row_bcast(ctx, ctx.rb, ctx.r2[:, 0:TBB], TBB, ctx.ps_bc)
                ot = otile()
                em.tt(ot[:, 0:TBB], ps, ctx.rb[:, 0:TBB], ALU.mult)
                em.dma("sync", dst_dram[mc * 128:(mc + 1) * 128,
                                        sub * TQ + t0:sub * TQ + t0 + TBB], ot[:, 0:TBB])
            return ep

        matmul_fm(ctx, None, 0, xn2_blocks, TBB, wl, io["wqa"], D, 0, D,
                  epilog=ep_l2(io["qhat_scr"], LN64, 256.0), nblocks=2)

        def ep_khat(dst_sh):
            base = ep_l2(dst_sh, LN4, 1.0)

            def ep(blk, mc, ps, oc):
                base(blk, 0, ps, oc)
            return ep

        matmul_fm(ctx, None, 0, xn2_blocks, TBB, wl, io["wkv"], D, 0, 128,
                  epilog=ep_l2(io["khat_sh_own"], LN4, 1.0), nblocks=2)

        def ep_v(blk, mc, ps, oc):
            t0 = blk * TBB
            em.act(v_fm[:, t0:t0 + TBB], ps, AF.Copy)

        matmul_fm(ctx, None, 0, xn2_blocks, TBB, wl, io["wkv"], D, 128, 256,
                  epilog=ep_v, nblocks=2)
        for ti in range(TQ // 128):
            em.trans(ctx.ps_t[:, :], v_fm[:, ti * 128:(ti + 1) * 128], ctx.ident[:, :])
            ot = otile()
            em.act(ot[:, 0:128], ctx.ps_t[:, :], AF.Copy)
            em.dma("sync", io["va_sh_own"][sub * TQ + ti * 128:sub * TQ + (ti + 1) * 128, :],
                   ot[:, 0:128])

        def ep_store(dst):
            def ep(blk, mc, ps, oc):
                t0 = blk * TBB
                ot = otile()
                em.act(ot[:, 0:TBB], ps, AF.Copy)
                em.dma("sync", dst[mc * 128:(mc + 1) * 128,
                                   sub * TQ + t0:sub * TQ + t0 + TBB], ot[:, 0:TBB])
            return ep

        matmul_fm(ctx, None, 0, xp_blocks, TBB, wl, io["wq0"], D, 0, D,
                  epilog=ep_store(io["q0_scr"]), nblocks=2)
        matmul_fm(ctx, None, 0, xp_blocks, TBB, wl, io["wk0"], D, 0, D,
                  epilog=ep_store(io["k0_sh_own"]), nblocks=2)

        def ep_v0(blk, mc, ps, oc):
            t0 = blk * TBB
            v_loc = otile()
            em.act(v_loc[:, 0:TBB], ps, AF.Copy)
            for tj in range(TBB // 128):
                em.trans(ctx.ps_t[:, :], v_loc[:, tj * 128:(tj + 1) * 128], ctx.ident[:, :])
                ot = otile()
                em.act(ot[:, 0:128], ctx.ps_t[:, :], AF.Copy)
                em.dma("sync", io["v0_sh_own"][sub * TQ + t0 + tj * 128:
                                               sub * TQ + t0 + (tj + 1) * 128,
                                               mc * 128:(mc + 1) * 128], ot[:, 0:128])

        matmul_fm(ctx, None, 0, xp_blocks, TBB, wl, io["wv0"], D, 0, D,
                  epilog=ep_v0, nblocks=2)


def emit_p12(em, ctx, io, sub, prefix_chunks, uid):
    TQ = 1024
    TBB = 512
    with em.scope():
        sb = em.sbuf_scoped
        wl = WLoader(ctx, f"wl1_{uid}", depth=4, cols=512, scoped=True)
        xn = sb(f"f1xn{uid}", [128, NCH * TQ], BF16)
        bufA = sb(f"f1A{uid}", [128, NCH * TQ], BF16)
        bufB = sb(f"f1B{uid}", [128, NCH * TQ], BF16)
        bufC = sb(f"f1C{uid}", [128, NCH * TQ], BF16)
        g_col = sb(f"f1gc{uid}", [128, 16], F32)
        ln2c = sb(f"f1l2{uid}", [128, 16], F32)
        modr = sb(f"f1mod{uid}", [16, TQ], F32)
        mgr = sb(f"f1mg{uid}", [16, TQ], BF16)
        mrow = sb(f"f1mr{uid}", [1, TBB], F32)
        rrow = sb(f"f1rr{uid}", [1, TBB], F32)

        load_col(em, "sync", g_col, io["gout"], 16)
        load_col(em, "sync", ln2c, io["ln2"], 16)
        load_fm(em, "sync", xn, io["xn_scr"], D, sub * TQ, (sub + 1) * TQ, TQ)
        load_fm(em, "sync", bufA, io["qhat_scr"], D, sub * TQ, (sub + 1) * TQ, TQ)
        att_scope = em.scope()
        att_scope.__enter__()
        khat = sb(f"f1kh{uid}", [128, T], BF16)
        vak = sb(f"f1va{uid}", [128, 32 * 128], BF16)
        k0h = [sb(f"f1k0h{uid}", [128, T], BF16)]
        v0h = [sb(f"f1v0h{uid}", [128, 32 * 128], BF16)]
        p_pool = [sb(f"f1pp{uid}_{i}", [128, 512], BF16) for i in range(3)]
        mask = sb(f"f1msk{uid}", [128, 1536], BF16)
        em.dma("gpsimd", khat[:, :], io["khat_sh"])
        for sq in range(8):
            em.dma("gpsimd", vak[:, :].rearrange("p (c f) -> p c f", c=32)[:, sq * 4:(sq + 1) * 4, :],
                   io["va_sh"].rearrange("(c p) f -> p c f", p=128)[:, sq * 4:(sq + 1) * 4, :])
        em.memset(mask[:, :], 1.0)
        em.affsel(mask[:, :], mask[:, :], [[1, 1536]], ALU.is_ge, 0.0, -512, -1)

        def qh_fn(h):
            return bufA[:, h * TQ:(h + 1) * TQ]

        def out_attn(h, i0, ps_o, rb):
            em.tt(bufB[:, h * TQ + i0: h * TQ + i0 + 512], ps_o, rb[:, 0:512], ALU.mult)

        attn_transposed(ctx, em, out_attn, qh_fn,
                        lambda h, jc: khat[:, jc * 128:(jc + 1) * 128],
                        lambda h, jc: vak[:, jc * 128:(jc + 1) * 128],
                        list(range(16)), TQ, lambda i0: 32, p_pool=p_pool)

        sblocks = [lambda c: bufB[:, c * TQ: c * TQ + TBB],
                   lambda c: bufB[:, c * TQ + TBB: c * TQ + 2 * TBB]]

        def ep_spre(blk, mc, ps, oc):
            em.act(bufC[:, mc * TQ + blk * TBB: mc * TQ + blk * TBB + TBB], ps, AF.Copy)

        matmul_fm(ctx, None, 0, sblocks, TBB, wl, io["wout"], D, 0, D,
                  epilog=ep_spre, nblocks=2)
        for blk in range(2):
            t0 = blk * TBB
            for c in range(NCH):
                xc = bufC[:, c * TQ + t0: c * TQ + t0 + TBB]
                em.mm(ctx.ps_row[:, 0:TBB], ctx.ones_col_f[:, :], xc, start=(c == 0),
                      stop=(c == NCH - 1))
                em.act(ctx.sq_f[:, 0:TBB], xc, AF.Square)
                em.mm(ctx.ps_row2[:, 0:TBB], ctx.ones_col32[:, :], ctx.sq_f[:, 0:TBB],
                      start=(c == 0), stop=(c == NCH - 1))
            em.act(mrow[:, 0:TBB], ctx.ps_row[:, 0:TBB], AF.Copy, scale=1.0 / D)
            em.act(ctx.r1[:, 0:TBB], mrow[:, 0:TBB], AF.Square)
            em.act(ctx.r2[:, 0:TBB], ctx.ps_row2[:, 0:TBB], AF.Copy, scale=1.0 / D)
            em.tt(ctx.r2[:, 0:TBB], ctx.r2[:, 0:TBB], ctx.r1[:, 0:TBB], ALU.subtract)
            em.act(ctx.r1[:, 0:TBB], ctx.r2[:, 0:TBB], AF.Sqrt, bias=1e-5)
            em.emit("vector", lambda e: e.reciprocal(rrow[:, 0:TBB], ctx.r1[:, 0:TBB]),
                    [ctx.r1], [rrow])
            row_bcast(ctx, ctx.rb, rrow[:, 0:TBB], TBB, ctx.ps_bc)
            em.tt(ctx.r1[:, 0:TBB], mrow[:, 0:TBB], rrow[:, 0:TBB], ALU.mult)
            row_bcast(ctx, ctx.mb, ctx.r1[:, 0:TBB], TBB, ctx.ps_bc)
            for c in range(NCH):
                xc = bufC[:, c * TQ + t0: c * TQ + t0 + TBB]
                em.tt(ctx.tmp_f[:, 0:TBB], xc, ctx.rb[:, 0:TBB], ALU.mult)
                em.tt(ctx.tmp_f[:, 0:TBB], ctx.tmp_f[:, 0:TBB], ctx.mb[:, 0:TBB],
                      ALU.subtract)
                em.ts(bufB[:, c * TQ + t0: c * TQ + t0 + TBB], ctx.tmp_f[:, 0:TBB],
                      g_col[:, c:c + 1], ALU.mult)

        load_fm(em, "sync", bufA, io["q0_scr"], D, sub * TQ, (sub + 1) * TQ, TQ)

        def load_head_kv(h):
            npc = prefix_chunks
            em.dma("gpsimd", k0h[0][:, 0:npc * 128],
                   io["k0_sh"][h * 128:(h + 1) * 128, 0:npc * 128])
            nseg = (npc + 3) // 4
            for sq in range(nseg):
                c1 = min((sq + 1) * 4, npc)
                em.dma("gpsimd",
                       v0h[0][:, :].rearrange("p (c f) -> p c f", c=32)[:, sq * 4:c1, :],
                       io["v0_sh"][:, h * 128:(h + 1) * 128]
                       .rearrange("(c p) f -> p c f", p=128)[:, sq * 4:c1, :])

        SC = 1.0 / float(np.sqrt(128.0))

        def out_attn0(h, i0, ps_o, rb):
            em.tt(bufA[:, h * TQ + i0: h * TQ + i0 + 512], ps_o, rb[:, 0:512], ALU.mult)

        g0 = (prefix_chunks * 128) - TQ
        for h in range(16):
            load_head_kv(h)
            attn_transposed(ctx, em, out_attn0,
                            lambda hh: bufA[:, hh * TQ:(hh + 1) * TQ],
                            lambda hh, jc: k0h[0][:, jc * 128:(jc + 1) * 128],
                            lambda hh, jc: v0h[0][:, jc * 128:(jc + 1) * 128],
                            [h], TQ, lambda i0: (g0 + i0 + 512) // 128,
                            mask_master=mask, g0=g0, p_pool=p_pool, extra_scale=SC)

        att_scope.__exit__(None, None, None)
        ablocks = [lambda c: bufA[:, c * TQ: c * TQ + TBB],
                   lambda c: bufA[:, c * TQ + TBB: c * TQ + 2 * TBB]]

        def ep_attnf(blk, mc, ps, oc):
            em.act(bufC[:, mc * TQ + blk * TBB: mc * TQ + blk * TBB + TBB], ps, AF.Copy)

        matmul_fm(ctx, None, 0, ablocks, TBB, wl, io["wo0"], D, 0, D,
                  epilog=ep_attnf, nblocks=2)

        ps_m = ctx.ps_bc
        for blk in range(2):
            t0 = blk * TBB
            for c in range(NCH):
                w = wl.get(io["a_lora"], c * 128, 0, rows=128, cols=16)
                em.mm(ps_m[0:16, 0:TBB], w, bufC[:, c * TQ + t0: c * TQ + t0 + TBB],
                      start=(c == 0), stop=(c == NCH - 1))
            em.act(modr[0:16, t0:t0 + TBB], ps_m[0:16, 0:TBB], AF.Tanh)
        for blk in range(2):
            t0 = blk * TBB
            for c in range(NCH):
                em.act(ctx.sq_f[:, 0:TBB], xn[:, c * TQ + t0: c * TQ + t0 + TBB], AF.Square)
                em.mm(ctx.ps_row2[:, 0:TBB], ctx.ones_col32[:, :], ctx.sq_f[:, 0:TBB],
                      start=(c == 0), stop=(c == NCH - 1))
            em.act(ctx.r1[:, 0:TBB], ctx.ps_row2[:, 0:TBB], AF.Ln, bias=1e-6, scale=1.0 / D)
            em.act(rrow[:, 0:TBB], ctx.r1[:, 0:TBB], AF.Exp, scale=-0.5)
            row_bcast(ctx, ctx.rb, rrow[:, 0:TBB], TBB, ctx.ps_bc)
            for c in range(NCH):
                em.tt(ctx.tmp_f[:, 0:TBB], xn[:, c * TQ + t0: c * TQ + t0 + TBB],
                      ctx.rb[:, 0:TBB], ALU.mult)
                em.ts(bufA[:, c * TQ + t0: c * TQ + t0 + TBB], ctx.tmp_f[:, 0:TBB],
                      ln2c[:, c:c + 1], ALU.mult)
        for blk in range(2):
            t0 = blk * TBB
            for c in range(NCH):
                w = wl.get(io["a_lora"], c * 128, 0, rows=128, cols=16)
                em.mm(ps_m[0:16, 0:TBB], w, bufA[:, c * TQ + t0: c * TQ + t0 + TBB],
                      start=(c == 0), stop=(c == NCH - 1))
            em.tt(ctx.tmp_f[0:16, 0:TBB], modr[0:16, t0:t0 + TBB], ps_m[0:16, 0:TBB],
                  ALU.mult)
            em.ts(mgr[0:16, t0:t0 + TBB], ctx.tmp_f[0:16, 0:TBB], 0.1, ALU.mult)

        TM = 512
        mlp_scope = em.scope()
        mlp_scope.__enter__()
        inter2 = sb(f"f1i2{uid}", [128, 32 * TM], BF16)
        gate_t = sb(f"f1gt{uid}", [128, TM], BF16)
        up_t = sb(f"f1ut{uid}", [128, TM], BF16)
        xchunk = sb(f"f1xc{uid}", [128, TM], F32)

        def inter_at(fc):
            if fc < 32:
                return bufC[:, fc * TM:(fc + 1) * TM]
            return inter2[:, (fc - 32) * TM:(fc - 31) * TM]

        for blk in range(2):
            t0 = blk * TM
            h2b = [lambda c, t0=t0: bufA[:, c * TQ + t0: c * TQ + t0 + TM]]
            for fg in range(0, 64, 2):
                for k in range(NCH):
                    w = wl.get(io["w_in"], k * 128, fg * 128, cols=256)
                    for i in range(2):
                        em.mm(ctx.psum_pool[i][:, 0:TM], w[0:128, i * 128:(i + 1) * 128],
                              h2b[0](k), start=(k == 0), stop=False)
                wb = wl.get(io["b_lora"], 0, fg * 128, rows=16, cols=256)
                for i in range(2):
                    em.mm(ctx.psum_pool[i][:, 0:TM], wb[0:16, i * 128:(i + 1) * 128],
                          mgr[0:16, t0:t0 + TM], start=False, stop=True)
                    em.act(gate_t[:, 0:TM], ctx.psum_pool[i][:, 0:TM], AF.Silu)
                    em.copy(inter_at(fg + i), gate_t[:, 0:TM], eng="gpsimd")
                for k in range(NCH):
                    w = wl.get(io["w_up"], k * 128, fg * 128, cols=256)
                    for i in range(2):
                        em.mm(ctx.psum_pool[2 + i][:, 0:TM], w[0:128, i * 128:(i + 1) * 128],
                              h2b[0](k), start=(k == 0), stop=(k == NCH - 1))
                for i in range(2):
                    em.act(up_t[:, 0:TM], ctx.psum_pool[2 + i][:, 0:TM], AF.Copy)
                    em.tt(inter_at(fg + i), inter_at(fg + i), up_t[:, 0:TM], ALU.mult)
            for dg in range(0, NCH, 2):
                for fc in range(64):
                    w = wl.get(io["w_dn"], fc * 128, dg * 128, cols=256)
                    for i in range(2):
                        em.mm(ctx.psum_pool[i][:, 0:TM], w[0:128, i * 128:(i + 1) * 128],
                              inter_at(fc), start=(fc == 0), stop=(fc == 63))
                for i in range(2):
                    dmc = dg + i
                    em.tt(ctx.tmp_f[:, 0:TM], ctx.psum_pool[i][:, 0:TM],
                          xn[:, dmc * TQ + t0: dmc * TQ + t0 + TM], ALU.add)
                    em.act(gate_t[:, 0:TM], ctx.tmp_f[:, 0:TM], AF.Silu)
                    em.tt(gate_t[:, 0:TM], gate_t[:, 0:TM],
                          bufB[:, dmc * TQ + t0: dmc * TQ + t0 + TM], ALU.mult)
                    em.dma("sync", xchunk[:, 0:TM],
                           io["xT"][dmc * 128:(dmc + 1) * 128,
                                    sub * TQ + t0: sub * TQ + t0 + TM])
                    em.tt(ctx.tmp_f[:, 0:TM], xchunk[:, 0:TM], gate_t[:, 0:TM], ALU.add)
                    em.dma("sync", io["x1_scr"][dmc * 128:(dmc + 1) * 128,
                                                sub * TQ + t0: sub * TQ + t0 + TM],
                           ctx.tmp_f[:, 0:TM])
        mlp_scope.__exit__(None, None, None)


def emit_l1_xn1(em, ctx, io, role, uid):
    TEND = 2048
    TBB = 512
    with em.scope():
        sb = em.sbuf_scoped
        xblk = sb(f"sxb{uid}", [128, NCH * TBB], F32)
        mrow_all = sb(f"smr{uid}", [1, TEND], F32)
        rrow_all = sb(f"srr{uid}", [1, TEND], F32)
        xn1dc = sb(f"sxn{uid}", [128, TBB], BF16)
        for blk in range(4):
            t0 = blk * TBB
            load_fm(em, "sync", xblk, io["x1_scr"], D, t0, t0 + TBB, TBB)
            emit_ln_rows(ctx, xblk, TBB, ctx.ps_row, ctx.ps_row2,
                         mrow_all[:, t0:t0 + TBB], rrow_all[:, t0:t0 + TBB], 1e-5,
                         sq_scratch=ctx.sq_f, ones=ctx.ones_col32)
            row_bcast(ctx, ctx.rb, rrow_all[:, t0:t0 + TBB], TBB, ctx.ps_bc)
            em.tt(ctx.r1[:, 0:TBB], mrow_all[:, t0:t0 + TBB], rrow_all[:, t0:t0 + TBB],
                  ALU.mult)
            row_bcast(ctx, ctx.mb, ctx.r1[:, 0:TBB], TBB, ctx.ps_bc)
            for c in range(NCH):
                em.tt(ctx.tmp_f[:, 0:TBB], chunk(xblk, c, TBB), ctx.rb[:, 0:TBB], ALU.mult)
                em.tt(xn1dc[:, 0:TBB], ctx.tmp_f[:, 0:TBB], ctx.mb[:, 0:TBB], ALU.subtract)
                em.dma("sync", io["xn1_sh_own"][c * 128:(c + 1) * 128, t0:t0 + TBB],
                       xn1dc[:, 0:TBB])
        em.dma("sync", io["lnrows_scr"][0:1, 0:TEND], mrow_all[:, 0:TEND])
        em.dma("sync", io["lnrows_scr"][1:2, 0:TEND], rrow_all[:, 0:TEND])


def emit_l1_scan(em, ctx, io, role, uid):
    TEND = 2048
    TBB = 512
    with em.scope():
        sb = em.sbuf_scoped
        wl = WLoader(ctx, f"wls_{uid}", depth=4, cols=512, scoped=True)
        xp = sb(f"scxp{uid}", [96, TEND], F32)
        mrow_all = sb(f"scmr{uid}", [1, TEND], F32)
        rrow_all = sb(f"scrr{uid}", [1, TEND], F32)
        w1s = sb(f"scw1{uid}", [96, 1], F32)
        acol = sb(f"scac{uid}", [128, 16 * 16], F32)
        bdtc = sb(f"scbd{uid}", [128, 16], F32)
        dpc = sb(f"scdp{uid}", [128, 16], F32)
        xn1dc = sb(f"scxn{uid}", [128, TEND], BF16)
        dtdc = sb(f"scdt{uid}", [128, TEND], F32)
        s1p = sb(f"scs1{uid}", [128, TEND], BF16)
        dA = sb(f"scda{uid}", [128, TEND], BF16)
        ut = sb(f"scut{uid}", [128, TEND], BF16)
        ht = sb(f"scht{uid}", [128, TEND], BF16)
        ydc = sb(f"scy{uid}", [128, TEND], F32)
        yt32 = sb(f"scy3{uid}", [128, TEND], F32)
        bbc = sb(f"scbb{uid}", [128, TBB], BF16)
        cbc = sb(f"sccb{uid}", [128, TEND], BF16)
        brow = sb(f"scbr{uid}", [1, TEND], F32)
        crow = sb(f"sccr{uid}", [1, TEND], F32)
        zrow = sb(f"sczr{uid}", [128, 1], F32)
        hin = sb(f"schi{uid}", [128, 16], F32)
        hfin = sb(f"schf{uid}", [128, 16], F32)
        em.memset(zrow[:, :], 0.0)

        load_col(em, "sync", bdtc, io["bdt"], 16)
        load_col(em, "sync", dpc, io["dcol"], 16)
        em.dma("sync", mrow_all[:, 0:TEND], io["lnrows_scr"][0:1, 0:TEND])
        em.dma("sync", rrow_all[:, 0:TEND], io["lnrows_scr"][1:2, 0:TEND])
        for c in range(NCH):
            em.dma("sync", ctx.tmp_f[:, 0:16], io["alog"][c * 128:(c + 1) * 128, :])
            em.act(acol[:, c * 16:(c + 1) * 16], ctx.tmp_f[:, 0:16], AF.Exp)
        em.ts(acol[:, :], acol[:, :], -1.0, ALU.mult)
        for k in range(NCH):
            w = wl.getf32(io["wxT"], k * 128, 0, rows=128, cols=96)
            em.mm(ctx.ps_bc[0:96, 0:1], w, ctx.ones_col32[:, :], start=(k == 0),
                  stop=(k == NCH - 1))
        em.act(w1s[:, :], ctx.ps_bc[0:96, 0:1], AF.Copy)
        xblk = dA
        for blk in range(4):
            t0 = blk * TBB
            for k in range(NCH):
                em.dma("sync", ctx.sq_f[:, 0:TBB],
                       io["x1_scr"][k * 128:(k + 1) * 128, t0:t0 + TBB])
                w = wl.getf32(io["wxT"], k * 128, 0, rows=128, cols=96)
                em.mm(ctx.ps_bc[0:96, 0:TBB], w, ctx.sq_f[:, 0:TBB], start=(k == 0),
                      stop=(k == NCH - 1))
            em.act(xp[0:96, t0:t0 + TBB], ctx.ps_bc[0:96, 0:TBB], AF.Copy)
        for blk in range(4):
            t0 = blk * TBB
            em.mm(ctx.ps_bc[0:96, 0:TBB], ctx.ones_row_f[:, 0:96], mrow_all[:, t0:t0 + TBB])
            em.ts(ctx.tmp_f[0:96, 0:TBB], ctx.ps_bc[0:96, 0:TBB], w1s[0:96, 0:1], ALU.mult)
            em.tt(ctx.tmp_f[0:96, 0:TBB], xp[0:96, t0:t0 + TBB], ctx.tmp_f[0:96, 0:TBB],
                  ALU.subtract)
            em.mm(ctx.ps_bc[0:96, 0:TBB], ctx.ones_row_f[:, 0:96], rrow_all[:, t0:t0 + TBB])
            em.emit("vector", lambda e, t0=t0: e.tensor_tensor(
                xp[0:96, t0:t0 + TBB], ctx.tmp_f[0:96, 0:TBB], ctx.ps_bc[0:96, 0:TBB],
                ALU.mult), [ctx.tmp_f, ctx.ps_bc], [xp])
        em.dma("sync", io["bc_scr"][:, 0:TEND], xp[64:96, 0:TEND])

        def bbD(r0, r1):
            return io["wg_dec"][r0:r1, 0:TEND]

        def cbD(r0, r1):
            return io["wg_dec"][r0:r1, TEND:2 * TEND]
        bstg = [sb(f"scbs{uid}_{i}", [128, TEND], BF16) for i in range(4)]
        for n in range(16):
            em.dma("sync", brow[:, 0:TEND], io["bc_scr"][n:n + 1, 0:TEND])
            em.dma("gpsimd", crow[:, 0:TEND], io["bc_scr"][16 + n:17 + n, 0:TEND])
            tb_b = bstg[(2 * n) % 4]
            tb_c = bstg[(2 * n + 1) % 4]
            for sb0 in range(0, TEND, TBB):
                em.mm(ctx.ps_bc[:, 0:TBB], ctx.ones_row_f[:, :], brow[:, sb0:sb0 + TBB])
                em.act(tb_b[:, sb0:sb0 + TBB], ctx.ps_bc[:, 0:TBB], AF.Copy)
            em.dma("sync", bbD(n * 128, (n + 1) * 128), tb_b[:, 0:TEND])
            for sb0 in range(0, TEND, TBB):
                em.mm(ctx.ps_bc[:, 0:TBB], ctx.ones_row_f[:, :], crow[:, sb0:sb0 + TBB])
                em.act(tb_c[:, sb0:sb0 + TBB], ctx.ps_bc[:, 0:TBB], AF.Copy)
            em.dma("gpsimd", cbD(n * 128, (n + 1) * 128), tb_c[:, 0:TEND])
        bbc2 = [sb(f"scb2{uid}_{i}", [128, TEND], BF16) for i in range(2)]
        cbc2 = [sb(f"scc2{uid}_{i}", [128, TEND], BF16) for i in range(2)]

        for dc in range(NCH):
            for blk in range(4):
                t0 = blk * TBB
                w = wl.getf32(io["wdtT"], 0, dc * 128, rows=64, cols=128)
                em.mm(ctx.psum_pool[0][:, 0:TBB], w, xp[0:64, t0:t0 + TBB])
                em.act(ctx.tmp_f[:, 0:TBB], ctx.psum_pool[0][:, 0:TBB], AF.Identity,
                       bias=bdtc[:, dc:dc + 1])
                em.act(ctx.sq_f[:, 0:TBB], ctx.tmp_f[:, 0:TBB], AF.Abs)
                em.act(ctx.sq_f[:, 0:TBB], ctx.sq_f[:, 0:TBB], AF.Exp, scale=-1.0)
                em.act(ctx.sq_f[:, 0:TBB], ctx.sq_f[:, 0:TBB], AF.Ln, bias=1.0)
                em.ts(ctx.tmp_f[:, 0:TBB], ctx.tmp_f[:, 0:TBB], 0.0, ALU.max)
                em.tt(dtdc[:, t0:t0 + TBB], ctx.tmp_f[:, 0:TBB], ctx.sq_f[:, 0:TBB],
                      ALU.add)
            em.dma("sync", xn1dc[:, 0:TEND],
                   io["xn1_sh_own"][dc * 128:(dc + 1) * 128, 0:TEND])
            if role == 1:
                em.dma("gpsimd", hin[:, 0:16], io["h1_sh"][dc * 128:(dc + 1) * 128, :])
            em.tt(s1p[:, 0:TEND], dtdc[:, 0:TEND], xn1dc[:, 0:TEND], ALU.mult)
            em.memset(ydc[:, :], 0.0)
            if dc == 0:
                em.dma("sync", bbc2[0][:, 0:TEND], bbD(0, 128))
                em.dma("gpsimd", cbc2[0][:, 0:TEND], cbD(0, 128))
            for n in range(16):
                nn = n + 1
                if nn < 16:
                    em.dma("sync", bbc2[nn % 2][:, 0:TEND],
                           bbD(nn * 128, (nn + 1) * 128))
                    em.dma("gpsimd", cbc2[nn % 2][:, 0:TEND],
                           cbD(nn * 128, (nn + 1) * 128))
                elif dc + 1 < NCH:
                    em.dma("sync", bbc2[0][:, 0:TEND], bbD(0, 128))
                    em.dma("gpsimd", cbc2[0][:, 0:TEND], cbD(0, 128))
                em.act(dA[:, 0:TEND], dtdc[:, 0:TEND], AF.Exp,
                       scale=acol[:, dc * 16 + n: dc * 16 + n + 1])
                em.tt(ut[:, 0:TEND], s1p[:, 0:TEND], bbc2[n % 2][:, 0:TEND], ALU.mult)
                if role == 0:
                    em.scan(ht[:, 0:TEND], dA[:, 0:TEND], ut[:, 0:TEND], zrow[:, 0:1])
                else:
                    em.scan(ht[:, 0:TEND], dA[:, 0:TEND], ut[:, 0:TEND], hin[:, n:n + 1])
                em.tt(yt32[:, 0:TEND], ht[:, 0:TEND], cbc2[n % 2][:, 0:TEND], ALU.mult)
                em.tt(ydc[:, 0:TEND], ydc[:, 0:TEND], yt32[:, 0:TEND], ALU.add)
                if role == 0:
                    em.copy(hfin[:, n:n + 1], ht[:, TEND - 1:TEND])
            if role == 0:
                em.dma("gpsimd", io["h1_sh"][dc * 128:(dc + 1) * 128, :], hfin[:, 0:16])
            for blk in range(4):
                t0 = blk * TBB
                em.ts(ctx.tmp_f[:, 0:TBB], xn1dc[:, t0:t0 + TBB], dpc[:, dc:dc + 1],
                      ALU.mult)
                em.tt(ctx.tmp_f[:, 0:TBB], ctx.tmp_f[:, 0:TBB], ydc[:, t0:t0 + TBB],
                      ALU.add)
                em.dma("sync", io["s1_scr"][dc * 128:(dc + 1) * 128, t0:t0 + TBB],
                       ctx.tmp_f[:, 0:TBB])


def emit_l1_ffn(em, ctx, io, role, uid):
    TALL = 4096
    with em.scope():
        sb = em.sbuf_scoped
        ctx.dq_nf = sb(f"qnf{uid}", [128, 512], F32)
        ctx.dq_g = sb(f"qg{uid}", [128, 512], F32)
        ctx.dq_j = sb(f"qj{uid}", [128, 512], F32)
        ctx.dq_r1 = sb(f"qr1{uid}", [128, 512], F32)
        ctx.dq_r2 = sb(f"qr2{uid}", [128, 512], F32)
        it_raw = sb(f"qir{uid}", [128, 512], I32)
        it_lo = sb(f"qil{uid}", [128, 512], I32)
        it_hi = sb(f"qih{uid}", [128, 512], I32)
        sc8 = sb(f"qs8{uid}", [8, 8192], F32)
        sct = sb(f"qst{uid}", [128, 512], F32)
        e8 = sb(f"qe8{uid}", [8, 128], F32)
        dec_o = [sb(f"qdo{uid}_{i}", [128, 512], BF16) for i in range(4)]
        em.memset(e8[:, :], 1.0)
        em.affsel(e8[:, :], e8[:, :], [[1, 128]], ALU.is_ge, 0.0, 0, -16)
        em.affsel(e8[:, :], e8[:, :], [[-1, 128]], ALU.is_ge, 0.0, 15, 16)

        for (pd, sd, od, nkc, ncol) in ((io["gpT"], io["gsT"], io["wg_dec"], 8, 16),
                                        (io["upT"], io["usT"], io["wu_dec"], 8, 16),
                                        (io["dnT"], io["dsT"], io["wd_dec"], 32, 4)):
            for kc in range(nkc):
                em.dma("sync", sc8[:, 0:ncol * 512], sd[kc * 8:(kc + 1) * 8, :])
                for nc4 in range(ncol):
                    n0 = nc4 * 512
                    em.dma("sync", it_raw[:, :], pd[kc * 128:(kc + 1) * 128, n0:n0 + 512])
                    em.mm(ctx.ps_bc[:, 0:512], e8[:, :], sc8[0:8, n0:n0 + 512])
                    em.act(sct[:, :], ctx.ps_bc[:, 0:512], AF.Copy)
                    em.ts(it_lo[:, :], it_raw[:, :], 15, ALU.bitwise_and, eng="vector")
                    em.ts(it_hi[:, :], it_raw[:, :], 4, ALU.logical_shift_right, eng="vector")
                    em.ts(it_hi[:, :], it_hi[:, :], 15, ALU.bitwise_and, eng="vector")
                    lo_t = dec_o[(kc * ncol + nc4) % 2]
                    hi_t = dec_o[2 + (kc * ncol + nc4) % 2]
                    emit_decode(ctx, it_lo[:, :], sct[:, :], lo_t[:, 0:512], 512)
                    em.dma("sync", od[kc * 256:kc * 256 + 128, n0:n0 + 512], lo_t[:, 0:512])
                    emit_decode(ctx, it_hi[:, :], sct[:, :], hi_t[:, 0:512], 512)
                    em.dma("sync", od[kc * 256 + 128:kc * 256 + 256, n0:n0 + 512],
                           hi_t[:, 0:512])

        xnt = em.sbuf_scoped(f"qxn{uid}", [128, 16 * 512], BF16)
        inter_t = em.sbuf_scoped(f"qit{uid}", [128, 64 * 512], BF16)
        gate_t = em.sbuf_scoped(f"qgt{uid}", [128, 512], BF16)
        up_t = em.sbuf_scoped(f"qut{uid}", [128, 512], BF16)
        out_t = [em.sbuf_scoped(f"qot{uid}_{i}", [128, 512], BF16) for i in range(2)]
        bl = BLoader(ctx, f"qbl{uid}", depth=4, cols=512, scoped=True)
        TBL = 512
        for tb in range(TALL // TBL):
            t0 = tb * TBL
            for kc in range(8):
                em.dma("sync", xnt[:, (2 * kc) * TBL:(2 * kc + 1) * TBL],
                       io["xn1_sh"][kc * 256:(kc + 1) * 256, t0:t0 + TBL]
                       .rearrange("(p two) t -> two p t", two=2)[0])
                em.dma("sync", xnt[:, (2 * kc + 1) * TBL:(2 * kc + 2) * TBL],
                       io["xn1_sh"][kc * 256:(kc + 1) * 256, t0:t0 + TBL]
                       .rearrange("(p two) t -> two p t", two=2)[1])
            for ng in range(0, 64, 4):
                for k in range(16):
                    w = bl.get(io["wg_dec"], k * 128, ng * 128, cols=512)
                    for i in range(4):
                        em.mm(ctx.psum_pool[i][:, 0:TBL], w[0:128, i * 128:(i + 1) * 128],
                              xnt[:, k * TBL:(k + 1) * TBL], start=(k == 0), stop=(k == 15))
                for i in range(4):
                    em.act(inter_t[:, (ng + i) * TBL:(ng + i + 1) * TBL],
                           ctx.psum_pool[i][:, 0:TBL], AF.Gelu_apprx_tanh)
                for k in range(16):
                    w = bl.get(io["wu_dec"], k * 128, ng * 128, cols=512)
                    for i in range(4):
                        em.mm(ctx.psum_pool[i][:, 0:TBL], w[0:128, i * 128:(i + 1) * 128],
                              xnt[:, k * TBL:(k + 1) * TBL], start=(k == 0), stop=(k == 15))
                for i in range(4):
                    em.tt(inter_t[:, (ng + i) * TBL:(ng + i + 1) * TBL],
                          inter_t[:, (ng + i) * TBL:(ng + i + 1) * TBL],
                          ctx.psum_pool[i][:, 0:TBL], ALU.mult)
            own_half = (t0 // 2048) == role
            dst = io["part_own"] if own_half else io["part_sh_mine"]
            dcol = t0 % 2048
            for dg in range(0, 16, 4):
                for k in range(64):
                    w = bl.get(io["wd_dec"], k * 128, dg * 128, cols=512)
                    for i in range(4):
                        em.mm(ctx.psum_pool[i][:, 0:TBL], w[0:128, i * 128:(i + 1) * 128],
                              inter_t[:, k * TBL:(k + 1) * TBL],
                              start=(k == 0), stop=(k == 63))
                for i in range(4):
                    ot = out_t[i % 2]
                    em.act(ot[:, 0:TBL], ctx.psum_pool[i][:, 0:TBL], AF.Copy)
                    em.dma("sync", dst[(dg + i) * 128:(dg + i + 1) * 128, dcol:dcol + TBL],
                           ot[:, 0:TBL])


def emit_l2(em, ctx, io, role, uid):
    TQ = 1024
    NKV = 4
    with em.scope():
        sb = em.sbuf_scoped
        acc = [sb(f"l2ac{uid}_{i}", [128, 2048], F32) for i in range(2)]
        pstage = [sb(f"l2ps{uid}_{i}", [128, 2048], BF16) for i in range(2)]
        for c in range(NCH):
            sl = slice(c * 128, (c + 1) * 128)
            em.dma("sync", pstage[0][:, 0:2048], io["part_own"][sl, :])
            em.dma("gpsimd", pstage[1][:, 0:2048], io["part_sh_other"][sl, :])
            em.tt(acc[0][:, 0:2048], pstage[0][:, 0:2048], pstage[1][:, 0:2048], ALU.add)
            em.act(acc[1][:, 0:2048], acc[0][:, 0:2048], AF.Silu)
            em.dma("sync", acc[0][:, 0:2048], io["s1_scr"][sl, :])
            em.tt(acc[1][:, 0:2048], acc[1][:, 0:2048], acc[0][:, 0:2048], ALU.mult)
            em.dma("sync", acc[0][:, 0:2048], io["x1_scr"][sl, :])
            em.tt(acc[0][:, 0:2048], acc[0][:, 0:2048], acc[1][:, 0:2048], ALU.add)
            em.dma("sync", io["x2s"][sl, 128:128 + 2048], acc[0][:, 0:2048])
        if role == 0:
            x2e = sb(f"l2xe{uid}", [128, 128], F32)
            for c in range(NCH):
                em.dma("sync", x2e[:, :], io["x2s"][c * 128:(c + 1) * 128,
                                                    128 + 1920:128 + 2048])
                em.dma("gpsimd", io["x2e_sh"][c * 128:(c + 1) * 128, :], x2e[:, :])
            io["pair"].send(F_X2E, 0, [io["h_x2e_sh"]])
        else:
            io["pair"].wait(F_X2E, 1, io["body_idx"])
            for c in range(NCH):
                em.dma("gpsimd", ctx.tmp_f[:, 0:128], io["x2e_sh"][c * 128:(c + 1) * 128, :])
                em.dma("sync", io["x2s"][c * 128:(c + 1) * 128, 0:128], ctx.tmp_f[:, 0:128])

    for sub in range(2):
        halo = 0 if (role == 0 and sub == 0) else 128
        TH = TQ + halo
        S0 = sub * 1024 + (128 - halo)
        OW0 = sub * 1024
        _emit_l2_sub(em, ctx, io, None, None, uid * 10 + sub, halo, TH, S0, OW0)


def _emit_l2_sub(em, ctx, io, _w1, _w2, uid, halo, TH, S0, OW0):
    TQ = 1024
    NKV = 4
    with em.scope():
        sb = em.sbuf_scoped
        wl = WLoader(ctx, f"wl2_{uid}", depth=4, cols=512, scoped=True)
        xn2 = sb(f"w2xn{uid}", [128, NCH * TH], BF16)
        qbuf = sb(f"w2qb{uid}", [128, NCH * TQ], BF16)
        qnc = sb(f"w2qn{uid}", [128, 1], F32)
        knc = sb(f"w2kn{uid}", [128, 1], F32)
        scol = sb(f"w2sc{uid}", [128, 1], F32)
        bqc = sb(f"w2bq{uid}", [128, 16], F32)
        bkc = sb(f"w2bk{uid}", [128, 4], F32)
        bvc = sb(f"w2bv{uid}", [128, 4], F32)
        boc = sb(f"w2bo{uid}", [128, 16], F32)
        b0c = sb(f"w2b0{uid}", [128, 64], F32)
        b1c = sb(f"w2b1{uid}", [128, 16], F32)
        mrow = sb(f"w2mr{uid}", [1, 512], F32)
        rrow = sb(f"w2rr{uid}", [1, 512], F32)
        sc_sb = sb(f"w2ss{uid}", [1, 1], F32)
        att_scope = em.scope()
        att_scope.__enter__()
        s2b = sb(f"w2s2{uid}", [128, NCH * TQ], BF16)
        ks = sb(f"w2ks{uid}", [128, NKV * TH], BF16)
        vtok = sb(f"w2vt{uid}", [128, NKV * 9 * 128], BF16)
        cosb = sb(f"w2cs{uid}", [64, TH], BF16)
        sinb = sb(f"w2sn{uid}", [64, TH], BF16)
        p_pool = [sb(f"w2pp{uid}_{i}", [128, 512], BF16) for i in range(3)]
        wmask = sb(f"w2wm{uid}", [128, 1664], BF16)
        odt = sb(f"w2od{uid}", [64, 512], BF16)

        load_col(em, "sync", bqc, io["bq"], 16)
        load_col(em, "sync", bkc, io["bk"], 4)
        load_col(em, "sync", bvc, io["bv"], 4)
        load_col(em, "sync", boc, io["bo"], 16)
        load_col(em, "sync", b0c, io["b0"], 64)
        load_col(em, "sync", b1c, io["b1"], 16)
        em.dma("sync", qnc[:, :], io["qnw"])
        em.dma("sync", knc[:, :], io["knw"])
        em.dma("sync", sc_sb[:, :], io["att_scale"])
        em.mm(ctx.ps_bc[:, 0:1], ctx.ones_row_f[:, :], sc_sb[:, :])
        em.act(scol[:, :], ctx.ps_bc[:, 0:1], AF.Copy)
        for t0 in range(0, TH, 512):
            t1 = min(t0 + 512, TH)
            em.dma("sync", ctx.tmp_f[0:64, 0:t1 - t0], io["thS"][:, S0 + t0:S0 + t1])
            em.act(sinb[0:64, t0:t1], ctx.tmp_f[0:64, 0:t1 - t0], AF.Sin)
            em.dma("sync", ctx.tmp_f[0:64, 0:t1 - t0], io["thC"][:, S0 + t0:S0 + t1])
            em.act(cosb[0:64, t0:t1], ctx.tmp_f[0:64, 0:t1 - t0], AF.Sin)
        em.memset(wmask[:, :], 1.0)
        em.affsel(wmask[:, :], wmask[:, :], [[1, 1664]], ALU.is_ge, 0.0, -576, -1)
        em.affsel(wmask[:, :], wmask[:, :], [[-1, 1664]], ALU.is_ge, 0.0, 576 + 64, 1)

        for t0 in range(0, TH, 512):
            t1 = min(t0 + 512, TH)
            tb = t1 - t0
            for c in range(NCH):
                em.dma("sync", ctx.sq_f[:, 0:tb],
                       io["x2s"][c * 128:(c + 1) * 128, S0 + t0:S0 + t1])
                em.mm(ctx.ps_row[:, 0:tb], ctx.ones_col32[:, :], ctx.sq_f[:, 0:tb],
                      start=(c == 0), stop=(c == NCH - 1))
                em.act(ctx.tmp_f[:, 0:tb], ctx.sq_f[:, 0:tb], AF.Square)
                em.mm(ctx.ps_row2[:, 0:tb], ctx.ones_col32[:, :], ctx.tmp_f[:, 0:tb],
                      start=(c == 0), stop=(c == NCH - 1))
            em.act(mrow[:, 0:tb], ctx.ps_row[:, 0:tb], AF.Copy, scale=1.0 / D)
            em.act(ctx.r1[:, 0:tb], mrow[:, 0:tb], AF.Square)
            em.act(ctx.r2[:, 0:tb], ctx.ps_row2[:, 0:tb], AF.Copy, scale=1.0 / D)
            em.tt(ctx.r2[:, 0:tb], ctx.r2[:, 0:tb], ctx.r1[:, 0:tb], ALU.subtract)
            em.act(ctx.r1[:, 0:tb], ctx.r2[:, 0:tb], AF.Sqrt, bias=1e-5)
            em.emit("vector", lambda e, tb=tb: e.reciprocal(rrow[:, 0:tb], ctx.r1[:, 0:tb]),
                    [ctx.r1], [rrow])
            row_bcast(ctx, ctx.rb, rrow[:, 0:tb], tb, ctx.ps_bc)
            em.tt(ctx.r1[:, 0:tb], mrow[:, 0:tb], rrow[:, 0:tb], ALU.mult)
            row_bcast(ctx, ctx.mb, ctx.r1[:, 0:tb], tb, ctx.ps_bc)
            for c in range(NCH):
                em.dma("sync", ctx.sq_f[:, 0:tb],
                       io["x2s"][c * 128:(c + 1) * 128, S0 + t0:S0 + t1])
                em.tt(ctx.tmp_f[:, 0:tb], ctx.sq_f[:, 0:tb], ctx.rb[:, 0:tb], ALU.mult)
                em.tt(xn2[:, c * TH + t0:c * TH + t1], ctx.tmp_f[:, 0:tb], ctx.mb[:, 0:tb],
                      ALU.subtract)

        def emit_rope(buf, off, n, tab0):
            ev = buf[0:64, off:off + n]
            od = buf[64:128, off:off + n]
            c_ = cosb[0:64, tab0:tab0 + n]
            s_ = sinb[0:64, tab0:tab0 + n]
            t1_ = ctx.tmp_f[0:64, 0:n]
            t2_ = ctx.sq_f[0:64, 0:n]
            t3_ = ctx.rb[0:64, 0:n]
            t4_ = ctx.mb[0:64, 0:n]
            em.copy(odt[0:64, 0:n], od)
            em.tt(t1_, ev, c_, ALU.mult)
            em.tt(t2_, odt[0:64, 0:n], s_, ALU.mult)
            em.tt(t3_, ev, s_, ALU.mult)
            em.tt(t4_, odt[0:64, 0:n], c_, ALU.mult)
            em.tt(ev, t1_, t2_, ALU.subtract)
            em.tt(od, t3_, t4_, ALU.add)

        def head_norm(ps, bias_col, w_col, out_ap, n, eps=1e-8):
            em.act(ctx.mb[:, 0:n], ps, AF.Identity, bias=bias_col)
            em.act(ctx.sq_f[:, 0:n], ctx.mb[:, 0:n], AF.Square)
            em.mm(ctx.ps_row[:, 0:n], ctx.ones_col32[:, :], ctx.sq_f[:, 0:n])
            em.act(ctx.r1[:, 0:n], ctx.ps_row[:, 0:n], AF.Ln, bias=eps, scale=1.0 / 128.0)
            em.act(ctx.r2[:, 0:n], ctx.r1[:, 0:n], AF.Exp, scale=-0.5)
            em.mm(ctx.ps_bc[:, 0:n], ctx.ones_row_f[:, :], ctx.r2[:, 0:n])
            em.act(ctx.rb[:, 0:n], ctx.ps_bc[:, 0:n], AF.Copy)
            em.tt(ctx.tmp_f[:, 0:n], ctx.mb[:, 0:n], ctx.rb[:, 0:n], ALU.mult)
            em.ts(out_ap, ctx.tmp_f[:, 0:n], w_col, ALU.mult)

        for h in range(16):
            for i0 in range(0, TQ, 512):
                ps = ctx.psum_pool[0]
                for k in range(NCH):
                    w = wl.get(io["wqT"], k * 128, h * 128)
                    em.mm(ps[:, 0:512], w,
                          xn2[:, k * TH + halo + i0: k * TH + halo + i0 + 512],
                          start=(k == 0), stop=(k == NCH - 1))
                head_norm(ps[:, 0:512], bqc[:, h:h + 1], qnc[:, 0:1],
                          qbuf[:, h * TQ + i0: h * TQ + i0 + 512], 512)
                emit_rope(qbuf, h * TQ + i0, 512, halo + i0)
        for kv in range(NKV):
            for i0 in range(0, TH, 512):
                t1 = min(i0 + 512, TH)
                n = t1 - i0
                ps = ctx.psum_pool[0]
                for k in range(NCH):
                    w = wl.get(io["wkT"], k * 128, kv * 128)
                    em.mm(ps[:, 0:n], w, xn2[:, k * TH + i0: k * TH + t1],
                          start=(k == 0), stop=(k == NCH - 1))
                head_norm(ps[:, 0:n], bkc[:, kv:kv + 1], knc[:, 0:1],
                          ks[:, kv * TH + i0: kv * TH + t1], n)
                emit_rope(ks, kv * TH + i0, n, i0)
            for i0 in range(0, TH, 512):
                t1 = min(i0 + 512, TH)
                n = t1 - i0
                ps = ctx.psum_pool[1]
                for k in range(NCH):
                    w = wl.get(io["wvT"], k * 128, kv * 128)
                    em.mm(ps[:, 0:n], w, xn2[:, k * TH + i0: k * TH + t1],
                          start=(k == 0), stop=(k == NCH - 1))
                em.act(p_pool[0][:, 0:n], ps[:, 0:n], AF.Identity, bias=bvc[:, kv:kv + 1])
                for tj in range(0, n, 128):
                    em.trans(ctx.ps_t[:, :], p_pool[0][:, tj:tj + 128], ctx.ident[:, :])
                    em.act(vtok[:, kv * 1152 + i0 + tj: kv * 1152 + i0 + tj + 128],
                           ctx.ps_t[:, :], AF.Copy)

        ps_s = ctx.psum_pool[2]
        ps_o = ctx.psum_pool[3]
        for h in range(16):
            kv = h // 4
            for i0 in range(0, TQ, 512):
                jlo = max(0, (halo + i0 - 64) // 128)
                jhi = (halo + i0 + 512 + 127) // 128
                first = True
                for jc in range(jlo, jhi):
                    em.mm(ps_s[:, 0:512], ks[:, kv * TH + jc * 128: kv * TH + (jc + 1) * 128],
                          qbuf[:, h * TQ + i0: h * TQ + i0 + 512])
                    p = p_pool[jc % 3]
                    em.act(p[:, 0:512], ps_s[:, 0:512], AF.Exp, scale=scol[:, 0:1])
                    delta = (halo + i0) - jc * 128
                    em.tt(p[:, 0:512], p[:, 0:512],
                          wmask[:, delta + 576: delta + 576 + 512], ALU.mult)
                    em.mm(ctx.ps_row[:, 0:512], ctx.ones_col_f[:, :], p[:, 0:512],
                          start=first, stop=(jc == jhi - 1))
                    em.mm(ps_o[:, 0:512],
                          vtok[:, kv * 1152 + jc * 128: kv * 1152 + (jc + 1) * 128],
                          p[:, 0:512], start=first, stop=(jc == jhi - 1))
                    first = False
                em.act(ctx.r1[:, 0:512], ctx.ps_row[:, 0:512], AF.Copy)
                em.emit("vector", lambda e: e.reciprocal(ctx.r2[:, 0:512], ctx.r1[:, 0:512]),
                        [ctx.r1], [ctx.r2])
                row_bcast(ctx, ctx.rb, ctx.r2[:, 0:512], 512, ctx.ps_bc)
                em.tt(s2b[:, h * TQ + i0: h * TQ + i0 + 512], ps_o[:, 0:512],
                      ctx.rb[:, 0:512], ALU.mult)

        sblocks = [lambda c: s2b[:, c * TQ: c * TQ + 512],
                   lambda c: s2b[:, c * TQ + 512: c * TQ + 1024]]

        def ep_s2(blk, mc, ps, oc):
            em.act(qbuf[:, mc * TQ + blk * 512: mc * TQ + blk * 512 + 512], ps, AF.Identity,
                   bias=boc[:, mc:mc + 1])
        matmul_fm(ctx, None, 0, sblocks, 512, wl, io["woT"], D, 0, D, epilog=ep_s2,
                  nblocks=2)
        att_scope.__exit__(None, None, None)

        TM2 = 512
        mlp_scope = em.scope()
        mlp_scope.__enter__()
        wl16 = WLoader(ctx, f"wl2h_{uid}", depth=4, cols=512, dtype=F16, scoped=True)
        xn16 = sb(f"w2x6{uid}", [128, 16 * TM2], F16)
        d16 = sb(f"w2d6{uid}", [128, 64 * TM2], F16)
        e16 = sb(f"w2e6{uid}", [128, TM2], F16)
        g2t = sb(f"w2g2{uid}", [128, TM2], BF16)
        for tb in range(TQ // TM2):
            t0 = tb * TM2
            for c in range(NCH):
                em.copy(xn16[:, c * TM2:(c + 1) * TM2],
                        xn2[:, c * TH + halo + t0: c * TH + halo + t0 + TM2])
            for fg in range(0, 64, 2):
                for k in range(NCH):
                    w = wl16.get(io["w0T"], k * 128, fg * 128, cols=256)
                    for i in range(2):
                        em.mm(ctx.psum_pool[i][:, 0:TM2], w[0:128, i * 128:(i + 1) * 128],
                              xn16[:, k * TM2:(k + 1) * TM2],
                              start=(k == 0), stop=(k == NCH - 1))
                for i in range(2):
                    em.act(d16[:, (fg + i) * TM2:(fg + i + 1) * TM2],
                           ctx.psum_pool[i][:, 0:TM2], AF.Gelu, bias=b0c[:, fg + i:fg + i + 1])
            for dg in range(0, NCH, 2):
                for fc in range(64):
                    w = wl16.get(io["w1T"], fc * 128, dg * 128, cols=256)
                    for i in range(2):
                        em.mm(ctx.psum_pool[i][:, 0:TM2], w[0:128, i * 128:(i + 1) * 128],
                              d16[:, fc * TM2:(fc + 1) * TM2],
                              start=(fc == 0), stop=(fc == 63))
                for i in range(2):
                    dmc = dg + i
                    em.act(e16[:, 0:TM2], ctx.psum_pool[i][:, 0:TM2], AF.Identity,
                           bias=b1c[:, dmc:dmc + 1])
                    em.act(g2t[:, 0:TM2], e16[:, 0:TM2], AF.Silu)
                    em.tt(ctx.tmp_f[:, 0:TM2], g2t[:, 0:TM2],
                          qbuf[:, dmc * TQ + t0: dmc * TQ + t0 + TM2], ALU.mult)
                    em.dma("sync", ctx.sq_f[:, 0:TM2],
                           io["x2s"][dmc * 128:(dmc + 1) * 128,
                                     128 + OW0 + t0: 128 + OW0 + t0 + TM2])
                    em.tt(ctx.tmp_f[:, 0:TM2], ctx.tmp_f[:, 0:TM2], ctx.sq_f[:, 0:TM2],
                          ALU.add)
                    em.dma("sync", io["x3_scr"][dmc * 128:(dmc + 1) * 128,
                                                OW0 + t0: OW0 + t0 + TM2],
                           ctx.tmp_f[:, 0:TM2])
        mlp_scope.__exit__(None, None, None)


def emit_l3(em, ctx, io, role, uid):
    TBB = 512
    NH = 32
    Q = 128
    pair = io["pair"]
    with em.scope():
        sb = em.sbuf_scoped
        wl = WLoader(ctx, f"wl3_{uid}", depth=4, cols=512, scoped=True)
        ident32 = sb(f"m3i32{uid}", [128, 128], F32)
        em.memset(ident32[:, :], 1.0)
        em.affsel(ident32[:, :], ident32[:, :], [[-1, 128]], ALU.is_ge, 0.0, 0, 1)
        em.affsel(ident32[:, :], ident32[:, :], [[1, 128]], ALU.is_ge, 0.0, 0, -1)
        identd = sb(f"m3id{uid}", [128, 64], BF16)
        em.memset(identd[:, :], 1.0)
        em.affsel(identd[:, :], identd[:, :], [[-1, 64]], ALU.is_ge, 0.0, 0, 1)
        em.affsel(identd[:, :], identd[:, :], [[1, 64]], ALU.is_ge, 0.0, 0, -1)
        em.dma("sync", identd[64:128, 0:64], identd[0:64, 0:64])
        tri = sb(f"m3tri{uid}", [128, 128], BF16)
        em.memset(tri[:, :], 1.0)
        em.affsel(tri[:, :], tri[:, :], [[1, 128]], ALU.is_ge, 0.0, 0, -1)
        ones_row_bf = sb(f"m3ob{uid}", [1, 128], BF16)
        em.memset(ones_row_bf[:, :], 1.0)

        xnb = sb(f"m3xn{uid}", [128, NCH * TBB], BF16)
        xbcb = sb(f"m3xb{uid}", [128, 17 * 515], BF16)
        xcv = sb(f"m3xc{uid}", [128, 17 * TBB], BF16)
        yfm = sb(f"m3y{uid}", [128, NCH * TBB], BF16)
        out_acc = sb(f"m3oa{uid}", [128, NCH * 512], F32)
        he = sb(f"m3he{uid}", [128, 8 * 512], BF16)
        zb = sb(f"m3zt{uid}", [128, TBB], BF16)
        S = sb(f"m3S{uid}", [64, NH * 64], BF16)
        dts = sb(f"m3dt{uid}", [32, TBB], F32)
        lam = sb(f"m3lm{uid}", [32, TBB], F32)
        alam = sb(f"m3al{uid}", [32, TBB], F32)
        edec = sb(f"m3ed{uid}", [32, TBB], F32)
        ww = sb(f"m3ww{uid}", [32, TBB], F32)
        eendr = sb(f"m3ee{uid}", [32, 4], F32)
        z32 = sb(f"m3z3{uid}", [32, TBB], F32)
        acol32 = sb(f"m3ac{uid}", [32, 1], F32)
        dtbc = sb(f"m3db{uid}", [32, 1], F32)
        abc = sb(f"m3ab{uid}", [128, 32], F32)
        arow = sb(f"m3ar{uid}", [1, 32], F32)
        alamT = sb(f"m3aT{uid}", [128, 128], F32)
        nalamT = sb(f"m3nT{uid}", [128, 128], F32)
        wwT = sb(f"m3wT{uid}", [128, 128], F32)
        dtT = sb(f"m3dT{uid}", [128, 128], F32)
        elamT = sb(f"m3eT{uid}", [128, 128], F32)
        btok = sb(f"m3bt{uid}", [128, 64 * 4], BF16)
        Bsb = sb(f"m3B{uid}", [64, TBB], BF16)
        Csb = sb(f"m3C{uid}", [64, TBB], BF16)
        alr = sb(f"m3alr{uid}", [1, 4096], F32)
        eer = sb(f"m3eer{uid}", [1, 128], F32)
        mrr = sb(f"m3mrr{uid}", [1, 8 * TBB], BF16)
        mh_t = sb(f"m3mh{uid}", [128, 128], BF16)
        dtile = sb(f"m3dti{uid}", [128, 128], BF16)
        ut_t = sb(f"m3ut{uid}", [128, 64], BF16)
        us_t = sb(f"m3us{uid}", [128, 64], BF16)
        yint = sb(f"m3yi{uid}", [128, 64], F32)
        ytok = sb(f"m3yt{uid}", [128, 64], BF16)
        contrib = sb(f"m3co{uid}", [64, 64], BF16)
        deccol = sb(f"m3dc{uid}", [64, 1], F32)
        cvwc = sb(f"m3cw{uid}", [128, 17 * 4], F32)
        cvbc = sb(f"m3cb{uid}", [128, 17], F32)
        dexpc = sb(f"m3de{uid}", [128, 16], F32)
        g8 = sb(f"m3g8{uid}", [128, 8], F32)
        cmp8 = sb(f"m3c8{uid}", [128, 8], F32)
        cnt8 = sb(f"m3n8{uid}", [128, 1], F32)
        m8 = sb(f"m3m8{uid}", [128, 8], F32)
        mbc = sb(f"m3mb{uid}", [128, 512], BF16)
        mrow5 = sb(f"m3mr{uid}", [1, TBB], F32)
        rrow5 = sb(f"m3rr{uid}", [1, TBB], F32)

        load_col(em, "sync", cvbc, io["convb"], 17)
        load_col(em, "sync", dexpc, io["dexp"], 16)
        for ch in range(17):
            em.dma("sync", cvwc[:, ch * 4:(ch + 1) * 4],
                   io["convw"][ch * 128:(ch + 1) * 128, :])
        em.dma("sync", dtbc[:, :], io["dtb32"])
        em.dma("sync", ctx.tmp_f[0:32, 0:1], io["alog32"])
        em.act(acol32[0:32, 0:1], ctx.tmp_f[0:32, 0:1], AF.Exp)
        em.ts(acol32[0:32, 0:1], acol32[0:32, 0:1], -1.0, ALU.mult)
        em.trans(ctx.ps_bc[0:1, 0:32], acol32[0:32, 0:1], ident32[0:32, 0:32])
        em.act(arow[:, :], ctx.ps_bc[0:1, 0:32], AF.Copy)
        em.mm(ctx.ps_bc[:, 0:32], ctx.ones_row_f[:, :], arow[:, :])
        em.act(abc[:, :], ctx.ps_bc[:, 0:32], AF.Copy)
        em.memset(z32[:, :], 0.0)
        em.memset(xbcb[:, :], 0.0)
        if role == 0:
            em.memset(S[:, :], 0.0)
        else:
            pair.wait(F_S3, 1, io["body_idx"])
            em.dma("gpsimd", S[:, :], io["Ssh"])
            pair.wait(F_CVT, 1, io["body_idx"])
            for ch in range(17):
                em.dma("gpsimd", xbcb[:, ch * 515:ch * 515 + 3],
                       io["xbctail_sh"][:, ch * 3:(ch + 1) * 3])

        def xbcb_sl(ch, a, b):
            return xbcb[:, ch * 515 + a: ch * 515 + b]

        for bk in range(4):
            t0 = bk * TBB
            for c in range(NCH):
                em.dma("sync", ctx.sq_f[:, 0:TBB],
                       io["x3_scr"][c * 128:(c + 1) * 128, t0:t0 + TBB])
                em.mm(ctx.ps_row[:, 0:TBB], ctx.ones_col32[:, :], ctx.sq_f[:, 0:TBB],
                      start=(c == 0), stop=(c == NCH - 1))
                em.act(ctx.tmp_f[:, 0:TBB], ctx.sq_f[:, 0:TBB], AF.Square)
                em.mm(ctx.ps_row2[:, 0:TBB], ctx.ones_col32[:, :], ctx.tmp_f[:, 0:TBB],
                      start=(c == 0), stop=(c == NCH - 1))
            em.act(mrow5[:, 0:TBB], ctx.ps_row[:, 0:TBB], AF.Copy, scale=1.0 / D)
            em.act(ctx.r1[:, 0:TBB], mrow5[:, 0:TBB], AF.Square)
            em.act(ctx.r2[:, 0:TBB], ctx.ps_row2[:, 0:TBB], AF.Copy, scale=1.0 / D)
            em.tt(ctx.r2[:, 0:TBB], ctx.r2[:, 0:TBB], ctx.r1[:, 0:TBB], ALU.subtract)
            em.act(ctx.r1[:, 0:TBB], ctx.r2[:, 0:TBB], AF.Sqrt, bias=1e-5)
            em.emit("vector", lambda e: e.reciprocal(rrow5[:, 0:TBB], ctx.r1[:, 0:TBB]),
                    [ctx.r1], [rrow5])
            row_bcast(ctx, ctx.rb, rrow5[:, 0:TBB], TBB, ctx.ps_bc)
            em.tt(ctx.r1[:, 0:TBB], mrow5[:, 0:TBB], rrow5[:, 0:TBB], ALU.mult)
            row_bcast(ctx, ctx.mb, ctx.r1[:, 0:TBB], TBB, ctx.ps_bc)
            for c in range(NCH):
                em.dma("sync", ctx.sq_f[:, 0:TBB],
                       io["x3_scr"][c * 128:(c + 1) * 128, t0:t0 + TBB])
                em.tt(ctx.tmp_f[:, 0:TBB], ctx.sq_f[:, 0:TBB], ctx.rb[:, 0:TBB], ALU.mult)
                em.tt(chunk(xnb, c, TBB), ctx.tmp_f[:, 0:TBB], ctx.mb[:, 0:TBB],
                      ALU.subtract)
            for oc in range(17):
                ps = ctx.psum_pool[0]
                for k in range(NCH):
                    w = wl.get(io["winT"], k * 128, 2048 + oc * 128)
                    em.mm(ps[:, 0:TBB], w, chunk(xnb, k, TBB), start=(k == 0),
                          stop=(k == NCH - 1))
                if bk > 0:
                    em.copy(xbcb_sl(oc, 0, 3), xbcb_sl(oc, 512, 515), eng="gpsimd")
                em.act(xbcb_sl(oc, 3, 515), ps[:, 0:TBB], AF.Copy)
            if role == 0 and bk == 3:
                for ch in range(17):
                    em.dma("gpsimd", io["xbctail_sh"][:, ch * 3:(ch + 1) * 3],
                           xbcb_sl(ch, 512, 515))
                pair.send(F_CVT, 0, [io["h_xbctail_sh"]])
            ps = ctx.psum_pool[0]
            for k in range(NCH):
                w = wl.get(io["winT"], k * 128, 4224, cols=32)
                em.mm(ps[0:32, 0:TBB], w[0:128, 0:32], chunk(xnb, k, TBB), start=(k == 0),
                      stop=(k == NCH - 1))
            em.act(ctx.tmp_f[0:32, 0:TBB], ps[0:32, 0:TBB], AF.Identity,
                   bias=dtbc[0:32, 0:1])
            em.act(ctx.sq_f[0:32, 0:TBB], ctx.tmp_f[0:32, 0:TBB], AF.Abs)
            em.act(ctx.sq_f[0:32, 0:TBB], ctx.sq_f[0:32, 0:TBB], AF.Exp, scale=-1.0)
            em.act(ctx.sq_f[0:32, 0:TBB], ctx.sq_f[0:32, 0:TBB], AF.Ln, bias=1.0)
            em.ts(ctx.tmp_f[0:32, 0:TBB], ctx.tmp_f[0:32, 0:TBB], 0.0, ALU.max)
            em.tt(dts[0:32, 0:TBB], ctx.tmp_f[0:32, 0:TBB], ctx.sq_f[0:32, 0:TBB],
                  ALU.add)
            for oc in range(16):
                ps = ctx.psum_pool[1]
                for k in range(NCH):
                    w = wl.get(io["winT"], k * 128, oc * 128)
                    em.mm(ps[:, 0:TBB], w, chunk(xnb, k, TBB), start=(k == 0),
                          stop=(k == NCH - 1))
                em.act(zb[:, 0:TBB], ps[:, 0:TBB], AF.Copy)
                em.dma("sync", io["z_scr"][oc * 128:(oc + 1) * 128, :], zb[:, 0:TBB])
            for ch in range(17):
                em.ts(ctx.tmp_f[:, 0:TBB], xbcb_sl(ch, 0, 512), cvwc[:, ch * 4:ch * 4 + 1],
                      ALU.mult)
                for kk in range(1, 4):
                    em.stt(ctx.tmp_f[:, 0:TBB], xbcb_sl(ch, kk, 512 + kk),
                           cvwc[:, ch * 4 + kk:ch * 4 + kk + 1], ctx.tmp_f[:, 0:TBB],
                           ALU.mult, ALU.add)
                em.act(xcv[:, ch * TBB:(ch + 1) * TBB], ctx.tmp_f[:, 0:TBB], AF.Silu,
                       bias=cvbc[:, ch:ch + 1])
            em.copy(Bsb[0:64, 0:TBB], xcv[0:64, 16 * TBB:17 * TBB])
            em.copy(Csb[0:64, 0:TBB], xcv[64:128, 16 * TBB:17 * TBB])
            for c in range(4):
                cs = slice(c * Q, (c + 1) * Q)
                em.scan(lam[0:32, cs], dts[0:32, cs], z32[0:32, 0:Q], 0.0,
                        op0=ALU.add, op1=ALU.add)
            em.ts(alam[0:32, 0:TBB], lam[0:32, 0:TBB], acol32[0:32, 0:1], ALU.mult)
            for c in range(4):
                cs = slice(c * Q, (c + 1) * Q)
                em.act(edec[0:32, cs], alam[0:32, cs], AF.Exp, scale=-1.0,
                       bias=alam[0:32, c * Q + Q - 1:c * Q + Q])
            em.tt(ww[0:32, 0:TBB], dts[0:32, 0:TBB], edec[0:32, 0:TBB], ALU.mult)
            for c in range(4):
                em.act(eendr[0:32, c:c + 1], alam[0:32, c * Q + Q - 1:c * Q + Q], AF.Exp)
            em.dma("sync", io["lam_scr"][:, 0:TBB], alam[0:32, 0:TBB])
            em.dma("sync", io["eend_scr"], eendr[0:32, 0:4])
            em.dma("sync", eer[:, 0:128],
                   io["eend_scr"].rearrange("h c -> (h c)").rearrange("(o f) -> o f", o=1))
            for c in range(4):
                cs = slice(c * Q, (c + 1) * Q)
                em.trans(ctx.ps_bc[:, 0:32], alam[0:32, cs], ident32[0:32, 0:32])
                em.act(alamT[:, c * 32:(c + 1) * 32], ctx.ps_bc[:, 0:32], AF.Copy)
                em.trans(ctx.ps_bc[:, 0:32], ww[0:32, cs], ident32[0:32, 0:32])
                em.act(wwT[:, c * 32:(c + 1) * 32], ctx.ps_bc[:, 0:32], AF.Copy)
                em.trans(ctx.ps_bc[:, 0:32], dts[0:32, cs], ident32[0:32, 0:32])
                em.act(dtT[:, c * 32:(c + 1) * 32], ctx.ps_bc[:, 0:32], AF.Copy)
            em.ts(nalamT[:, 0:128], alamT[:, 0:128], -1.0, ALU.mult)
            em.act(elamT[:, 0:128], alamT[:, 0:128], AF.Exp)
            for c in range(4):
                em.trans(ctx.ps_t[:, 0:64], Bsb[0:64, c * Q:(c + 1) * Q],
                         ctx.ident[0:64, 0:64])
                em.act(btok[:, c * 64:(c + 1) * 64], ctx.ps_t[:, 0:64], AF.Copy)
            for c in range(4):
                cs = slice(c * Q, (c + 1) * Q)
                em.dma("sync", alr[:, 0:4096].rearrange("o (h t) -> o h t", h=32),
                       io["lam_scr"][:, c * Q:(c + 1) * Q].rearrange("h t -> (h) t")
                       .rearrange("h t -> h () t").transpose([1, 0, 2]))
                ps_cb = ctx.psum_pool[0]
                em.mm(ps_cb[:, 0:Q], Bsb[0:64, cs], Csb[0:64, cs])
                em.tt(mh_t[:, 0:Q], ps_cb[:, 0:Q], tri[:, :], ALU.mult)
                for h in range(NH):
                    hc = h * 64
                    xch = h // 2
                    xrw = (h % 2) * 64
                    em.mm(ctx.ps_bc[:, 0:Q], ctx.ones_row_f[:, :], alr[:, h * Q:(h + 1) * Q])
                    em.act(ctx.rb[:, 0:Q], ctx.ps_bc[:, 0:Q], AF.Identity,
                           bias=nalamT[:, c * 32 + h:c * 32 + h + 1])
                    em.ts(ctx.rb[:, 0:Q], ctx.rb[:, 0:Q], 0.0, ALU.min)
                    em.act(dtile[:, 0:Q], ctx.rb[:, 0:Q], AF.Exp)
                    em.tt(dtile[:, 0:Q], dtile[:, 0:Q], mh_t[:, 0:Q], ALU.mult)
                    em.trans(ctx.ps_t[:, 0:64], xcv[xrw:xrw + 64, xch * TBB + c * Q:
                                                    xch * TBB + (c + 1) * Q],
                             identd[xrw:xrw + 64, 0:64])
                    em.act(ut_t[:, 0:64], ctx.ps_t[:, 0:64], AF.Copy,
                           scale=dtT[:, c * 32 + h:c * 32 + h + 1])
                    em.act(us_t[:, 0:64], ctx.ps_t[:, 0:64], AF.Copy,
                           scale=wwT[:, c * 32 + h:c * 32 + h + 1])
                    ps_y = ctx.psum_pool[1]
                    ps_y2 = ctx.psum_pool[2]
                    em.mm(ps_y[:, 0:64], dtile[:, 0:Q], ut_t[:, 0:64])
                    em.mm(ps_y2[:, 0:64], Csb[0:64, cs], S[:, hc:hc + 64])
                    em.act(yint[:, 0:64], ps_y2[:, 0:64], AF.Copy,
                           scale=elamT[:, c * 32 + h:c * 32 + h + 1])
                    em.tt(ytok[:, 0:64], yint[:, 0:64], ps_y[:, 0:64], ALU.add)
                    em.trans(ctx.ps_t[0:64, 0:128], ytok[:, 0:64], ctx.ident[:, :])
                    em.act(yfm[xrw:xrw + 64, xch * TBB + c * Q: xch * TBB + (c + 1) * Q],
                           ctx.ps_t[0:64, 0:128], AF.Copy)
                    ps_s4 = ctx.psum_pool[3]
                    em.mm(ps_s4[0:64, 0:64], btok[:, c * 64:(c + 1) * 64], us_t[:, 0:64])
                    em.act(contrib[0:64, 0:64], ps_s4[0:64, 0:64], AF.Copy)
                    em.mm(ctx.ps_bc[0:64, 0:1], ctx.ones_row_f[:, 0:64],
                          eer[:, h * 4 + c:h * 4 + c + 1])
                    em.act(deccol[0:64, 0:1], ctx.ps_bc[0:64, 0:1], AF.Copy)
                    em.stt(S[:, hc:hc + 64], S[:, hc:hc + 64], deccol[0:64, 0:1],
                           contrib[0:64, 0:64], ALU.mult, ALU.add)
            if role == 0 and bk == 3:
                em.dma("gpsimd", io["Ssh"], S[:, :])
                pair.send(F_S3, 0, [io["h_Ssh"]])
            for oc in range(16):
                em.ts(ctx.tmp_f[:, 0:TBB], xcv[:, oc * TBB:(oc + 1) * TBB],
                      dexpc[:, oc:oc + 1], ALU.mult)
                em.tt(ctx.tmp_f[:, 0:TBB], ctx.tmp_f[:, 0:TBB], chunk(yfm, oc, TBB),
                      ALU.add)
                em.dma("sync", zb[:, 0:TBB], io["z_scr"][oc * 128:(oc + 1) * 128, :])
                em.act(ctx.sq_f[:, 0:TBB], zb[:, 0:TBB], AF.Silu)
                em.tt(chunk(yfm, oc, TBB), ctx.tmp_f[:, 0:TBB], ctx.sq_f[:, 0:TBB],
                      ALU.mult)
            for tc in range(4):
                ps = ctx.psum_pool[0]
                for k in range(NCH):
                    w = wl.get(io["wg8"], k * 128, 0, cols=8)
                    em.mm(ps[0:128, 0:8],
                          xnb[:, k * TBB + tc * 128: k * TBB + (tc + 1) * 128],
                          w[0:128, 0:8], start=(k == 0), stop=(k == NCH - 1))
                em.act(g8[:, 0:8], ps[0:128, 0:8], AF.Copy)
                for e in range(8):
                    em.ts(cmp8[:, 0:8], g8[:, 0:8], g8[:, e:e + 1], ALU.is_gt)
                    em.red(cnt8[:, 0:1], cmp8[:, 0:8], ALU.add)
                    em.ts(m8[:, e:e + 1], cnt8[:, 0:1], 3.5, ALU.is_le)
                em.trans(ctx.ps_bc[0:8, 0:128], m8[:, 0:8], ident32[:, :])
                em.act(mbc[0:8, 0:128], ctx.ps_bc[0:8, 0:128], AF.Copy)
                em.dma("sync", io["mrow_scr"][:, tc * 128:(tc + 1) * 128],
                       mbc[0:8, 0:128])
            em.dma("sync", mrr[:, 0:8 * TBB],
                   io["mrow_scr"].rearrange("e t -> (e t)").rearrange("(o f) -> o f", o=1))
            TMO = 512
            for dmc in range(16):
                em.memset(out_acc[:, dmc * TMO:(dmc + 1) * TMO], 0.0)
            for e in range(8):
                em.mm(ctx.ps_bc[:, 0:TMO], ones_row_bf[:, :], mrr[:, e * TBB: e * TBB + TMO])
                em.act(mbc[:, 0:TMO], ctx.ps_bc[:, 0:TMO], AF.Copy)
                for hc in range(8):
                    ps = ctx.psum_pool[1]
                    for k in range(NCH):
                        w = wl.get(io["w1Ts"], e * 2048 + k * 128, hc * 128)
                        em.mm(ps[:, 0:TMO], w, xnb[:, k * TBB: k * TBB + TMO],
                              start=(k == 0), stop=(k == NCH - 1))
                    em.act(he[:, hc * TMO: (hc + 1) * TMO], ps[:, 0:TMO], AF.Relu)
                    em.tt(he[:, hc * TMO: (hc + 1) * TMO],
                          he[:, hc * TMO: (hc + 1) * TMO], mbc[:, 0:TMO], ALU.mult)
                for dmc in range(16):
                    ps = ctx.psum_pool[2]
                    for hc in range(8):
                        w = wl.get(io["w2Ts"], e * 1024 + hc * 128, dmc * 128)
                        em.mm(ps[:, 0:TMO], w, he[:, hc * TMO: (hc + 1) * TMO],
                              start=(hc == 0), stop=(hc == 7))
                    em.emit("vector", lambda e2, dmc=dmc: e2.tensor_tensor(
                        out_acc[:, dmc * TMO:(dmc + 1) * TMO],
                        out_acc[:, dmc * TMO:(dmc + 1) * TMO],
                        ctx.psum_pool[2][:, 0:TMO], ALU.add),
                        [out_acc, ctx.psum_pool[2]], [out_acc])
            for dmc in range(16):
                ps = ctx.psum_pool[3]
                for k in range(NCH):
                    w = wl.get(io["woutT"], k * 128, dmc * 128)
                    em.mm(ps[:, 0:TMO], w, chunk(yfm, k, TBB)[:, 0:TMO],
                          start=(k == 0), stop=(k == NCH - 1))
                em.act(ctx.sq_f[:, 0:TMO], out_acc[:, dmc * TMO:(dmc + 1) * TMO],
                       AF.Silu)
                em.emit("vector", lambda e2: e2.tensor_tensor(
                    ctx.tmp_f[:, 0:TMO], ctx.sq_f[:, 0:TMO],
                    ctx.psum_pool[3][:, 0:TMO], ALU.mult),
                    [ctx.sq_f, ctx.psum_pool[3]], [ctx.tmp_f])
                em.dma("sync", ctx.sq_f[:, 0:TMO],
                       io["x3_scr"][dmc * 128:(dmc + 1) * 128, t0: t0 + TMO])
                em.tt(ctx.tmp_f[:, 0:TMO], ctx.tmp_f[:, 0:TMO], ctx.sq_f[:, 0:TMO],
                      ALU.add)
                em.dma("sync", io["o_x4"][dmc * 128:(dmc + 1) * 128, t0: t0 + TMO],
                       ctx.tmp_f[:, 0:TMO])


def build_fused(debug=False, stages=("p0", "p12", "l1", "l2", "l3"), roles=(0, 1)):
    nc = bass.Bass("TRN2", target_bir_lowering=False)
    em = Em(nc)

    def din(name, shape, dtype=F32):
        return em.dram(name, shape, dtype, kind="ExternalInput")

    ins = dict(
        xT=din("xT", [D, 2048]), nonce=din("nonce", [1, 1]),
        wqa=din("wqa", [D, D]), wkv=din("wkv", [D, 256]),
        gnorm=din("gnorm", [D, 1]), ln1=din("ln1", [D, 1]),
        wq0=din("wq0", [D, D]), wk0=din("wk0", [D, D]), wv0=din("wv0", [D, D]),
        wout=din("wout", [D, D]), gout=din("gout", [D, 1]), wo0=din("wo0", [D, D]),
        a_lora=din("a_lora", [D, 16]), b_lora=din("b_lora", [16, 8192]),
        ln2=din("ln2", [D, 1]), w_in=din("w_in", [D, 8192]), w_up=din("w_up", [D, 8192]),
        w_dn=din("w_dn", [8192, D]),
        wxT=din("wxT", [D, 96]), wdtT=din("wdtT", [64, D]), bdt=din("bdt", [D, 1]),
        alog=din("alog", [D, 16]), dcol=din("dcol", [D, 1]),
        gpT=din("gpT", [1024, 8192], I32), gsT=din("gsT", [64, 8192]),
        upT=din("upT", [1024, 8192], I32), usT=din("usT", [64, 8192]),
        dnT=din("dnT", [4096, D], I32), dsT=din("dsT", [256, D]),
        wqT=din("wqT", [D, D]), bq=din("bq", [D, 1]),
        wkT=din("wkT", [D, 512]), bk=din("bk", [512, 1]),
        wvT=din("wvT", [D, 512]), bv=din("bv", [512, 1]),
        woT=din("woT", [D, D]), bo=din("bo", [D, 1]),
        qnw=din("qnw", [128, 1]), knw=din("knw", [128, 1]),
        att_scale=din("att_scale", [1, 1]),
        thS=din("thS", [64, 2176]), thC=din("thC", [64, 2176]),
        w0T=din("w0T", [D, 8192]), b0=din("b0", [8192, 1]),
        w1T=din("w1T", [8192, D]), b1=din("b1", [D, 1]),
        winT=din("winT", [D, 4256]), convw=din("convw", [2176, 4]),
        convb=din("convb", [2176, 1]), woutT=din("woutT", [D, D]),
        alog32=din("alog32", [32, 1]), dtb32=din("dtb32", [32, 1]),
        dexp=din("dexp", [D, 1]), wg8=din("wg8", [D, 8]),
        w1Ts=din("w1Ts", [8 * D, 1024]), w2Ts=din("w2Ts", [8 * 1024, D]),
    )
    o_x4 = em.dram("o_x4", [D, 2048], F32, kind="ExternalOutput")
    scr = dict(
        xn_scr=em.dram("xn_scr", [D, 2048], BF16),
        qhat_scr=em.dram("qhat_scr", [D, 2048], BF16),
        q0_scr=em.dram("q0_scr", [D, 2048], BF16),
        x1_scr=em.dram("x1_scr", [D, 2048], F32),
        lnrows_scr=em.dram("lnrows_scr", [2, 2048], F32),
        bc_scr=em.dram("bcr_scr", [32, 2048], F32),
        s1_scr=em.dram("s1_scr", [D, 2048], F32),
        part_own=em.dram("part_own", [D, 2048], BF16),
        wg_dec=em.dram("wg_dec", [2048, 8192], BF16),
        wu_dec=em.dram("wu_dec", [2048, 8192], BF16),
        wd_dec=em.dram("wd_dec", [8192, 2048], BF16),
        x2s=em.dram("x2s", [D, 2176], F32),
        x3_scr=em.dram("x3_scr", [D, 2048], F32),
        lam_scr=em.dram("lam_scr", [32, 512], F32),
        eend_scr=em.dram("eend_scr", [32, 4], F32),
        mrow_scr=em.dram("mrow_scr", [8, 512], BF16),
        z_scr=em.dram("z_scr", [D, 512], BF16),
    )
    sh = dict(
        khat_sh=em.dram("khat_sh", [128, T], BF16, addr_space="Shared"),
        va_sh=em.dram("va_sh", [T, 128], BF16, addr_space="Shared"),
        k0_sh=em.dram("k0_sh", [D, T], BF16, addr_space="Shared"),
        v0_sh=em.dram("v0_sh", [T, D], BF16, addr_space="Shared"),
        xn1_sh=em.dram("xn1_sh", [D, T], BF16, addr_space="Shared"),
        h1_sh=em.dram("h1_sh", [D, 16], F32, addr_space="Shared"),
        part_sh0=em.dram("part_sh0", [D, 2048], BF16, addr_space="Shared"),
        part_sh1=em.dram("part_sh1", [D, 2048], BF16, addr_space="Shared"),
        x2e_sh=em.dram("x2e_sh", [D, 128], F32, addr_space="Shared"),
        xbctail_sh=em.dram("xbctail_sh", [128, 51], BF16, addr_space="Shared"),
        Ssh=em.dram("Ssh", [64, 32 * 64], BF16, addr_space="Shared"),
    )
    dbg_outs = {}
    if debug:
        for nm, rr, w, dt_ in (("d_x1", D, 2048, F32), ("d_s1", D, 2048, F32),
                               ("d_xn1", D, T, BF16), ("d_x2", D, 2176, F32),
                               ("d_x3", D, 2048, F32), ("d_po", D, 2048, BF16),
                               ("d_ps", D, 2048, BF16), ("d_xn", D, 2048, BF16),
                               ("d_qh", D, 2048, BF16), ("d_kh", 128, T, BF16)):
            dbg_outs[nm] = em.dram(nm, [rr, w], dt_, kind="ExternalOutput")
    pair = Pair(em, ins["nonce"])
    ctx = make_ctx(em, tb=512)
    for cv in (1.0 / D, 1e-5, 1e-6, 1e-12, -0.5, float(np.log(64.0)), float(np.log(4.0)),
               256.0, -1.0, 2.0, -4.0, -12.0, 1.0 / 128.0, 1e-8, 7.0, 0.1, 3.5, 0.5,
               1.0 / float(np.sqrt(128.0))):
        em.constf(cv)

    for bi, role in enumerate(roles):
        if bi == 1:
            em.new_body()
        if bi == 1:
            ctx_init(ctx)
        pair.setup()
        own0 = role * 2048
        io = {k: (v.ap() if hasattr(v, "ap") else v) for k, v in {**ins, **scr, **sh}.items()}
        io["pair"] = pair
        io["role"] = role
        io["body_idx"] = role
        io["o_x4"] = o_x4.ap()
        io["khat_sh_own"] = sh["khat_sh"].ap()[:, own0:own0 + 2048]
        io["va_sh_own"] = sh["va_sh"].ap()[own0:own0 + 2048, :]
        io["k0_sh_own"] = sh["k0_sh"].ap()[:, own0:own0 + 2048]
        io["v0_sh_own"] = sh["v0_sh"].ap()[own0:own0 + 2048, :]
        io["xn1_sh_own"] = sh["xn1_sh"].ap()[:, own0:own0 + 2048]
        io["part_sh_mine"] = sh[f"part_sh{role}"].ap()
        io["part_sh_other"] = sh[f"part_sh{1 - role}"].ap()
        io["h_x2e_sh"] = sh["x2e_sh"]
        io["h_xbctail_sh"] = sh["xbctail_sh"]
        io["h_Ssh"] = sh["Ssh"]

        if "p0" in stages:
            for sub in range(2):
                emit_p0(em, ctx, io, sub, uid=role * 100 + sub)
            pair.send(F_KV, role, [sh["khat_sh"], sh["va_sh"], sh["k0_sh"], sh["v0_sh"]])
        if "p12" in stages:
            pair.wait(F_KV, role, role)
            for sub in range(2):
                pc = role * 16 + (sub + 1) * 8
                emit_p12(em, ctx, io, sub, pc, uid=role * 100 + 10 + sub)
        if "l1" in stages:
            emit_l1_xn1(em, ctx, io, role, uid=role * 100 + 20)
            pair.send(F_XN1, role, [sh["xn1_sh"]])
            if role == 0:
                emit_l1_scan(em, ctx, io, 0, uid=role * 100 + 30)
                pair.send(F_H1, 0, [sh["h1_sh"]])
                pair.wait(F_XN1, 0, role)
                emit_l1_ffn(em, ctx, io, 0, uid=role * 100 + 40)
                pair.send(F_PART, 0, [sh["part_sh0"]])
            else:
                pair.wait(F_XN1, 1, role)
                emit_l1_ffn(em, ctx, io, 1, uid=role * 100 + 40)
                pair.send(F_PART, 1, [sh["part_sh1"]])
                pair.wait(F_H1, 1, role)
                emit_l1_scan(em, ctx, io, 1, uid=role * 100 + 30)
        if "l2" in stages:
            pair.wait(F_PART, role, role)
            emit_l2(em, ctx, io, role, uid=role * 100 + 50)
        if "l3" in stages:
            emit_l3(em, ctx, io, role, uid=role * 100 + 60)
        if debug:
            for nm, src_t, w in (("d_x1", scr["x1_scr"], 2048), ("d_s1", scr["s1_scr"], 2048),
                                 ("d_xn1", sh["xn1_sh"], T), ("d_x2", scr["x2s"], 2176),
                                 ("d_x3", scr["x3_scr"], 2048), ("d_po", scr["part_own"], 2048),
                                 ("d_ps", sh["part_sh0"], 2048),
                                 ("d_xn", scr["xn_scr"], 2048), ("d_qh", scr["qhat_scr"], 2048),
                                 ("d_kh", sh["khat_sh"], T)):
                od = dbg_outs[nm]
                for c in range(od.shape[0] // 128):
                    em.dma("sync", od.ap()[c * 128:(c + 1) * 128, :],
                           src_t.ap()[c * 128:(c + 1) * 128, :])
        em.finish()
    return em.build(), em


def kernel_fused(_debug=False, **inputs):
    d = {k: np.asarray(v) for k, v in inputs.items()}
    x = d["L0_attn_x"].astype(np.float32)
    C = np.ascontiguousarray
    nc, em = _get_prog(("fused", _debug), build_fused, _debug)
    inv = 1.0 / 10000.0 ** (np.arange(0, 128, 2).astype(np.float64) / 128)
    pq = _rope_perm16()
    pk = _rope_perm16()[:512]
    pk128 = np.concatenate([np.arange(0, 128, 2), np.arange(1, 128, 2)])
    w1 = d["L3_ffn_w1"]
    w2 = d["L3_ffn_w2"]
    dn = d["L1_ffn_down_W_packed"]
    ds = d["L1_ffn_down_scales"]
    dnT = C(dn.T)
    dsT = C(ds.T)
    base = dict(
        wqa=C(d["L0_attn_Wq"]), wkv=C(d["L0_attn_Wkv"]),
        gnorm=_col(d["L0_attn_g_norm"]), ln1=_col(d["L0_ffn_ln1_w"]),
        wq0=C(d["L0_ffn_Wq"].T), wk0=C(d["L0_ffn_Wk"].T), wv0=C(d["L0_ffn_Wv"].T),
        wout=C(d["L0_attn_Wout"]), gout=_col(d["L0_attn_g_out"]),
        wo0=C(d["L0_ffn_Wo"].T), a_lora=C(d["L0_ffn_A"]), b_lora=C(d["L0_ffn_B"]),
        ln2=_col(d["L0_ffn_ln2_w"]), w_in=C(d["L0_ffn_W_in_base"].T),
        w_up=C(d["L0_ffn_W_up_base"].T), w_dn=C(d["L0_ffn_W_down"].T),
        wxT=C(d["L1_attn_Wx"].T), wdtT=C(d["L1_attn_Wdt"].T),
        bdt=_col(d["L1_attn_bdt"]), alog=C(d["L1_attn_A_log"]),
        dcol=_col(d["L1_attn_D"]),
        wqT=C(d["L2_attn_Wq"].T[:, pq]), bq=_col(d["L2_attn_bq"][pq]),
        wkT=C(d["L2_attn_Wk"].T[:, pk]), bk=_col(d["L2_attn_bk"][pk]),
        wvT=C(d["L2_attn_Wv"].T), bv=_col(d["L2_attn_bv"]),
        woT=C(d["L2_attn_Wo"].T), bo=_col(d["L2_attn_bo"]),
        qnw=_col(d["L2_attn_qn_w"][pk128]), knw=_col(d["L2_attn_kn_w"][pk128]),
        att_scale=np.asarray(d["L2_attn_scale"], np.float32).reshape(1, 1),
        w0T=C(d["L2_ffn_W0"].T), b0=_col(d["L2_ffn_bias0"]),
        w1T=C(d["L2_ffn_W1"].T), b1=_col(d["L2_ffn_bias1"]),
        winT=C(d["L3_attn_W_in"].T), convw=C(d["L3_attn_conv_w"][:, 0, :]),
        convb=_col(d["L3_attn_conv_b"]), woutT=C(d["L3_attn_W_out"].T),
        alog32=_col(d["L3_attn_A_log"]), dtb32=_col(d["L3_attn_dt_bias"]),
        dexp=_col(np.repeat(np.asarray(d["L3_attn_D_param"]), 64)),
        wg8=C(d["L3_ffn_wg"]),
        w1Ts=C(np.concatenate([w1[e].T for e in range(8)], axis=0)),
        w2Ts=C(np.concatenate([w2[e].T for e in range(8)], axis=0)),
    )
    nonce = np.full((1, 1), float(np.random.randint(1 << 20, 1 << 23)), np.float32)
    ims = []
    for c in range(8):
        b, role = c // 4, c % 2
        pr = _perm_rows_l1(role * 8192, 8192)
        gp, gs = d["L1_ffn_gate_W_packed"], d["L1_ffn_gate_scales"]
        up, us = d["L1_ffn_up_W_packed"], d["L1_ffn_up_scales"]
        lo = role * 2048 - 128
        pos = np.arange(lo, role * 2048 + 2048).astype(np.float64)
        th = pos[None, :] * inv[:, None]
        thS = ((th + np.pi) % (2 * np.pi) - np.pi).astype(np.float32)
        thC = ((th + np.pi / 2 + np.pi) % (2 * np.pi) - np.pi).astype(np.float32)
        im = dict(base)
        im.update(
            xT=C(x[b].T[:, role * 2048:(role + 1) * 2048]), nonce=nonce,
            gpT=C(gp[pr].T), gsT=C(gs[pr].T), upT=C(up[pr].T), usT=C(us[pr].T),
            dnT=C(dnT[role * 4096:(role + 1) * 4096]),
            dsT=C(dsT[role * 256:(role + 1) * 256]),
            thS=C(thS), thC=C(thC),
        )
        ims.append(im)
    res = run_bass_kernel_spmd(nc, ims, core_ids=list(range(8))).results
    x4 = np.zeros((B, T, D), np.float32)
    for b in range(B):
        for role in range(2):
            cidx = b * 4 + role
            x4[b, role * 2048:(role + 1) * 2048] = np.asarray(res[cidx]["o_x4"]).T
    if _debug:
        return x4, res
    return x4
